# version 17
# baseline (speedup 1.0000x reference)
import numpy as np
import ml_dtypes
from contextlib import ExitStack

import concourse.bass as bass
import concourse.mybir as mybir
from concourse.bass_utils import run_bass_kernel_spmd

F32 = mybir.dt.float32
BF16 = mybir.dt.bfloat16
AF = mybir.ActivationFunctionType
ALU = mybir.AluOpType
AX = mybir.AxisListType

B, S, D = 16, 2048, 1024
NH, NKV, HD = 16, 4, 64
CHUNK = 128
NCHUNK = S // CHUNK
THETA = 10000.0
DH = 4096
FT = DH // 128
NCORE = 8
TOK = (B // NCORE) * S
EPS = 1e-5
IT = 512
NIT = S // IT

MAGIC23 = float(1.5 * 2.0 ** 23)
MAGIC_HI = float(np.float32(1.5 * 2.0 ** 52 - 1.5 * 2.0 ** 23))

PHASE_B = True
import os
KSTAGE = int(os.environ.get("KSTAGE", "99"))
KSUB = int(os.environ.get("KSUB", "99"))
KPOST = int(os.environ.get("KPOST", "99"))
KEVAC = os.environ.get("KEVAC", "")


def host_constants():
    h = np.arange(NH, dtype=np.float64)
    slopes = 2.0 ** (-8.0 / NH * (h + 1.0))

    t = np.arange(S, dtype=np.float64)
    inv = 1.0 / THETA ** (np.arange(0, HD, 2, dtype=np.float64) / HD)
    freqs = np.outer(t, inv)
    emb = np.concatenate([freqs, freqs], axis=-1)
    cos_r = np.ascontiguousarray(
        np.cos(emb).reshape(NCHUNK, 128, HD).transpose(1, 0, 2)).astype(np.float32)
    sin_pm = np.sin(emb)
    sin_pm[:, :HD // 2] *= -1.0
    sin_r = np.ascontiguousarray(
        sin_pm.reshape(NCHUNK, 128, HD).transpose(1, 0, 2)).astype(np.float32)

    tc = np.arange(CHUNK, dtype=np.float64)
    dm = tc[None, :] - tc[:, None]
    maskT = np.where(dm[:, None, :] >= 0.0,
                     np.exp(-slopes[None, :, None] * dm[:, None, :]), 0.0)
    maskT = np.ascontiguousarray(maskT).astype(np.float32)

    qdec = np.exp(-slopes[:, None] * (tc + 1.0)[None, :])
    qdec_tile = np.empty((128, NH // 2, CHUNK), np.float32)
    for j in range(NH // 2):
        qdec_tile[:64, j, :] = qdec[2 * j]
        qdec_tile[64:, j, :] = qdec[2 * j + 1]

    kdec = np.exp(-slopes[None, :] * (CHUNK - 1.0 - tc)[:, None]).astype(np.float32)

    bd = np.exp(-slopes * CHUNK)
    bdvec = np.empty((128, NH // 2), np.float32)
    for j in range(NH // 2):
        bdvec[:64, j] = bd[2 * j]
        bdvec[64:, j] = bd[2 * j + 1]

    return dict(
        cosr=cos_r.reshape(128, NCHUNK * HD),
        sinr=sin_r.reshape(128, NCHUNK * HD),
        maskT=maskT.reshape(128, NH * CHUNK),
        qdec=qdec_tile.reshape(128, (NH // 2) * CHUNK),
        kdec=kdec, bdvec=bdvec,
        identb=np.eye(128, dtype=ml_dtypes.bfloat16),
        identf=np.eye(128, dtype=np.float32),
        onesb=np.ones((4, 128), dtype=ml_dtypes.bfloat16),
        onesfr=np.ones((4, 128), dtype=np.float32),
        onesfc=np.ones((128, 1), dtype=np.float32),
    )


def build_nc():
    nc = bass.Bass("TRN2", target_bir_lowering=False, debug=False,
                   detect_race_conditions=False)
    di = lambda n, shp, dt=F32: nc.dram_tensor(n, shp, dt, kind="ExternalInput").ap()

    x_h = di("x", [TOK, D])
    wcat_h = di("wcatT", [128, 8 * 1536])
    woT_h = di("woT", [128, 8 * D])
    w1T_h = di("w1T", [8, 128, DH])
    w2T_h = di("w2T", [32, 128, D])
    cwt_h = di("cwt", [128, FT * 3])
    cbias_h = di("cbias", [128, FT])
    cos_h = di("cosr", [128, NCHUNK * HD])
    sin_h = di("sinr", [128, NCHUNK * HD])
    maskT_h = di("maskT", [128, NH * CHUNK])
    qdec_h = di("qdec", [128, (NH // 2) * CHUNK])
    kdec_h = di("kdec", [128, NH])
    bdvec_h = di("bdvec", [128, NH // 2])
    identb_h = di("identb", [128, 128], BF16)
    identf_h = di("identf", [128, 128])
    onesb_h = di("onesb", [4, 128], BF16)
    onesfr_h = di("onesfr", [4, 128])
    onesfc_h = di("onesfc", [128, 1])

    out_h = nc.dram_tensor("out", [TOK, D], F32, kind="ExternalOutput").ap()
    scr_att = nc.dram_tensor("scr_att", [TOK, D], F32, kind="Internal").ap()
    scr_w1 = nc.dram_tensor("scr_w1", [8, 8, 128, 512], BF16, kind="Internal").ap()
    scr_w2 = nc.dram_tensor("scr_w2", [32, 128, D], BF16, kind="Internal").ap()

    esg = ExitStack()
    SB = lambda es, n, shp, dt=F32: es.enter_context(nc.sbuf_tensor(n, shp, dt))
    PSUM = lambda es, n, shp: es.enter_context(nc.psum_tensor(n, shp, F32))

    class Sem:
        __slots__ = ("h", "v")
        def __init__(self, name):
            self.h = esg.enter_context(nc.semaphore(name))
            self.v = 0
        def inc(self, instr, n=1):
            instr.then_inc(self.h, n)
            self.v += n
        def wait(self, e, v=None):
            e.wait_ge(self.h, self.v if v is None else v)

    mvec = SB(esg, "mvec", [128, 1])
    negM = SB(esg, "negM", [128, 1])
    actwarm = SB(esg, "actwarm", [128, 1])
    cwt_sb = SB(esg, "cwt_sb", [128, FT * 3])
    cbias_sb = SB(esg, "cbias_sb", [128, FT])
    identb_sb = SB(esg, "identb_sb", [128, 128], BF16)
    identf_sb = SB(esg, "identf_sb", [128, 128])
    onesb_sb = SB(esg, "onesb_sb", [4, 128], BF16)
    onesfr_sb = SB(esg, "onesfr_sb", [4, 128])
    onesfc_sb = SB(esg, "onesfc_sb", [128, 1])
    rall = SB(esg, "rall", [128, 2 * NCHUNK])
    vhist = SB(esg, "vhist", [128, 2 * NCHUNK])
    xhist = SB(esg, "xhist", [128, 2 * NCHUNK])
    lhist = SB(esg, "lhist", [128, 2 * NCHUNK])
    bhist = SB(esg, "bhist", [128, 2 * NCHUNK])
    ball = SB(esg, "ball", [128, 2 * NCHUNK])
    mw1 = SB(esg, "mw1", [128, 1])
    mw2 = SB(esg, "mw2", [128, 1])
    sw12 = SB(esg, "sw12", [128, 2])
    wcat_sb = SB(esg, "wcat_sb", [128, 8 * 1536], BF16)
    woT_sb = SB(esg, "woT_sb", [128, 8 * D], BF16)

    s_c = Sem("s_c"); s_x = Sem("s_x"); s_w = Sem("s_w"); s_gd = Sem("s_gd")
    s_v = Sem("s_v"); s_a = Sem("s_a"); s_t = Sem("s_t"); s_o = Sem("s_o")
    s_ts = Sem("s_ts"); s_tk = Sem("s_tk"); s_m = Sem("s_m"); s_k = Sem("s_k")
    s_e = Sem("s_e"); s_q = Sem("s_q"); s_we = Sem("s_we"); s_wo = Sem("s_wo")
    s_xb = Sem("s_xb"); s_scb = Sem("s_scb"); s_w1 = Sem("s_w1"); s_w2 = Sem("s_w2")
    s_t1 = Sem("s_t1"); s_t2 = Sem("s_t2"); s_xc = Sem("s_xc"); s_sce = Sem("s_sce")

    with nc.Block() as blk:
        def f_sync(e):
            for dst, src in ((cwt_sb, cwt_h), (cbias_sb, cbias_h), (identb_sb, identb_h),
                             (identf_sb, identf_h), (onesb_sb, onesb_h),
                             (onesfr_sb, onesfr_h), (onesfc_sb, onesfc_h)):
                s_c.inc(e.dma_start(out=dst[:], in_=src), 16)
        blk.sync(f_sync)

        def f_a_warm(e):
            s_c.wait(e)
            e.activation(out=actwarm[:], in_=onesfc_sb[:], func=AF.Ln)
            e.drain()
            e.activation(out=actwarm[:], in_=actwarm[:], func=AF.Exp)
            e.drain()
        blk.scalar(f_a_warm)

        def f_v(e):
            s_c.wait(e)
            e.memset(mvec[:], 0.25)
            e.drain()
            e.tensor_scalar(out=mvec[:], in0=mvec[:], scalar1=MAGIC23, scalar2=MAGIC23,
                            op0=ALU.add, op1=ALU.subtract)
            e.drain()
            e.tensor_scalar(out=mvec[:], in0=mvec[:], scalar1=0.25, scalar2=None, op0=ALU.is_equal)
            e.drain()
            e.tensor_scalar(out=mvec[:], in0=mvec[:], scalar1=MAGIC_HI, scalar2=MAGIC23,
                            op0=ALU.mult, op1=ALU.add)
            e.drain()
            s_v.inc(e.tensor_scalar_mul(out=negM[:], in0=mvec[:], scalar1=-1.0))
        blk.vector(f_v)

    es_p = ExitStack()
    wtile = [SB(es_p, f"wtile{i}", [128, DH]) for i in range(2)]
    wqb = [SB(es_p, f"wqb{i}", [128, DH], BF16) for i in range(2)]
    wsum1 = SB(es_p, "wsum1", [128, 1])
    wsum2 = SB(es_p, "wsum2", [128, 1])
    rtmp = SB(es_p, "rtmp", [128, 1])
    sc2 = SB(es_p, "sc2", [1, 2])
    wttmp = SB(es_p, "wttmp", [128, DH])
    ps_pp = PSUM(es_p, "ps_pp", [128, 16])

    NWT = 40
    def w_src(i):
        return (w1T_h[i], DH) if i < 8 else (w2T_h[i - 8], D)

    with nc.Block() as blk:
        v0 = s_v.v
        def f_sync(e):
            for i in range(NWT):
                if i >= 2:
                    e.wait_ge(s_v.h, v0 + i - 1)
                src, flen = w_src(i)
                s_w.inc(e.dma_start(out=wtile[i % 2][:, 0:flen], in_=src), 16)
        blk.sync(f_sync)

        def f_v(e):
            e.memset(wsum1[:], 0.0)
            e.memset(wsum2[:], 0.0)
            for i in range(NWT):
                _, flen = w_src(i)
                e.wait_ge(s_w.h, 16 * (i + 1))
                e.reduce_sum(out=rtmp[:], in_=wtile[i % 2][:, 0:flen], axis=AX.X,
                             apply_absolute_value=True)
                e.drain()
                acc = wsum1 if i < 8 else wsum2
                s_v.inc(e.tensor_add(out=acc[:], in0=acc[:], in1=rtmp[:]))
            s_v.inc(e.drain())
        blk.vector(f_v)

        def f_t(e):
            s_v.wait(e)
            e.matmul(out=ps_pp[0:1, 0:1], lhsT=wsum1[:], rhs=onesfc_sb[:], start=True, stop=True)
            s_t.inc(e.matmul(out=ps_pp[0:1, 1:2], lhsT=wsum2[:], rhs=onesfc_sb[:],
                             start=True, stop=True))
        blk.tensor(f_t)

        def f_v2(e):
            s_t.wait(e)
            e.tensor_scalar(out=sc2[0:1, 0:1], in0=ps_pp[0:1, 0:1],
                            scalar1=1.0 / (DH * D), scalar2=EPS, op0=ALU.mult, op1=ALU.max)
            e.tensor_scalar(out=sc2[0:1, 1:2], in0=ps_pp[0:1, 1:2],
                            scalar1=1.0 / (DH * D), scalar2=EPS, op0=ALU.mult, op1=ALU.max)
            s_v.inc(e.drain())
        blk.vector(f_v2)

        def f_t2(e):
            s_v.wait(e)
            s_t.inc(e.matmul(out=ps_pp[:, 2:4], lhsT=onesfr_sb[0:1, :], rhs=sc2[0:1, :],
                             start=True, stop=True))
        blk.tensor(f_t2)

        def f_v3(e):
            s_t.wait(e)
            e.tensor_copy(out=mw1[:], in_=ps_pp[:, 2:3])
            e.tensor_copy(out=mw2[:], in_=ps_pp[:, 3:4])
            e.drain()
            e.reciprocal(out=sw12[:, 0:1], in_=mw1[:])
            e.reciprocal(out=sw12[:, 1:2], in_=mw2[:])
            s_v.inc(e.drain())
        blk.vector(f_v3)

    with nc.Block() as blk:
        v0 = s_v.v
        g0 = s_gd.v
        def f_sync(e):
            s_v.wait(e)
            for i in range(NWT):
                if i >= 2:
                    e.wait_ge(s_v.h, v0 + i - 1)
                src, flen = w_src(i)
                s_w.inc(e.dma_start(out=wtile[i % 2][:, 0:flen], in_=src), 16)
        blk.sync(f_sync)

        w0 = s_w.v - 16 * NWT
        def f_v(e):
            for i in range(NWT):
                _, flen = w_src(i)
                col = 0 if i < 8 else 1
                e.wait_ge(s_w.h, w0 + 16 * (i + 1))
                if i >= 2:
                    e.wait_ge(s_gd.h, g0 + 16 * (i - 1))
                sl = (slice(0, 128), slice(0, flen))
                e.tensor_scalar(out=wttmp[sl], in0=wtile[i % 2][sl], scalar1=sw12[:, col:col + 1],
                                scalar2=mvec[:], op0=ALU.mult, op1=ALU.add)
                e.tensor_scalar(out=wttmp[sl], in0=wttmp[sl], scalar1=mvec[:],
                                scalar2=1.0, op0=ALU.subtract, op1=ALU.min)
                s_v.inc(e.tensor_scalar(out=wqb[i % 2][sl], in0=wttmp[sl],
                                        scalar1=-1.0, scalar2=None, op0=ALU.max))
        blk.vector(f_v)

        def f_g(e):
            for i in range(NWT):
                e.wait_ge(s_v.h, v0 + i + 1)
                if i < 8:
                    dst_ap = scr_w1[:, i].rearrange("g p f -> p g f")
                    s_gd.inc(e.dma_start(out=dst_ap, in_=wqb[i % 2][:]), 16)
                else:
                    s_gd.inc(e.dma_start(out=scr_w2[i - 8], in_=wqb[i % 2][:, 0:D]), 16)
        blk.gpsimd(f_g)

    with nc.Block() as blk:
        def f_g(e):
            s_gd.inc(e.dma_start(out=wcat_sb[:], in_=wcat_h), 16)
            s_gd.inc(e.dma_start(out=woT_sb[:], in_=woT_h), 16)
        blk.gpsimd(f_g)

    es_p.close()
    es_a = ExitStack()
    xs = [SB(es_a, f"xs{i}", [128, D]) for i in range(2)]
    att = [SB(es_a, f"att{i}", [128, D]) for i in range(2)]
    xsum = SB(es_a, "xsum", [128, 1])
    sqsum = SB(es_a, "sqsum", [128, 1])
    mcol = SB(es_a, "mcol", [128, 1])
    msq = SB(es_a, "msq", [128, 1])
    vtmp = SB(es_a, "vtmp", [128, 1])
    ltmp = SB(es_a, "ltmp", [128, 1])
    btmp = SB(es_a, "btmp", [128, 1])
    xn_bf = SB(es_a, "xn_bf", [128, D], BF16)
    xnT = SB(es_a, "xnT", [128, 8, 128], BF16)
    tmpa = SB(es_a, "tmpa", [128, D])
    tmpb = SB(es_a, "tmpb", [128, D])
    q_bf = SB(es_a, "q_bf", [128, D], BF16)
    k_bf = SB(es_a, "k_bf", [128, 256], BF16)
    v_bf = SB(es_a, "v_bf", [128, 256], BF16)
    qT = SB(es_a, "qT", [128, 8, 128], BF16)
    qdq = SB(es_a, "qdq", [128, 8, 128], BF16)
    kTba = SB(es_a, "kTba", [128, 4, 128], BF16)
    kdk = SB(es_a, "kdk", [128, NH * 64], BF16)
    sc_bf = SB(es_a, "sc_bf", [128, 4 * 128], BF16)
    oT = SB(es_a, "oT", [128, 8, 128], BF16)
    kv_sb = [SB(es_a, f"kv_sb{i}", [128, 8, 64]) for i in range(2)]
    kv_bf = SB(es_a, "kv_bf", [128, 8, 64], BF16)
    cos_sb = SB(es_a, "cos_sb", [128, NCHUNK * HD])
    sin_sb = SB(es_a, "sin_sb", [128, NCHUNK * HD])
    maskT_sb = SB(es_a, "maskT_sb", [128, NH * CHUNK])
    qdec_sb = SB(es_a, "qdec_sb", [128, (NH // 2) * CHUNK])
    kdec_sb = SB(es_a, "kdec_sb", [128, NH])
    bdvec_sb = SB(es_a, "bdvec_sb", [128, NH // 2])

    dbg_tensors = dict(xhist=xhist, vhist=vhist, lhist=lhist, bhist=bhist, xn_bf=xn_bf, q_bf=q_bf, k_bf=k_bf, v_bf=v_bf, qT=qT,
                       kTba=kTba, qdq=qdq, kdk=kdk, sc_bf=sc_bf, oT=oT,
                       kv_sb0=kv_sb[0], kv_bf=kv_bf, rall=rall, ball=ball,
                       mvec=mvec, negM=negM, wcat_sb=wcat_sb, woT_sb=woT_sb,
                       mw1=mw1, mw2=mw2, sw12=sw12, xnT=xnT)
    ps_qkv = PSUM(es_a, "ps_qkv", [128, 1536])
    ps_sctr = PSUM(es_a, "ps_sctr", [128, 1024])
    ps_o = PSUM(es_a, "ps_o", [128, 1024])
    ps_kao = PSUM(es_a, "ps_kao", [128, 512])

    def tr_slot(j):
        return ps_sctr[:, 512 * (j % 2):512 * (j % 2) + 128]

    def sc_slot(h):
        base = 512 * (h % 2) + 128 * ((h // 2) % 2)
        return ps_sctr[:, base:base + 128]

    def o_slot(j):
        base = 512 * (j % 2) + 128 * ((j // 2) % 2)
        return ps_o[:, base:base + 128]

    with nc.Block() as blk:
        def f_sync(e):
            e.wait_ge(s_gd.h, s_gd.v)
            for dst, src in ((cos_sb, cos_h), (sin_sb, sin_h), (maskT_sb, maskT_h),
                             (qdec_sb, qdec_h), (kdec_sb, kdec_h), (bdvec_sb, bdvec_h)):
                s_c.inc(e.dma_start(out=dst[:], in_=src), 16)
            s_x.inc(e.dma_start(out=xs[0][:], in_=x_h[0:128]), 16)
        blk.sync(f_sync)

    rope_done = {}
    attst_target = scr_att if (PHASE_B and KSTAGE >= 4) else out_h

    n_chunks = 0 if KSTAGE < 2 else (1 if KSTAGE == 2 else 2 * NCHUNK)
    for c in range(n_chunks):
        seq, cc = divmod(c, NCHUNK)
        col = slice(c, c + 1)
        with nc.Block() as blk:
            if c + 1 < 2 * NCHUNK:
                def f_sync(e, c=c):
                    if c + 1 >= 2:
                        e.wait_ge(s_we.h, 2 * c)
                    s_x.inc(e.dma_start(out=xs[(c + 1) % 2][:],
                                        in_=x_h[128 * (c + 1):128 * (c + 2)]), 16)
                blk.sync(f_sync)

            def f_v(e, c=c):
                e.wait_ge(s_x.h, 16 * (c + 1))
                e.reduce_sum(out=xsum[:], in_=xs[c % 2][:], axis=AX.X)
                s_v.inc(e.drain())
            blk.vector(f_v)

            def f_a(e, c=c):
                s_v.wait(e)
                e.activation(out=tmpa[:], in_=xs[c % 2][:], func=AF.Square,
                             accum_out=sqsum[:])
                s_a.inc(e.drain())
            blk.scalar(f_a)

            def f_v(e, c=c, col=col):
                s_a.wait(e)
                e.tensor_scalar_mul(out=mcol[:], in0=xsum[:], scalar1=1.0 / D)
                e.drain()
                e.tensor_mul(out=msq[:], in0=mcol[:], in1=mcol[:])
                e.tensor_scalar(out=vtmp[:], in0=sqsum[:], scalar1=1.0 / D, scalar2=EPS,
                                op0=ALU.mult, op1=ALU.add)
                e.drain()
                e.tensor_sub(out=vtmp[:], in0=vtmp[:], in1=msq[:])
                s_v.inc(e.drain())
            blk.vector(f_v)

            def f_a(e, col=col):
                s_v.wait(e)
                e.activation(out=ltmp[:], in_=vtmp[:], func=AF.Ln)
                e.drain()
                e.activation(out=rall[:, col], in_=ltmp[:], func=AF.Exp, scale=-0.5)
                s_a.inc(e.drain())
            blk.scalar(f_a)

            def f_v(e, col=col):
                s_a.wait(e)
                e.tensor_mul(out=btmp[:], in0=mcol[:], in1=rall[:, col])
                e.drain()
                e.tensor_scalar_mul(out=ball[:, col], in0=btmp[:], scalar1=-1.0)
                s_v.inc(e.drain())
            blk.vector(f_v)

            def f_a(e, c=c, col=col):
                s_v.wait(e)
                s_a.inc(e.activation(out=xn_bf[:], in_=xs[c % 2][:], func=AF.Identity,
                                     scale=rall[:, col], bias=ball[:, col]))
            blk.scalar(f_a)

            if KSUB < 2:
                continue
            tb = s_t.v; ab = s_a.v
            def f_t(e, tb=tb, ab=ab, c=c):
                s_a.wait(e)
                if c == 0:
                    s_c.wait(e)
                else:
                    e.wait_ge(s_m.h, 16 * c)
                for j in range(8):
                    if j >= 2:
                        e.wait_ge(s_a.h, ab + j - 1)
                    s_t.inc(e.matmul(out=tr_slot(j),
                                     lhsT=xn_bf[:, 128 * j:128 * (j + 1)], rhs=identb_sb[:],
                                     start=True, stop=True))
            blk.tensor(f_t)

            def f_a(e, tb=tb):
                for j in range(8):
                    e.wait_ge(s_t.h, tb + j + 1)
                    s_a.inc(e.copy(out=xnT[:, j, :], in_=tr_slot(j)))
            blk.scalar(f_a)

            if KSUB < 3:
                continue
            ab = s_a.v
            def f_t(e, c=c, ab=ab):
                e.wait_ge(s_a.h, ab)
                e.wait_ge(s_gd.h, 16 * 42)
                if c > 0:
                    e.wait_ge(s_v.h, rope_done[c - 1])
                for dc in range(8):
                    for b3 in range(3):
                        ins = e.matmul(out=ps_qkv[:, 512 * b3:512 * (b3 + 1)],
                                       lhsT=xnT[:, dc, :],
                                       rhs=wcat_sb[:, 1536 * dc + 512 * b3:1536 * dc + 512 * (b3 + 1)],
                                       start=(dc == 0), stop=(dc == 7))
                s_t.inc(ins)
            blk.tensor(f_t)

            def f_v(e, c=c, cc=cc):
                s_t.wait(e)
                if c == 0:
                    s_c.wait(e)
                csl = cos_sb[:, HD * cc:HD * (cc + 1)].rearrange("p (o d) -> p o d", o=1)
                ssl = sin_sb[:, HD * cc:HD * (cc + 1)].rearrange("p (o d) -> p o d", o=1)
                qv = ps_qkv[:, 0:D].rearrange("p (h d) -> p h d", h=NH)
                ta = tmpa[:].rearrange("p (h d) -> p h d", h=NH)
                tb_ = tmpb[:].rearrange("p (h d) -> p h d", h=NH)
                e.tensor_mul(out=ta, in0=qv, in1=csl.broadcast_to((128, NH, HD)))
                e.tensor_mul(out=tb_[:, :, 0:32], in0=qv[:, :, 32:64],
                             in1=ssl[:, :, 0:32].broadcast_to((128, NH, 32)))
                e.tensor_mul(out=tb_[:, :, 32:64], in0=qv[:, :, 0:32],
                             in1=ssl[:, :, 32:64].broadcast_to((128, NH, 32)))
                e.tensor_add(out=q_bf[:], in0=tmpa[:], in1=tmpb[:])
                kv_ = ps_qkv[:, D:D + 256].rearrange("p (h d) -> p h d", h=NKV)
                ta4 = tmpa[:, 0:256].rearrange("p (h d) -> p h d", h=NKV)
                tb4 = tmpb[:, 0:256].rearrange("p (h d) -> p h d", h=NKV)
                e.tensor_mul(out=ta4, in0=kv_, in1=csl.broadcast_to((128, NKV, HD)))
                e.tensor_mul(out=tb4[:, :, 0:32], in0=kv_[:, :, 32:64],
                             in1=ssl[:, :, 0:32].broadcast_to((128, NKV, 32)))
                e.tensor_mul(out=tb4[:, :, 32:64], in0=kv_[:, :, 0:32],
                             in1=ssl[:, :, 32:64].broadcast_to((128, NKV, 32)))
                e.tensor_add(out=k_bf[:], in0=tmpa[:, 0:256], in1=tmpb[:, 0:256])
                e.tensor_copy(out=v_bf[:], in_=ps_qkv[:, D + 256:D + 512])
                for hh in range(NH):
                    g = hh // 4
                    ins = e.tensor_scalar_mul(out=kdk[:, 64 * hh:64 * (hh + 1)],
                                              in0=k_bf[:, 64 * g:64 * (g + 1)],
                                              scalar1=kdec_sb[:, hh:hh + 1])
                s_v.inc(ins)
                rope_done[c] = s_v.v
            blk.vector(f_v)

            if KSUB < 4:
                continue
            tb = s_t.v; ab = s_a.v
            def f_t(e, tb=tb, ab=ab, c=c):
                e.wait_ge(s_v.h, rope_done[c])
                for j in range(8):
                    if j >= 2:
                        e.wait_ge(s_a.h, ab + j - 1)
                    s_t.inc(e.matmul(out=tr_slot(j),
                                     lhsT=q_bf[:, 128 * j:128 * (j + 1)], rhs=identb_sb[:],
                                     start=True, stop=True))
                for g in range(4):
                    jj = 8 + g
                    e.wait_ge(s_a.h, ab + jj - 1)
                    sl = tr_slot(jj)
                    e.matmul(out=sl[0:64, :], lhsT=k_bf[:, 64 * g:64 * (g + 1)],
                             rhs=identb_sb[:], start=True, stop=True)
                    s_t.inc(e.matmul(out=sl[64:128, :], lhsT=k_bf[:, 64 * g:64 * (g + 1)],
                                     rhs=identb_sb[:], start=True, stop=True))
            blk.tensor(f_t)

            def f_a(e, tb=tb):
                for j in range(8):
                    e.wait_ge(s_t.h, tb + j + 1)
                    s_a.inc(e.copy(out=qT[:, j, :], in_=tr_slot(j)))
                for g in range(4):
                    jj = 8 + g
                    e.wait_ge(s_t.h, tb + jj + 1)
                    s_a.inc(e.copy(out=kTba[:, g, :], in_=tr_slot(jj)))
            blk.scalar(f_a)

            ab = s_a.v
            def f_v(e, ab=ab, c=c):
                e.wait_ge(s_a.h, ab - 4)
                if c == 0:
                    s_c.wait(e)
                for j in range(8):
                    ins = e.tensor_mul(out=qdq[:, j, :], in0=qT[:, j, :],
                                       in1=qdec_sb[:, 128 * j:128 * (j + 1)])
                s_q.inc(ins)
            blk.vector(f_v)

            if KSUB < 5:
                continue
            TSb = s_ts.v; TKb = s_tk.v; Mb = s_m.v; Kb = s_k.v; Eb = s_e.v
            Ab = s_a.v
            def f_t(e, c=c, cc=cc, TSb=TSb, TKb=TKb, Mb=Mb, Kb=Kb, Eb=Eb, Ab=Ab):
                e.wait_ge(s_a.h, Ab)
                e.wait_ge(s_q.h, s_q.v)
                if c > 0:
                    e.wait_ge(s_we.h, 2 * c)
                for hh in range(NH):
                    g = hh // 4
                    p64 = 64 * (hh % 2)
                    j = hh // 2
                    slot = hh % 4
                    if hh >= 2:
                        e.wait_ge(s_m.h, Mb + hh - 1)
                    s_ts.inc(e.matmul(out=sc_slot(hh),
                                      lhsT=kTba[p64:p64 + 64, g, :],
                                      rhs=qT[p64:p64 + 64, hh // 2, :],
                                      start=True, stop=True))
                    e.wait_ge(s_m.h, Mb + hh + 1)
                    if j >= 2 and hh % 2 == 0:
                        e.wait_ge(s_e.h, Eb + j - 1)
                    first = (cc == 0)
                    e.matmul(out=o_slot(j)[p64:p64 + 64, :],
                             lhsT=v_bf[:, 64 * g:64 * (g + 1)],
                             rhs=sc_bf[:, 128 * slot:128 * (slot + 1)],
                             start=True, stop=first)
                    if not first:
                        e.wait_ge(s_k.h, Kb - 8 + j + 1)
                        e.matmul(out=o_slot(j)[p64:p64 + 64, :],
                                 lhsT=kv_bf[p64:p64 + 64, j, :],
                                 rhs=qdq[p64:p64 + 64, j, :],
                                 start=False, stop=True)
                    kslot = j % 4
                    if j >= 1 and hh % 2 == 0:
                        e.wait_ge(s_k.h, Kb + j)
                    ins = e.matmul(out=ps_kao[p64:p64 + 64, 64 * kslot:64 * (kslot + 1)],
                                   lhsT=kdk[:, 64 * hh:64 * (hh + 1)],
                                   rhs=v_bf[:, 64 * g:64 * (g + 1)],
                                   start=True, stop=True)
                    if hh % 2 == 1:
                        s_tk.inc(ins)
            blk.tensor(f_t)

            def f_v(e, cc=cc, TSb=TSb, TKb=TKb, c=c):
                if c == 0:
                    s_c.wait(e)
                for hh in range(NH):
                    slot = hh % 4
                    e.wait_ge(s_ts.h, TSb + hh + 1)
                    s_m.inc(e.tensor_mul(out=sc_bf[:, 128 * slot:128 * (slot + 1)],
                                         in0=sc_slot(hh),
                                         in1=maskT_sb[:, 128 * hh:128 * (hh + 1)]))
                    if hh % 2 == 1:
                        j = hh // 2
                        kslot = j % 4
                        e.wait_ge(s_tk.h, TKb + j + 1)
                        if cc == 0:
                            e.tensor_copy(out=kv_sb[c % 2][:, j, :],
                                          in_=ps_kao[:, 64 * kslot:64 * (kslot + 1)])
                            e.tensor_copy(out=kv_bf[:, j, :],
                                          in_=ps_kao[:, 64 * kslot:64 * (kslot + 1)])
                            s_k.inc(e.drain())
                        else:
                            e.scalar_tensor_tensor(
                                out=kv_sb[c % 2][:, j, :], in0=kv_sb[(c - 1) % 2][:, j, :],
                                scalar=bdvec_sb[:, j:j + 1],
                                in1=ps_kao[:, 64 * kslot:64 * (kslot + 1)],
                                op0=ALU.mult, op1=ALU.add)
                            e.scalar_tensor_tensor(
                                out=kv_bf[:, j, :], in0=kv_sb[(c - 1) % 2][:, j, :],
                                scalar=bdvec_sb[:, j:j + 1],
                                in1=ps_kao[:, 64 * kslot:64 * (kslot + 1)],
                                op0=ALU.mult, op1=ALU.add)
                            s_k.inc(e.drain())
            blk.vector(f_v)

            def f_a(e, TKb=TKb):
                for j in range(8):
                    e.wait_ge(s_tk.h, TKb + j + 1)
                    s_e.inc(e.copy(out=oT[:, j, :], in_=o_slot(j)))
            blk.scalar(f_a)

            if KSUB < 6:
                continue
            Eb2 = s_e.v
            def f_t(e, c=c, Eb2=Eb2):
                e.wait_ge(s_e.h, Eb2)
                s_k.wait(e)
                for half in range(2):
                    if 2 * c + half >= 1:
                        e.wait_ge(s_we.h, 2 * c + half)
                    for j in range(8):
                        ins = e.matmul(out=ps_kao[:],
                                       lhsT=oT[:, j, :],
                                       rhs=woT_sb[:, D * j + 512 * half:D * j + 512 * (half + 1)],
                                       start=(j == 0), stop=(j == 7))
                    s_wo.inc(ins)
            blk.tensor(f_t)

            def f_v(e, c=c):
                for half in range(2):
                    e.wait_ge(s_wo.h, 2 * c + half + 1)
                    if c >= 2:
                        e.wait_ge(s_o.h, 16 * (c - 1))
                    s_we.inc(e.tensor_add(out=att[c % 2][:, 512 * half:512 * (half + 1)],
                                          in0=ps_kao[:],
                                          in1=xs[c % 2][:, 512 * half:512 * (half + 1)]))
            blk.vector(f_v)

            def f_sync(e, c=c):
                e.wait_ge(s_we.h, 2 * c + 2)
                s_o.inc(e.dma_start(out=attst_target[128 * c:128 * (c + 1)],
                                    in_=att[c % 2][:]), 16)
            blk.sync(f_sync)

    es_a.close()

    if PHASE_B and KSTAGE >= 4:
        es_b = ExitStack()
        hb = [SB(es_b, f"hb{i}", [128, FT, IT], BF16) for i in range(2)]
        hL = [SB(es_b, f"hL{i}", [128, FT], BF16) for i in range(2)]
        x_pb = [SB(es_b, f"x_pb{i}", [128, D]) for i in range(2)]
        scr_pb = [SB(es_b, f"scr_pb{i}", [128, D]) for i in range(2)]
        outsb = [SB(es_b, f"outsb{i}", [128, D]) for i in range(2)]
        xn_t = SB(es_b, "xn_t", [128, D])
        t1_t = SB(es_b, "t1_t", [128, D])
        xq_sb = SB(es_b, "xq_sb", [128, 4, D], BF16)
        xqT_sb = SB(es_b, "xqT_sb", [128, 8, IT], BF16)
        w1g = [SB(es_b, f"w1g{i}", [128, 8, 512], BF16) for i in range(2)]
        w2g = [SB(es_b, f"w2g{i}", [128, 4, D], BF16) for i in range(2)]
        ctmp = [SB(es_b, f"ctmp{i}", [128, IT]) for i in range(2)]
        mtree = SB(es_b, "mtree", [128, IT])
        m2c = SB(es_b, "m2c", [128, 4])
        s2v = SB(es_b, "s2v", [128, 4])
        c2v = SB(es_b, "c2v", [128, 4])
        s4blk = SB(es_b, "s4blk", [1, IT])
        s2b = SB(es_b, "s2b", [128, IT])
        q1s = SB(es_b, "q1s", [128, 1])
        q1c = SB(es_b, "q1c", [128, 1])
        q1b = SB(es_b, "q1b", [128, 1])
        dbg_tensors.update(hb0=hb[0], hb1=hb[1], mtree=mtree, m2c=m2c, s2v=s2v,
                           c2v=c2v, s2b=s2b, xq_sb=xq_sb, xqT_sb=xqT_sb, hL0=hL[0], hL1=hL[1])
        ps_mm = PSUM(es_b, "ps_mm", [128, 2048])
        ps_tr2 = PSUM(es_b, "ps_tr2", [128, 1024])
        tr2_slot = lambda i: ps_tr2[:, 512 * (i % 2):512 * (i % 2) + 128]
        ps_bc2 = PSUM(es_b, "ps_bc2", [128, 512])

        def emit_h(it):
            seq, k = divmod(it, NIT)
            with nc.Block() as blk:
                def f_sync(e, it=it):
                    if it == 0:
                        e.wait_ge(s_o.h, s_o.v)
                    for tt in range(4):
                        gtt = 4 * it + tt
                        if gtt >= 2:
                            e.wait_ge(s_xc.h, gtt - 1)
                        row = 128 * gtt
                        s_xb.inc(e.dma_start(out=x_pb[gtt % 2][:],
                                             in_=x_h[row:row + 128]), 16)
                    for g in range(8):
                        gg = 8 * it + g
                        if gg >= 2:
                            e.wait_ge(s_t1.h, gg - 1)
                        s_w1.inc(e.dma_start(out=w1g[g % 2][:],
                                             in_=scr_w1[g].rearrange("dc p f -> p dc f")), 16)
                blk.sync(f_sync)

                for tt in range(4):
                    gtt = 4 * it + tt
                    col = slice(seq * NCHUNK + 4 * k + tt, seq * NCHUNK + 4 * k + tt + 1)
                    def f_a1(e, gtt=gtt, col=col):
                        e.wait_ge(s_xb.h, 16 * (gtt + 1))
                        s_a.inc(e.activation(out=xn_t[:], in_=x_pb[gtt % 2][:], func=AF.Identity,
                                             scale=rall[:, col], bias=ball[:, col]))
                        s_xc.inc(e.drain())
                    blk.scalar(f_a1)

                    def f_v1(e):
                        s_a.wait(e)
                        e.reduce_max(out=q1s[:], in_=xn_t[:], axis=AX.X, apply_absolute_value=True)
                        e.drain()
                        e.tensor_scalar(out=q1s[:], in0=q1s[:], scalar1=1e-5, scalar2=None, op0=ALU.max)
                        e.drain()
                        e.tensor_scalar_mul(out=q1c[:], in0=q1s[:], scalar1=1.0 / 127.0)
                        e.reciprocal(out=q1s[:], in_=q1s[:])
                        e.drain()
                        e.tensor_mul(out=q1c[:], in0=q1c[:], in1=mw1[:])
                        e.tensor_scalar_mul(out=q1s[:], in0=q1s[:], scalar1=127.0)
                        e.drain()
                        e.tensor_mul(out=q1b[:], in0=negM[:], in1=q1c[:])
                        s_v.inc(e.drain())
                    blk.vector(f_v1)

                    def f_a2(e, tt=tt):
                        s_v.wait(e)
                        e.activation(out=t1_t[:], in_=xn_t[:], func=AF.Identity,
                                     scale=q1s[:], bias=mvec[:])
                        e.activation(out=xn_t[:], in_=t1_t[:], func=AF.Identity,
                                     bias=negM[:])
                        s_a.inc(e.activation(out=xq_sb[:, tt, :], in_=xn_t[:], func=AF.Identity,
                                             scale=q1c[:]))
                    blk.scalar(f_a2)

                tb = s_t.v; ab = s_a.v
                def f_t(e, tb=tb, ab=ab):
                    for tt in range(4):
                        e.wait_ge(s_a.h, ab - 8 + 2 * (tt + 1))
                        for j in range(8):
                            idx = 8 * tt + j
                            if idx >= 2:
                                e.wait_ge(s_a.h, ab + idx - 1)
                            s_t.inc(e.matmul(out=tr2_slot(idx),
                                             lhsT=xq_sb[:, tt, 128 * j:128 * (j + 1)],
                                             rhs=identb_sb[:], start=True, stop=True))
                blk.tensor(f_t)

                def f_a(e, tb=tb):
                    for idx in range(32):
                        tt, j = divmod(idx, 8)
                        e.wait_ge(s_t.h, tb + idx + 1)
                        s_a.inc(e.copy(out=xqT_sb[:, j, 128 * tt:128 * (tt + 1)],
                                       in_=tr2_slot(idx)))
                blk.scalar(f_a)

                ab = s_a.v
                vb_mm1 = s_v.v
                def f_t(e, it=it, ab=ab, vb_mm1=vb_mm1):
                    e.wait_ge(s_a.h, ab)
                    for g in range(8):
                        e.wait_ge(s_w1.h, 16 * (8 * it + g + 1))
                        if g >= 1:
                            e.wait_ge(s_v.h, vb_mm1 + 4 * g)
                        for dc in range(8):
                            for fi in range(4):
                                ins = e.matmul(out=ps_mm[:, 512 * fi:512 * (fi + 1)],
                                               lhsT=w1g[g % 2][:, dc, 128 * fi:128 * (fi + 1)],
                                               rhs=xqT_sb[:, dc, :],
                                               start=(dc == 0), stop=(dc == 7))
                        s_t1.inc(ins)
                blk.tensor(f_t)

                t1b = s_t1.v - 8
                def f_v(e, it=it, t1b=t1b):
                    for g in range(8):
                        e.wait_ge(s_t1.h, t1b + g + 1)
                        for fi in range(4):
                            ct = 4 * g + fi
                            s_v.inc(e.tensor_copy(out=hb[it % 2][:, ct, :],
                                                  in_=ps_mm[:, 512 * fi:512 * (fi + 1)]))
                blk.vector(f_v)
            return s_v.v

        def emit_post(j, h_done_v, hnext_done_v):
            seq, k = divmod(j, NIT)
            with nc.Block() as blk:
                def f_v0(e, j=j):
                    e.wait_ge(s_v.h, h_done_v)
                    if hnext_done_v is not None:
                        e.wait_ge(s_v.h, hnext_done_v)
                    s_v.inc(e.tensor_copy(out=hL[(j + 1) % 2][:],
                                          in_=hb[j % 2][:, :, IT - 1]))
                blk.vector(f_v0)

                for ct in range(FT):
                    w0 = cwt_sb[:, 3 * ct + 0:3 * ct + 1]
                    w1 = cwt_sb[:, 3 * ct + 1:3 * ct + 2]
                    w2 = cwt_sb[:, 3 * ct + 2:3 * ct + 3]
                    def f_v(e, ct=ct, j=j, k=k, w0=w0, w1=w1, w2=w2):
                        if ct >= 2:
                            e.wait_ge(s_a.h, s_a.v - 1)
                        buf = ctmp[ct % 2]
                        e.tensor_scalar_mul(out=buf[:], in0=hb[j % 2][:, ct, :], scalar1=w1)
                        e.scalar_tensor_tensor(out=buf[:, 1:IT], in0=hb[j % 2][:, ct, 0:IT - 1],
                                               scalar=w0, in1=buf[:, 1:IT],
                                               op0=ALU.mult, op1=ALU.add)
                        ins = e.scalar_tensor_tensor(out=buf[:, 0:IT - 1], in0=hb[j % 2][:, ct, 1:IT],
                                                     scalar=w2, in1=buf[:, 0:IT - 1],
                                                     op0=ALU.mult, op1=ALU.add)
                        if k < NIT - 1:
                            ins = e.scalar_tensor_tensor(out=buf[:, IT - 1:IT],
                                                         in0=hb[(j + 1) % 2][:, ct, 0:1],
                                                         scalar=w2, in1=buf[:, IT - 1:IT],
                                                         op0=ALU.mult, op1=ALU.add)
                        if k > 0:
                            ins = e.scalar_tensor_tensor(out=buf[:, 0:1], in0=hL[j % 2][:, ct:ct + 1],
                                                         scalar=w0, in1=buf[:, 0:1],
                                                         op0=ALU.mult, op1=ALU.add)
                        s_v.inc(e.drain())
                    blk.vector(f_v)

                    def f_a(e, ct=ct, j=j):
                        s_v.wait(e)
                        s_a.inc(e.activation(out=hb[j % 2][:, ct, :], in_=ctmp[ct % 2][:],
                                             func=AF.Silu, bias=cbias_sb[:, ct:ct + 1]))
                    blk.scalar(f_a)

                if KPOST < 2:
                    return
                ab = s_a.v
                def f_v(e, j=j, ab=ab):
                    e.wait_ge(s_a.h, ab)
                    hbr = hb[j % 2][:].rearrange("p ct t -> p t ct")
                    mr = mtree[:].rearrange("p (t o) -> p t o", o=1)
                    s_v.inc(e.reduce_max(out=mr, in_=hbr, axis=AX.X,
                                         apply_absolute_value=True))
                blk.vector(f_v)

                if KPOST < 3:
                    return
                tb = s_t.v
                def f_t(e, tb=tb):
                    s_v.wait(e)
                    for tt in range(4):
                        if tt >= 2:
                            e.wait_ge(s_v.h, s_v.v + tt - 1)
                        s_t.inc(e.matmul(out=tr2_slot(tt),
                                         lhsT=mtree[:, 128 * tt:128 * (tt + 1)],
                                         rhs=identf_sb[:], start=True, stop=True))
                blk.tensor(f_t)

                def f_v(e, tb=tb):
                    for tt in range(4):
                        e.wait_ge(s_t.h, tb + tt + 1)
                        s_v.inc(e.reduce_max(out=m2c[:, tt:tt + 1],
                                             in_=tr2_slot(tt),
                                             axis=AX.X))
                    e.drain()
                    e.tensor_scalar(out=m2c[:], in0=m2c[:], scalar1=1e-5, scalar2=None, op0=ALU.max)
                    e.drain()
                    e.reciprocal(out=s2v[:], in_=m2c[:])
                    e.tensor_scalar_mul(out=c2v[:], in0=m2c[:], scalar1=1.0 / 127.0)
                    e.drain()
                    e.tensor_scalar_mul(out=s2v[:], in0=s2v[:], scalar1=127.0)
                    e.tensor_mul(out=c2v[:], in0=c2v[:], in1=mw2[:].broadcast_to((128, 4)))
                    e.drain()
                    s_v.inc(e.nop())
                blk.vector(f_v)

                tb = s_t.v
                def f_t(e, tb=tb):
                    s_v.wait(e)
                    for tt in range(4):
                        ins = e.matmul(out=ps_bc2[0:1, 128 * tt:128 * (tt + 1)],
                                       lhsT=s2v[:, tt:tt + 1], rhs=identf_sb[:],
                                       start=True, stop=True)
                    s_t.inc(ins)
                blk.tensor(f_t)

                def f_v(e, tb=tb):
                    e.wait_ge(s_t.h, tb + 1)
                    e.tensor_copy(out=s4blk[0:1, :], in_=ps_bc2[0:1, :])
                    s_v.inc(e.drain())
                blk.vector(f_v)

                tb = s_t.v
                def f_t(e, tb=tb):
                    s_v.wait(e)
                    s_t.inc(e.matmul(out=ps_bc2[:], lhsT=onesfr_sb[0:1, :], rhs=s4blk[0:1, :],
                                     start=True, stop=True))
                blk.tensor(f_t)

                if KPOST < 4:
                    return
                def f_v(e, tb=tb, j=j):
                    e.wait_ge(s_t.h, tb + 1)
                    e.tensor_copy(out=s2b[:], in_=ps_bc2[:])
                    for ct in range(FT):
                        buf = ctmp[ct % 2]
                        e.tensor_mul(out=buf[:], in0=hb[j % 2][:, ct, :], in1=s2b[:])
                        e.tensor_scalar(out=hb[j % 2][:, ct, :], in0=buf[:],
                                        scalar1=mvec[:], scalar2=mvec[:],
                                        op0=ALU.add, op1=ALU.subtract)
                    s_q.inc(e.drain())
                blk.vector(f_v)

                if KPOST < 5:
                    return
                TPN = 1 if KPOST == 5 else 2
                qv = s_q.v
                def f_sync(e, j=j, seq=seq, k=k):
                    for tt in range(min(2, 2 * TPN)):
                        gtt = 4 * j + tt
                        if gtt >= 2:
                            e.wait_ge(s_sce.h, gtt - 1)
                        row = 128 * gtt
                        s_scb.inc(e.dma_start(out=scr_pb[gtt % 2][:], in_=scr_att[row:row + 128]), 16)
                    for tp in range(TPN):
                        for cg in range(8):
                            gi = 16 * j + 8 * tp + cg
                            if gi >= 2:
                                e.wait_ge(s_t2.h, gi - 1)
                            s_w2.inc(e.dma_start(
                                out=w2g[gi % 2][:],
                                in_=scr_w2[4 * cg:4 * cg + 4].rearrange("cc p m -> p cc m")), 16)
                    for tt in range(2, 2 * TPN):
                        gtt = 4 * j + tt
                        e.wait_ge(s_sce.h, gtt - 1)
                        row = 128 * gtt
                        s_scb.inc(e.dma_start(out=scr_pb[gtt % 2][:], in_=scr_att[row:row + 128]), 16)
                blk.sync(f_sync)

                def f_t(e, j=j, qv=qv):
                    e.wait_ge(s_q.h, qv)
                    for tp in range(TPN):
                        if tp == 1:
                            e.wait_ge(s_v.h, s_v.v + 4)
                        for cg in range(8):
                            gi = 16 * j + 8 * tp + cg
                            e.wait_ge(s_w2.h, 16 * (gi + 1))
                            for c4 in range(4):
                                ct = 4 * cg + c4
                                for ti in range(2):
                                    tt = 2 * tp + ti
                                    for mh in range(2):
                                        ins = e.matmul(
                                            out=ps_mm[:, 1024 * ti + 512 * mh:1024 * ti + 512 * (mh + 1)],
                                            lhsT=hb[j % 2][:, ct, 128 * tt:128 * (tt + 1)],
                                            rhs=w2g[gi % 2][:, c4, 512 * mh:512 * (mh + 1)],
                                            start=(cg == 0 and c4 == 0),
                                            stop=(cg == 7 and c4 == 3))
                            s_t2.inc(ins)
                blk.tensor(f_t)

                t2b = s_t2.v - 8 * TPN
                def f_v(e, j=j, t2b=t2b):
                    for tp in range(TPN):
                        e.wait_ge(s_t2.h, t2b + 8 * (tp + 1))
                        for ti in range(2):
                            tt = 2 * tp + ti
                            gtt = 4 * j + tt
                            e.wait_ge(s_scb.h, 16 * (gtt + 1))
                            if gtt >= 2:
                                e.wait_ge(s_o.h, 16 * (32 + gtt - 1))
                            for mh in range(2):
                                osl = outsb[gtt % 2][:, 512 * mh:512 * (mh + 1)]
                                psl = ps_mm[:, 1024 * ti + 512 * mh:1024 * ti + 512 * (mh + 1)]
                                ssl = scr_pb[gtt % 2][:, 512 * mh:512 * (mh + 1)]
                                if KEVAC == "scr":
                                    s_v.inc(e.tensor_copy(out=osl, in_=ssl))
                                elif KEVAC == "mm2":
                                    s_v.inc(e.tensor_scalar_mul(out=osl, in0=psl,
                                                                scalar1=c2v[:, tt:tt + 1]))
                                else:
                                    s_v.inc(e.scalar_tensor_tensor(
                                        out=osl, in0=psl, scalar=c2v[:, tt:tt + 1],
                                        in1=ssl, op0=ALU.mult, op1=ALU.add))
                            s_sce.inc(e.drain())
                blk.vector(f_v)

                vb = s_v.v - 4 * TPN
                def f_sync2(e, j=j, vb=vb):
                    for tt in range(2 * TPN):
                        gtt = 4 * j + tt
                        e.wait_ge(s_v.h, vb + 2 * (tt + 1))
                        row = 128 * gtt
                        s_o.inc(e.dma_start(out=out_h[row:row + 128], in_=outsb[gtt % 2][:]), 16)
                blk.sync(f_sync2)

        h_done = {}
        if KSTAGE == 4:
            h_done[0] = emit_h(0)
        elif KSTAGE == 5:
            h_done[0] = emit_h(0)
            h_done[1] = emit_h(1)
            emit_post(0, h_done[0], h_done[1])
        else:
            for seq in range(2):
                for k in range(NIT):
                    it = NIT * seq + k
                    h_done[it] = emit_h(it)
                    if k >= 1:
                        emit_post(it - 1, h_done[it - 1], h_done[it])
                emit_post(NIT * seq + NIT - 1, h_done[NIT * seq + NIT - 1], None)

        es_b.close()
    else:
        with nc.Block() as blk:
            blk.gpsimd(lambda e: e.wait_ge(s_o.h, s_o.v))

    kdump = os.environ.get("KDUMP", "")
    if kdump:
        dbg = {}
        for nm in kdump.split(","):
            t = dbg_tensors[nm]
            shp = [t.shape[0], int(np.prod(t.shape[1:]))] if len(t.shape) > 2 else list(t.shape)
            dt = t[:].dtype if hasattr(t[:], 'dtype') else F32
            dbg[nm] = nc.dram_tensor("dbg_" + nm, shp, dt, kind="ExternalOutput").ap()
        with nc.Block() as blk:
            def f_g(e):
                e.wait_ge(s_o.h, s_o.v)
                for nm in kdump.split(","):
                    t = dbg_tensors[nm]
                    s_gd.inc(e.dma_start(out=dbg[nm], in_=t[:].rearrange("p ... -> p (...)") if len(t.shape) > 2 else t[:]), 16)
            blk.gpsimd(f_g)

    with nc.Block() as blk:
        def f_g(e):
            e.wait_ge(s_o.h, s_o.v)
            e.wait_ge(s_gd.h, s_gd.v)
        blk.gpsimd(f_g)

    esg.close()
    return nc


_NC_CACHE = None


def kernel(**inputs):
    global _NC_CACHE
    x = np.ascontiguousarray(np.asarray(inputs["L0_attn_x"], np.float32))
    Wq = np.asarray(inputs["L0_attn_Wq"], np.float32)
    Wk = np.asarray(inputs["L0_attn_Wk"], np.float32)
    Wv = np.asarray(inputs["L0_attn_Wv"], np.float32)
    Wo = np.asarray(inputs["L0_attn_Wo"], np.float32)
    w1 = np.asarray(inputs["L0_ffn_w1"], np.float32)
    cw = np.asarray(inputs["L0_ffn_conv_w"], np.float32)
    cb = np.asarray(inputs["L0_ffn_conv_b"], np.float32)
    w2 = np.asarray(inputs["L0_ffn_w2"], np.float32)

    wcat = np.concatenate([Wq, Wk, Wv], axis=0)
    wcatT = np.ascontiguousarray(
        wcat.T.reshape(8, 128, 1536).transpose(1, 0, 2).reshape(128, 8 * 1536))
    woT = np.ascontiguousarray(
        Wo.T.reshape(8, 128, D).transpose(1, 0, 2).reshape(128, 8 * D))
    w1T = np.ascontiguousarray(w1.T.reshape(8, 128, DH))
    w2T = np.ascontiguousarray(w2.T.reshape(32, 128, D))
    cwt = np.ascontiguousarray(
        cw[:, 0, :].reshape(FT, 128, 3).transpose(1, 0, 2).reshape(128, FT * 3))
    cbias = np.ascontiguousarray(cb.reshape(FT, 128).T)

    consts = host_constants()

    if _NC_CACHE is None:
        _NC_CACHE = build_nc()
    nc = _NC_CACHE

    common = dict(wcatT=wcatT, woT=woT, w1T=w1T, w2T=w2T, cwt=cwt, cbias=cbias,
                  **consts)
    xr = x.reshape(B, S, D)
    in_maps = []
    for i in range(NCORE):
        m = dict(common)
        m["x"] = np.ascontiguousarray(xr[2 * i:2 * i + 2].reshape(TOK, D))
        in_maps.append(m)

    res = run_bass_kernel_spmd(nc, in_maps, core_ids=list(range(NCORE)))
    kernel._LAST = res.results
    out = np.concatenate([r["out"].reshape(2, S, D) for r in res.results], axis=0)
    return out


# version 19 (final)
# speedup vs baseline: 1.0391x; 1.0391x over previous
import numpy as np
import ml_dtypes
from contextlib import ExitStack

import concourse.bass as bass
import concourse.mybir as mybir
from concourse.bass_utils import run_bass_kernel_spmd

F32 = mybir.dt.float32
BF16 = mybir.dt.bfloat16
AF = mybir.ActivationFunctionType
ALU = mybir.AluOpType
AX = mybir.AxisListType

B, S, D = 16, 2048, 1024
NH, NKV, HD = 16, 4, 64
CHUNK = 128
NCHUNK = S // CHUNK
THETA = 10000.0
DH = 4096
FT = DH // 128
NCORE = 8
TOK = (B // NCORE) * S
EPS = 1e-5
IT = 512
NIT = S // IT

MAGIC23 = float(1.5 * 2.0 ** 23)
MAGIC_HI = float(np.float32(1.5 * 2.0 ** 52 - 1.5 * 2.0 ** 23))

PHASE_B = True
import os
KSTAGE = int(os.environ.get("KSTAGE", "99"))
KSUB = int(os.environ.get("KSUB", "99"))
KPOST = int(os.environ.get("KPOST", "99"))
KEVAC = os.environ.get("KEVAC", "")


def host_constants():
    import jax
    import jax.numpy as jnp

    with jax.default_device(jax.devices("cpu")[0]):
        slopes = jnp.asarray([2.0 ** (-8.0 / NH * (i + 1)) for i in range(NH)],
                             dtype=jnp.float32)
        tj = jnp.arange(CHUNK, dtype=jnp.float32)
        sl = slopes[:, None, None]
        mask = jnp.where(tj[:, None] >= tj[None, :],
                         jnp.exp(-sl * (tj[:, None] - tj[None, :])), 0.0)
        qdec = jnp.exp(-slopes[:, None] * (tj + 1.0))
        kdec_j = jnp.exp(-slopes[:, None] * (CHUNK - 1.0 - tj))
        bd_j = jnp.exp(-slopes * CHUNK)
        mask = np.asarray(mask)
        qdec = np.asarray(qdec)
        kdec_j = np.asarray(kdec_j)
        bd_j = np.asarray(bd_j)

    t = np.arange(S, dtype=np.float64)
    inv = 1.0 / THETA ** (np.arange(0, HD, 2, dtype=np.float64) / HD)
    freqs = np.outer(t, inv)
    emb = np.concatenate([freqs, freqs], axis=-1)
    cos32 = np.cos(emb).astype(np.float32)
    sin32 = np.sin(emb).astype(np.float32)
    cos_r = np.ascontiguousarray(cos32.reshape(NCHUNK, 128, HD).transpose(1, 0, 2))
    sin_pm = sin32.copy()
    sin_pm[:, :HD // 2] = -sin_pm[:, :HD // 2]
    sin_r = np.ascontiguousarray(sin_pm.reshape(NCHUNK, 128, HD).transpose(1, 0, 2))

    maskT = np.ascontiguousarray(mask.transpose(2, 0, 1))

    qdec_tile = np.empty((128, NH // 2, CHUNK), np.float32)
    for j in range(NH // 2):
        qdec_tile[:64, j, :] = qdec[2 * j]
        qdec_tile[64:, j, :] = qdec[2 * j + 1]

    kdec = np.ascontiguousarray(kdec_j.T)

    bdvec = np.empty((128, NH // 2), np.float32)
    for j in range(NH // 2):
        bdvec[:64, j] = bd_j[2 * j]
        bdvec[64:, j] = bd_j[2 * j + 1]

    return dict(
        cosr=cos_r.reshape(128, NCHUNK * HD),
        sinr=sin_r.reshape(128, NCHUNK * HD),
        maskT=maskT.reshape(128, NH * CHUNK),
        qdec=qdec_tile.reshape(128, (NH // 2) * CHUNK),
        kdec=kdec, bdvec=bdvec,
        identb=np.eye(128, dtype=ml_dtypes.bfloat16),
        identf=np.eye(128, dtype=np.float32),
        onesb=np.ones((4, 128), dtype=ml_dtypes.bfloat16),
        onesfr=np.ones((4, 128), dtype=np.float32),
        onesfc=np.ones((128, 1), dtype=np.float32),
    )


def build_nc():
    nc = bass.Bass("TRN2", target_bir_lowering=False, debug=False,
                   detect_race_conditions=False)
    di = lambda n, shp, dt=F32: nc.dram_tensor(n, shp, dt, kind="ExternalInput").ap()

    x_h = di("x", [TOK, D])
    wcat_h = di("wcatT", [128, 8 * 1536])
    woT_h = di("woT", [128, 8 * D])
    w1T_h = di("w1T", [8, 128, DH])
    w2T_h = di("w2T", [32, 128, D])
    cwt_h = di("cwt", [128, FT * 3])
    cbias_h = di("cbias", [128, FT])
    cos_h = di("cosr", [128, NCHUNK * HD])
    sin_h = di("sinr", [128, NCHUNK * HD])
    maskT_h = di("maskT", [128, NH * CHUNK])
    qdec_h = di("qdec", [128, (NH // 2) * CHUNK])
    kdec_h = di("kdec", [128, NH])
    bdvec_h = di("bdvec", [128, NH // 2])
    identb_h = di("identb", [128, 128], BF16)
    identf_h = di("identf", [128, 128])
    onesb_h = di("onesb", [4, 128], BF16)
    onesfr_h = di("onesfr", [4, 128])
    onesfc_h = di("onesfc", [128, 1])

    out_h = nc.dram_tensor("out", [TOK, D], F32, kind="ExternalOutput").ap()
    scr_att = nc.dram_tensor("scr_att", [TOK, D], F32, kind="Internal").ap()
    scr_w1 = nc.dram_tensor("scr_w1", [8, 8, 128, 512], BF16, kind="Internal").ap()
    scr_w2 = nc.dram_tensor("scr_w2", [32, 128, D], BF16, kind="Internal").ap()

    esg = ExitStack()
    SB = lambda es, n, shp, dt=F32: es.enter_context(nc.sbuf_tensor(n, shp, dt))
    PSUM = lambda es, n, shp: es.enter_context(nc.psum_tensor(n, shp, F32))

    class Sem:
        __slots__ = ("h", "v")
        def __init__(self, name):
            self.h = esg.enter_context(nc.semaphore(name))
            self.v = 0
        def inc(self, instr, n=1):
            instr.then_inc(self.h, n)
            self.v += n
        def wait(self, e, v=None):
            e.wait_ge(self.h, self.v if v is None else v)

    mvec = SB(esg, "mvec", [128, 1])
    negM = SB(esg, "negM", [128, 1])
    actwarm = SB(esg, "actwarm", [128, 1])
    cwt_sb = SB(esg, "cwt_sb", [128, FT * 3])
    cbias_sb = SB(esg, "cbias_sb", [128, FT])
    identb_sb = SB(esg, "identb_sb", [128, 128], BF16)
    identf_sb = SB(esg, "identf_sb", [128, 128])
    onesb_sb = SB(esg, "onesb_sb", [4, 128], BF16)
    onesfr_sb = SB(esg, "onesfr_sb", [4, 128])
    onesfc_sb = SB(esg, "onesfc_sb", [128, 1])
    rall = SB(esg, "rall", [128, 2 * NCHUNK])
    mall = SB(esg, "mall", [128, 2 * NCHUNK])
    shist = SB(esg, "shist", [128, 2 * NCHUNK])
    vbat = SB(esg, "vbat", [128, 2 * NCHUNK])
    ball = SB(esg, "ball", [128, 2 * NCHUNK])
    mw1 = SB(esg, "mw1", [128, 1])
    mw2 = SB(esg, "mw2", [128, 1])
    sw12 = SB(esg, "sw12", [128, 2])
    wcat_sb = SB(esg, "wcat_sb", [128, 8 * 1536], BF16)
    woT_sb = SB(esg, "woT_sb", [128, 8 * D], BF16)

    s_c = Sem("s_c"); s_x = Sem("s_x"); s_w = Sem("s_w"); s_gd = Sem("s_gd")
    s_v = Sem("s_v"); s_a = Sem("s_a"); s_t = Sem("s_t"); s_o = Sem("s_o")
    s_ts = Sem("s_ts"); s_tk = Sem("s_tk"); s_m = Sem("s_m"); s_k = Sem("s_k")
    s_e = Sem("s_e"); s_q = Sem("s_q"); s_we = Sem("s_we"); s_wo = Sem("s_wo")
    s_xb = Sem("s_xb"); s_scb = Sem("s_scb"); s_w1 = Sem("s_w1"); s_w2 = Sem("s_w2")
    s_t1 = Sem("s_t1"); s_t2 = Sem("s_t2"); s_xc = Sem("s_xc"); s_sce = Sem("s_sce")
    s_k2 = Sem("s_k2")

    with nc.Block() as blk:
        def f_sync(e):
            for dst, src in ((cwt_sb, cwt_h), (cbias_sb, cbias_h), (identb_sb, identb_h),
                             (identf_sb, identf_h), (onesb_sb, onesb_h),
                             (onesfr_sb, onesfr_h), (onesfc_sb, onesfc_h)):
                s_c.inc(e.dma_start(out=dst[:], in_=src), 16)
        blk.sync(f_sync)

        def f_a_warm(e):
            s_c.wait(e)
            e.activation(out=actwarm[:], in_=onesfc_sb[:], func=AF.Ln)
            e.drain()
            e.activation(out=actwarm[:], in_=actwarm[:], func=AF.Exp)
            e.drain()
        blk.scalar(f_a_warm)

        def f_v(e):
            s_c.wait(e)
            e.memset(mvec[:], 0.25)
            e.drain()
            e.tensor_scalar(out=mvec[:], in0=mvec[:], scalar1=MAGIC23, scalar2=MAGIC23,
                            op0=ALU.add, op1=ALU.subtract)
            e.drain()
            e.tensor_scalar(out=mvec[:], in0=mvec[:], scalar1=0.25, scalar2=None, op0=ALU.is_equal)
            e.drain()
            e.tensor_scalar(out=mvec[:], in0=mvec[:], scalar1=MAGIC_HI, scalar2=MAGIC23,
                            op0=ALU.mult, op1=ALU.add)
            e.drain()
            s_v.inc(e.tensor_scalar_mul(out=negM[:], in0=mvec[:], scalar1=-1.0))
        blk.vector(f_v)

    es_p = ExitStack()
    wtile = [SB(es_p, f"wtile{i}", [128, DH]) for i in range(2)]
    wqb = [SB(es_p, f"wqb{i}", [128, DH], BF16) for i in range(2)]
    wsum1 = SB(es_p, "wsum1", [128, 1])
    wsum2 = SB(es_p, "wsum2", [128, 1])
    rtmp = SB(es_p, "rtmp", [128, 1])
    sc2 = SB(es_p, "sc2", [1, 2])
    wttmp = SB(es_p, "wttmp", [128, DH])
    ps_pp = PSUM(es_p, "ps_pp", [128, 16])

    NWT = 40
    def w_src(i):
        return (w1T_h[i], DH) if i < 8 else (w2T_h[i - 8], D)

    with nc.Block() as blk:
        v0 = s_v.v
        def f_sync(e):
            for i in range(NWT):
                if i >= 2:
                    e.wait_ge(s_v.h, v0 + i - 1)
                src, flen = w_src(i)
                s_w.inc(e.dma_start(out=wtile[i % 2][:, 0:flen], in_=src), 16)
        blk.sync(f_sync)

        def f_v(e):
            e.memset(wsum1[:], 0.0)
            e.memset(wsum2[:], 0.0)
            for i in range(NWT):
                _, flen = w_src(i)
                e.wait_ge(s_w.h, 16 * (i + 1))
                e.reduce_sum(out=rtmp[:], in_=wtile[i % 2][:, 0:flen], axis=AX.X,
                             apply_absolute_value=True)
                e.drain()
                acc = wsum1 if i < 8 else wsum2
                s_v.inc(e.tensor_add(out=acc[:], in0=acc[:], in1=rtmp[:]))
            s_v.inc(e.drain())
        blk.vector(f_v)

        def f_t(e):
            s_v.wait(e)
            e.matmul(out=ps_pp[0:1, 0:1], lhsT=wsum1[:], rhs=onesfc_sb[:], start=True, stop=True)
            s_t.inc(e.matmul(out=ps_pp[0:1, 1:2], lhsT=wsum2[:], rhs=onesfc_sb[:],
                             start=True, stop=True))
        blk.tensor(f_t)

        def f_v2(e):
            s_t.wait(e)
            e.tensor_scalar(out=sc2[0:1, 0:1], in0=ps_pp[0:1, 0:1],
                            scalar1=1.0 / (DH * D), scalar2=EPS, op0=ALU.mult, op1=ALU.max)
            e.tensor_scalar(out=sc2[0:1, 1:2], in0=ps_pp[0:1, 1:2],
                            scalar1=1.0 / (DH * D), scalar2=EPS, op0=ALU.mult, op1=ALU.max)
            s_v.inc(e.drain())
        blk.vector(f_v2)

        def f_t2(e):
            s_v.wait(e)
            s_t.inc(e.matmul(out=ps_pp[:, 2:4], lhsT=onesfr_sb[0:1, :], rhs=sc2[0:1, :],
                             start=True, stop=True))
        blk.tensor(f_t2)

        def f_v3(e):
            s_t.wait(e)
            e.tensor_copy(out=mw1[:], in_=ps_pp[:, 2:3])
            e.tensor_copy(out=mw2[:], in_=ps_pp[:, 3:4])
            e.drain()
            e.reciprocal(out=sw12[:, 0:1], in_=mw1[:])
            e.reciprocal(out=sw12[:, 1:2], in_=mw2[:])
            s_v.inc(e.drain())
        blk.vector(f_v3)

    with nc.Block() as blk:
        v0 = s_v.v
        g0 = s_gd.v
        def f_sync(e):
            s_v.wait(e)
            for i in range(NWT):
                if i >= 2:
                    e.wait_ge(s_v.h, v0 + i - 1)
                src, flen = w_src(i)
                s_w.inc(e.dma_start(out=wtile[i % 2][:, 0:flen], in_=src), 16)
        blk.sync(f_sync)

        w0 = s_w.v - 16 * NWT
        def f_v(e):
            for i in range(NWT):
                _, flen = w_src(i)
                col = 0 if i < 8 else 1
                e.wait_ge(s_w.h, w0 + 16 * (i + 1))
                if i >= 2:
                    e.wait_ge(s_gd.h, g0 + 16 * (i - 1))
                sl = (slice(0, 128), slice(0, flen))
                e.tensor_scalar(out=wttmp[sl], in0=wtile[i % 2][sl], scalar1=sw12[:, col:col + 1],
                                scalar2=mvec[:], op0=ALU.mult, op1=ALU.add)
                e.tensor_scalar(out=wttmp[sl], in0=wttmp[sl], scalar1=mvec[:],
                                scalar2=1.0, op0=ALU.subtract, op1=ALU.min)
                s_v.inc(e.tensor_scalar(out=wqb[i % 2][sl], in0=wttmp[sl],
                                        scalar1=-1.0, scalar2=None, op0=ALU.max))
        blk.vector(f_v)

        def f_g(e):
            for i in range(NWT):
                e.wait_ge(s_v.h, v0 + i + 1)
                if i < 8:
                    dst_ap = scr_w1[:, i].rearrange("g p f -> p g f")
                    s_gd.inc(e.dma_start(out=dst_ap, in_=wqb[i % 2][:]), 16)
                else:
                    s_gd.inc(e.dma_start(out=scr_w2[i - 8], in_=wqb[i % 2][:, 0:D]), 16)
        blk.gpsimd(f_g)

    p0tmp = SB(es_p, "p0tmp", [128, D])
    with nc.Block() as blk:
        w0b = s_w.v
        v0b = s_v.v
        a0b = s_a.v
        def f_sync(e, w0b=w0b, v0b=v0b, a0b=a0b):
            for c in range(2 * NCHUNK):
                if c >= 2:
                    e.wait_ge(s_v.h, v0b + c - 1)
                    e.wait_ge(s_a.h, a0b + c - 1)
                s_w.inc(e.dma_start(out=wtile[c % 2][:, 0:D], in_=x_h[128 * c:128 * (c + 1)]), 16)
        blk.sync(f_sync)

        def f_v(e, w0b=w0b):
            for c in range(2 * NCHUNK):
                e.wait_ge(s_w.h, w0b + 16 * (c + 1))
                s_v.inc(e.reduce_sum(out=mall[:, c:c + 1], in_=wtile[c % 2][:, 0:D], axis=AX.X))
        blk.vector(f_v)

        def f_a(e, w0b=w0b, v0b=v0b):
            for c in range(2 * NCHUNK):
                e.wait_ge(s_v.h, v0b + c + 1)
                s_a.inc(e.activation(out=p0tmp[:], in_=wtile[c % 2][:, 0:D], func=AF.Square,
                                     accum_out=shist[:, c:c + 1]))
            s_a.inc(e.drain())
        blk.scalar(f_a)

        def f_v2(e):
            s_a.wait(e)
            e.drain()
            e.tensor_scalar_mul(out=mall[:], in0=mall[:], scalar1=1.0 / D)
            e.drain()
            e.tensor_mul(out=vbat[:], in0=mall[:], in1=mall[:])
            e.tensor_scalar(out=shist[:], in0=shist[:], scalar1=1.0 / D, scalar2=EPS,
                            op0=ALU.mult, op1=ALU.add)
            e.drain()
            e.tensor_sub(out=vbat[:], in0=shist[:], in1=vbat[:])
            s_v.inc(e.drain())
        blk.vector(f_v2)

        def f_a2(e):
            s_v.wait(e)
            e.activation(out=vbat[:], in_=vbat[:], func=AF.Ln)
            e.drain()
            e.activation(out=rall[:], in_=vbat[:], func=AF.Exp, scale=-0.5)
            s_a.inc(e.drain())
        blk.scalar(f_a2)

        def f_v3(e):
            s_a.wait(e)
            e.tensor_mul(out=ball[:], in0=mall[:], in1=rall[:])
            e.drain()
            e.tensor_scalar_mul(out=ball[:], in0=ball[:], scalar1=-1.0)
            s_v.inc(e.drain())
        blk.vector(f_v3)

    with nc.Block() as blk:
        def f_g(e):
            s_gd.inc(e.dma_start(out=wcat_sb[:], in_=wcat_h), 16)
            s_gd.inc(e.dma_start(out=woT_sb[:], in_=woT_h), 16)
        blk.gpsimd(f_g)

    es_p.close()
    es_a = ExitStack()
    xs = [SB(es_a, f"xs{i}", [128, D]) for i in range(2)]
    att = [SB(es_a, f"att{i}", [128, D]) for i in range(2)]
    xsum = SB(es_a, "xsum", [128, 1])
    sqsum = SB(es_a, "sqsum", [128, 1])
    mcol = SB(es_a, "mcol", [128, 1])
    msq = SB(es_a, "msq", [128, 1])
    vtmp = SB(es_a, "vtmp", [128, 1])
    ltmp = SB(es_a, "ltmp", [128, 1])
    btmp = SB(es_a, "btmp", [128, 1])
    xn_bf = SB(es_a, "xn_bf", [128, D], BF16)
    xnT = SB(es_a, "xnT", [128, 8, 128], BF16)
    tmpa = SB(es_a, "tmpa", [128, D])
    tmpb = SB(es_a, "tmpb", [128, D])
    q_bf = SB(es_a, "q_bf", [128, D], BF16)
    k_bf = SB(es_a, "k_bf", [128, 256], BF16)
    v_bf = SB(es_a, "v_bf", [128, 256], BF16)
    qT = SB(es_a, "qT", [128, 8, 128], BF16)
    qdq = SB(es_a, "qdq", [128, 8, 128], BF16)
    kTba = SB(es_a, "kTba", [128, 4, 128], BF16)
    kdk = SB(es_a, "kdk", [128, NH * 64], BF16)
    sc_bf = SB(es_a, "sc_bf", [128, 4 * 128], BF16)
    oT = SB(es_a, "oT", [128, 8, 128], BF16)
    kv_sb = [SB(es_a, f"kv_sb{i}", [128, 8, 64]) for i in range(2)]
    kv_bf = SB(es_a, "kv_bf", [128, 8, 64], BF16)
    cos_sb = SB(es_a, "cos_sb", [128, NCHUNK * HD])
    sin_sb = SB(es_a, "sin_sb", [128, NCHUNK * HD])
    maskT_sb = SB(es_a, "maskT_sb", [128, NH * CHUNK])
    qdec_sb = SB(es_a, "qdec_sb", [128, (NH // 2) * CHUNK])
    kdec_sb = SB(es_a, "kdec_sb", [128, NH])
    bdvec_sb = SB(es_a, "bdvec_sb", [128, NH // 2])

    dbg_tensors = dict(xn_bf=xn_bf, q_bf=q_bf, k_bf=k_bf, v_bf=v_bf, qT=qT,
                       kTba=kTba, qdq=qdq, kdk=kdk, sc_bf=sc_bf, oT=oT,
                       kv_sb0=kv_sb[0], kv_bf=kv_bf, rall=rall, ball=ball,
                       mvec=mvec, negM=negM, wcat_sb=wcat_sb, woT_sb=woT_sb,
                       mw1=mw1, mw2=mw2, sw12=sw12, xnT=xnT)
    ps_qkv = PSUM(es_a, "ps_qkv", [128, 1536])
    ps_sctr = PSUM(es_a, "ps_sctr", [128, 1024])
    ps_o = PSUM(es_a, "ps_o", [128, 1024])
    ps_kao = PSUM(es_a, "ps_kao", [128, 512])

    def tr_slot(j):
        return ps_sctr[:, 512 * (j % 2):512 * (j % 2) + 128]

    def sc_slot(h):
        base = 512 * (h % 2) + 128 * ((h // 2) % 2)
        return ps_sctr[:, base:base + 128]

    def o_slot(j):
        base = 512 * (j % 2) + 128 * ((j // 2) % 2)
        return ps_o[:, base:base + 128]

    with nc.Block() as blk:
        def f_sync(e):
            e.wait_ge(s_gd.h, s_gd.v)
            for dst, src in ((cos_sb, cos_h), (sin_sb, sin_h), (maskT_sb, maskT_h),
                             (qdec_sb, qdec_h), (kdec_sb, kdec_h), (bdvec_sb, bdvec_h)):
                s_c.inc(e.dma_start(out=dst[:], in_=src), 16)
            s_x.inc(e.dma_start(out=xs[0][:], in_=x_h[0:128]), 16)
        blk.sync(f_sync)

    ln_ready = s_v.v
    rope_done = {}
    attst_target = scr_att if (PHASE_B and KSTAGE >= 4) else out_h

    n_chunks = 0 if KSTAGE < 2 else (1 if KSTAGE == 2 else 2 * NCHUNK)
    for c in range(n_chunks):
        seq, cc = divmod(c, NCHUNK)
        col = slice(c, c + 1)
        with nc.Block() as blk:
            if c + 1 < 2 * NCHUNK:
                def f_sync(e, c=c):
                    if c + 1 >= 2:
                        e.wait_ge(s_we.h, 2 * c)
                    s_x.inc(e.dma_start(out=xs[(c + 1) % 2][:],
                                        in_=x_h[128 * (c + 1):128 * (c + 2)]), 16)
                blk.sync(f_sync)

            def f_a(e, c=c, col=col, lnv=ln_ready):
                e.wait_ge(s_x.h, 16 * (c + 1))
                if c == 0:
                    e.wait_ge(s_v.h, lnv)
                s_a.inc(e.activation(out=xn_bf[:], in_=xs[c % 2][:], func=AF.Identity,
                                     scale=rall[:, col], bias=ball[:, col]))
            blk.scalar(f_a)

            if KSUB < 2:
                continue
            tb = s_t.v; ab = s_a.v
            def f_t(e, tb=tb, ab=ab, c=c):
                s_a.wait(e)
                if c == 0:
                    s_c.wait(e)
                else:
                    e.wait_ge(s_m.h, 16 * c)
                for j in range(8):
                    if j >= 2:
                        e.wait_ge(s_a.h, ab + j - 1)
                    s_t.inc(e.matmul(out=tr_slot(j),
                                     lhsT=xn_bf[:, 128 * j:128 * (j + 1)], rhs=identb_sb[:],
                                     start=True, stop=True))
            blk.tensor(f_t)

            def f_a(e, tb=tb):
                for j in range(8):
                    e.wait_ge(s_t.h, tb + j + 1)
                    s_a.inc(e.activation(out=xnT[:, j, :], in_=tr_slot(j), func=AF.Identity))
            blk.scalar(f_a)

            if KSUB < 3:
                continue
            ab = s_a.v
            def f_t(e, c=c, ab=ab):
                e.wait_ge(s_a.h, ab)
                e.wait_ge(s_gd.h, 16 * 42)
                if c > 0:
                    e.wait_ge(s_v.h, rope_done[c - 1])
                for dc in range(8):
                    for b3 in range(3):
                        ins = e.matmul(out=ps_qkv[:, 512 * b3:512 * (b3 + 1)],
                                       lhsT=xnT[:, dc, :],
                                       rhs=wcat_sb[:, 1536 * dc + 512 * b3:1536 * dc + 512 * (b3 + 1)],
                                       start=(dc == 0), stop=(dc == 7))
                s_t.inc(ins)
            blk.tensor(f_t)

            def f_v(e, c=c, cc=cc):
                s_t.wait(e)
                if c == 0:
                    s_c.wait(e)
                csl = cos_sb[:, HD * cc:HD * (cc + 1)].rearrange("p (o d) -> p o d", o=1)
                ssl = sin_sb[:, HD * cc:HD * (cc + 1)].rearrange("p (o d) -> p o d", o=1)
                qv = ps_qkv[:, 0:D].rearrange("p (h d) -> p h d", h=NH)
                ta = tmpa[:].rearrange("p (h d) -> p h d", h=NH)
                tb_ = tmpb[:].rearrange("p (h d) -> p h d", h=NH)
                e.tensor_mul(out=ta, in0=qv, in1=csl.broadcast_to((128, NH, HD)))
                e.tensor_mul(out=tb_[:, :, 0:32], in0=qv[:, :, 32:64],
                             in1=ssl[:, :, 0:32].broadcast_to((128, NH, 32)))
                e.tensor_mul(out=tb_[:, :, 32:64], in0=qv[:, :, 0:32],
                             in1=ssl[:, :, 32:64].broadcast_to((128, NH, 32)))
                e.tensor_add(out=q_bf[:], in0=tmpa[:], in1=tmpb[:])
                kv_ = ps_qkv[:, D:D + 256].rearrange("p (h d) -> p h d", h=NKV)
                ta4 = tmpa[:, 0:256].rearrange("p (h d) -> p h d", h=NKV)
                tb4 = tmpb[:, 0:256].rearrange("p (h d) -> p h d", h=NKV)
                e.tensor_mul(out=ta4, in0=kv_, in1=csl.broadcast_to((128, NKV, HD)))
                e.tensor_mul(out=tb4[:, :, 0:32], in0=kv_[:, :, 32:64],
                             in1=ssl[:, :, 0:32].broadcast_to((128, NKV, 32)))
                e.tensor_mul(out=tb4[:, :, 32:64], in0=kv_[:, :, 0:32],
                             in1=ssl[:, :, 32:64].broadcast_to((128, NKV, 32)))
                e.tensor_add(out=k_bf[:], in0=tmpa[:, 0:256], in1=tmpb[:, 0:256])
                e.tensor_copy(out=v_bf[:], in_=ps_qkv[:, D + 256:D + 512])
                for hh in range(NH):
                    g = hh // 4
                    ins = e.tensor_scalar_mul(out=kdk[:, 64 * hh:64 * (hh + 1)],
                                              in0=k_bf[:, 64 * g:64 * (g + 1)],
                                              scalar1=kdec_sb[:, hh:hh + 1])
                s_v.inc(ins)
                rope_done[c] = s_v.v
            blk.vector(f_v)

            if KSUB < 4:
                continue
            tb = s_t.v; ab = s_a.v
            def f_t(e, tb=tb, ab=ab, c=c):
                e.wait_ge(s_v.h, rope_done[c])
                for j in range(8):
                    if j >= 2:
                        e.wait_ge(s_a.h, ab + j - 1)
                    s_t.inc(e.matmul(out=tr_slot(j),
                                     lhsT=q_bf[:, 128 * j:128 * (j + 1)], rhs=identb_sb[:],
                                     start=True, stop=True))
                for g in range(4):
                    jj = 8 + g
                    e.wait_ge(s_a.h, ab + jj - 1)
                    sl = tr_slot(jj)
                    e.matmul(out=sl[0:64, :], lhsT=k_bf[:, 64 * g:64 * (g + 1)],
                             rhs=identb_sb[:], start=True, stop=True)
                    s_t.inc(e.matmul(out=sl[64:128, :], lhsT=k_bf[:, 64 * g:64 * (g + 1)],
                                     rhs=identb_sb[:], start=True, stop=True))
            blk.tensor(f_t)

            def f_a(e, tb=tb):
                for j in range(8):
                    e.wait_ge(s_t.h, tb + j + 1)
                    s_a.inc(e.activation(out=qT[:, j, :], in_=tr_slot(j), func=AF.Identity))
                for g in range(4):
                    jj = 8 + g
                    e.wait_ge(s_t.h, tb + jj + 1)
                    s_a.inc(e.activation(out=kTba[:, g, :], in_=tr_slot(jj), func=AF.Identity))
            blk.scalar(f_a)

            ab = s_a.v
            def f_v(e, ab=ab, c=c):
                e.wait_ge(s_a.h, ab - 4)
                if c == 0:
                    s_c.wait(e)
                for j in range(8):
                    ins = e.tensor_mul(out=qdq[:, j, :], in0=qT[:, j, :],
                                       in1=qdec_sb[:, 128 * j:128 * (j + 1)])
                s_q.inc(ins)
            blk.vector(f_v)

            if KSUB < 5:
                continue
            TSb = s_ts.v; TKb = s_tk.v; Mb = s_m.v; Kb = s_k.v; Eb = s_e.v
            Ab = s_a.v
            def f_t(e, c=c, cc=cc, TSb=TSb, TKb=TKb, Mb=Mb, Kb=Kb, Eb=Eb, Ab=Ab):
                e.wait_ge(s_a.h, Ab)
                e.wait_ge(s_q.h, s_q.v)
                if c > 0:
                    e.wait_ge(s_we.h, 2 * c)
                for hh in range(NH):
                    g = hh // 4
                    p64 = 64 * (hh % 2)
                    j = hh // 2
                    slot = hh % 4
                    if hh >= 2:
                        e.wait_ge(s_m.h, Mb + hh - 1)
                    s_ts.inc(e.matmul(out=sc_slot(hh),
                                      lhsT=kTba[p64:p64 + 64, g, :],
                                      rhs=qT[p64:p64 + 64, hh // 2, :],
                                      start=True, stop=True))
                    e.wait_ge(s_m.h, Mb + hh + 1)
                    if j >= 2 and hh % 2 == 0:
                        e.wait_ge(s_e.h, Eb + j - 1)
                    first = (cc == 0)
                    e.matmul(out=o_slot(j)[p64:p64 + 64, :],
                             lhsT=v_bf[:, 64 * g:64 * (g + 1)],
                             rhs=sc_bf[:, 128 * slot:128 * (slot + 1)],
                             start=True, stop=first)
                    if not first:
                        e.wait_ge(s_k2.h, c)
                        e.matmul(out=o_slot(j)[p64:p64 + 64, :],
                                 lhsT=kv_bf[p64:p64 + 64, j, :],
                                 rhs=qdq[p64:p64 + 64, j, :],
                                 start=False, stop=True)
                    kslot = j % 4
                    if j >= 1 and hh % 2 == 0:
                        e.wait_ge(s_k.h, Kb + j)
                    ins = e.matmul(out=ps_kao[p64:p64 + 64, 64 * kslot:64 * (kslot + 1)],
                                   lhsT=kdk[:, 64 * hh:64 * (hh + 1)],
                                   rhs=v_bf[:, 64 * g:64 * (g + 1)],
                                   start=True, stop=True)
                    if hh % 2 == 1:
                        s_tk.inc(ins)
            blk.tensor(f_t)

            def f_v(e, cc=cc, TSb=TSb, TKb=TKb, c=c):
                if c == 0:
                    s_c.wait(e)
                for hh in range(NH):
                    slot = hh % 4
                    e.wait_ge(s_ts.h, TSb + hh + 1)
                    s_m.inc(e.tensor_mul(out=sc_bf[:, 128 * slot:128 * (slot + 1)],
                                         in0=sc_slot(hh),
                                         in1=maskT_sb[:, 128 * hh:128 * (hh + 1)]))
                    if hh % 2 == 1:
                        j = hh // 2
                        kslot = j % 4
                        e.wait_ge(s_tk.h, TKb + j + 1)
                        if cc == 0:
                            e.tensor_copy(out=kv_sb[c % 2][:, j, :],
                                          in_=ps_kao[:, 64 * kslot:64 * (kslot + 1)])
                            s_k.inc(e.tensor_copy(out=kv_bf[:, j, :],
                                          in_=ps_kao[:, 64 * kslot:64 * (kslot + 1)]))
                        else:
                            e.scalar_tensor_tensor(
                                out=kv_sb[c % 2][:, j, :], in0=kv_sb[(c - 1) % 2][:, j, :],
                                scalar=bdvec_sb[:, j:j + 1],
                                in1=ps_kao[:, 64 * kslot:64 * (kslot + 1)],
                                op0=ALU.mult, op1=ALU.add)
                            s_k.inc(e.scalar_tensor_tensor(
                                out=kv_bf[:, j, :], in0=kv_sb[(c - 1) % 2][:, j, :],
                                scalar=bdvec_sb[:, j:j + 1],
                                in1=ps_kao[:, 64 * kslot:64 * (kslot + 1)],
                                op0=ALU.mult, op1=ALU.add))
                s_k2.inc(e.drain())
            blk.vector(f_v)

            def f_a(e, TKb=TKb):
                for j in range(8):
                    e.wait_ge(s_tk.h, TKb + j + 1)
                    s_e.inc(e.activation(out=oT[:, j, :], in_=o_slot(j), func=AF.Identity))
            blk.scalar(f_a)

            if KSUB < 6:
                continue
            Eb2 = s_e.v
            def f_t(e, c=c, Eb2=Eb2):
                e.wait_ge(s_e.h, Eb2)
                s_k.wait(e)
                for half in range(2):
                    if 2 * c + half >= 1:
                        e.wait_ge(s_we.h, 2 * c + half)
                    for j in range(8):
                        ins = e.matmul(out=ps_kao[:],
                                       lhsT=oT[:, j, :],
                                       rhs=woT_sb[:, D * j + 512 * half:D * j + 512 * (half + 1)],
                                       start=(j == 0), stop=(j == 7))
                    s_wo.inc(ins)
            blk.tensor(f_t)

            def f_v(e, c=c):
                for half in range(2):
                    e.wait_ge(s_wo.h, 2 * c + half + 1)
                    if c >= 2:
                        e.wait_ge(s_o.h, 16 * (c - 1))
                    s_we.inc(e.tensor_add(out=att[c % 2][:, 512 * half:512 * (half + 1)],
                                          in0=ps_kao[:],
                                          in1=xs[c % 2][:, 512 * half:512 * (half + 1)]))
            blk.vector(f_v)

            def f_sync(e, c=c):
                e.wait_ge(s_we.h, 2 * c + 2)
                s_o.inc(e.dma_start(out=attst_target[128 * c:128 * (c + 1)],
                                    in_=att[c % 2][:]), 16)
            blk.sync(f_sync)

    es_a.close()

    if PHASE_B and KSTAGE >= 4:
        es_b = ExitStack()
        hb = [SB(es_b, f"hb{i}", [128, FT, IT], BF16) for i in range(2)]
        hL = [SB(es_b, f"hL{i}", [128, FT], BF16) for i in range(2)]
        x_pb = [SB(es_b, f"x_pb{i}", [128, D]) for i in range(2)]
        scr_pb = [SB(es_b, f"scr_pb{i}", [128, D]) for i in range(2)]
        outsb = [SB(es_b, f"outsb{i}", [128, D]) for i in range(2)]
        xn_t = SB(es_b, "xn_t", [128, D])
        t1_t = SB(es_b, "t1_t", [128, D])
        xq_sb = SB(es_b, "xq_sb", [128, 4, D], BF16)
        xqT_sb = SB(es_b, "xqT_sb", [128, 8, IT], BF16)
        w1g = [SB(es_b, f"w1g{i}", [128, 8, 512], BF16) for i in range(2)]
        w2g = [SB(es_b, f"w2g{i}", [128, 4, D], BF16) for i in range(2)]
        ctmp = [SB(es_b, f"ctmp{i}", [128, IT]) for i in range(4)]
        mtree = SB(es_b, "mtree", [128, IT])
        m2c = SB(es_b, "m2c", [128, 4])
        s2v = SB(es_b, "s2v", [128, 4])
        c2v = SB(es_b, "c2v", [128, 4])
        s4blk = SB(es_b, "s4blk", [1, IT])
        s2b = SB(es_b, "s2b", [128, IT])
        q1s = SB(es_b, "q1s", [128, 1])
        q1c = SB(es_b, "q1c", [128, 1])
        q1b = SB(es_b, "q1b", [128, 1])
        dbg_tensors.update(hb0=hb[0], hb1=hb[1], mtree=mtree, m2c=m2c, s2v=s2v,
                           c2v=c2v, s2b=s2b, xq_sb=xq_sb, xqT_sb=xqT_sb, hL0=hL[0], hL1=hL[1])
        ps_mm = PSUM(es_b, "ps_mm", [128, 2048])
        ps_tr2 = PSUM(es_b, "ps_tr2", [128, 1024])
        tr2_slot = lambda i: ps_tr2[:, 512 * (i % 2):512 * (i % 2) + 128]
        ps_bc2 = PSUM(es_b, "ps_bc2", [128, 512])

        def emit_h(it):
            seq, k = divmod(it, NIT)
            with nc.Block() as blk:
                def f_sync(e, it=it):
                    if it == 0:
                        e.wait_ge(s_o.h, s_o.v)
                    for tt in range(4):
                        gtt = 4 * it + tt
                        if gtt >= 2:
                            e.wait_ge(s_xc.h, gtt - 1)
                        row = 128 * gtt
                        s_xb.inc(e.dma_start(out=x_pb[gtt % 2][:],
                                             in_=x_h[row:row + 128]), 16)
                    for g in range(8):
                        gg = 8 * it + g
                        if gg >= 2:
                            e.wait_ge(s_t1.h, gg - 1)
                        s_w1.inc(e.dma_start(out=w1g[g % 2][:],
                                             in_=scr_w1[g].rearrange("dc p f -> p dc f")), 16)
                blk.sync(f_sync)

                for tt in range(4):
                    gtt = 4 * it + tt
                    col = slice(seq * NCHUNK + 4 * k + tt, seq * NCHUNK + 4 * k + tt + 1)
                    def f_a1(e, gtt=gtt, col=col):
                        e.wait_ge(s_xb.h, 16 * (gtt + 1))
                        s_a.inc(e.activation(out=xn_t[:], in_=x_pb[gtt % 2][:], func=AF.Identity,
                                             scale=rall[:, col], bias=ball[:, col]))
                        s_xc.inc(e.drain())
                    blk.scalar(f_a1)

                    def f_v1(e):
                        s_a.wait(e)
                        e.reduce_max(out=q1s[:], in_=xn_t[:], axis=AX.X, apply_absolute_value=True)
                        e.drain()
                        e.tensor_scalar(out=q1s[:], in0=q1s[:], scalar1=1e-5, scalar2=None, op0=ALU.max)
                        e.drain()
                        e.tensor_scalar_mul(out=q1c[:], in0=q1s[:], scalar1=1.0 / 127.0)
                        e.reciprocal(out=q1s[:], in_=q1s[:])
                        e.drain()
                        e.tensor_mul(out=q1c[:], in0=q1c[:], in1=mw1[:])
                        e.tensor_scalar_mul(out=q1s[:], in0=q1s[:], scalar1=127.0)
                        e.drain()
                        e.tensor_mul(out=q1b[:], in0=negM[:], in1=q1c[:])
                        s_v.inc(e.drain())
                    blk.vector(f_v1)

                    def f_a2(e, tt=tt):
                        s_v.wait(e)
                        e.activation(out=t1_t[:], in_=xn_t[:], func=AF.Identity,
                                     scale=q1s[:], bias=mvec[:])
                        e.activation(out=xn_t[:], in_=t1_t[:], func=AF.Identity,
                                     bias=negM[:])
                        s_a.inc(e.activation(out=xq_sb[:, tt, :], in_=xn_t[:], func=AF.Identity,
                                             scale=q1c[:]))
                    blk.scalar(f_a2)

                tb = s_t.v; ab = s_a.v
                def f_t(e, tb=tb, ab=ab):
                    for tt in range(4):
                        e.wait_ge(s_a.h, ab - 8 + 2 * (tt + 1))
                        for j in range(8):
                            idx = 8 * tt + j
                            if idx >= 2:
                                e.wait_ge(s_a.h, ab + idx - 1)
                            s_t.inc(e.matmul(out=tr2_slot(idx),
                                             lhsT=xq_sb[:, tt, 128 * j:128 * (j + 1)],
                                             rhs=identb_sb[:], start=True, stop=True))
                blk.tensor(f_t)

                def f_a(e, tb=tb):
                    for idx in range(32):
                        tt, j = divmod(idx, 8)
                        e.wait_ge(s_t.h, tb + idx + 1)
                        s_a.inc(e.activation(out=xqT_sb[:, j, 128 * tt:128 * (tt + 1)],
                                             in_=tr2_slot(idx), func=AF.Identity))
                blk.scalar(f_a)

                ab = s_a.v
                vb_mm1 = s_v.v
                def f_t(e, it=it, ab=ab, vb_mm1=vb_mm1):
                    e.wait_ge(s_a.h, ab)
                    for g in range(8):
                        e.wait_ge(s_w1.h, 16 * (8 * it + g + 1))
                        if g >= 1:
                            e.wait_ge(s_v.h, vb_mm1 + 4 * g)
                        for dc in range(8):
                            for fi in range(4):
                                ins = e.matmul(out=ps_mm[:, 512 * fi:512 * (fi + 1)],
                                               lhsT=w1g[g % 2][:, dc, 128 * fi:128 * (fi + 1)],
                                               rhs=xqT_sb[:, dc, :],
                                               start=(dc == 0), stop=(dc == 7))
                        s_t1.inc(ins)
                blk.tensor(f_t)

                t1b = s_t1.v - 8
                def f_v(e, it=it, t1b=t1b):
                    for g in range(8):
                        e.wait_ge(s_t1.h, t1b + g + 1)
                        for fi in range(4):
                            ct = 4 * g + fi
                            s_v.inc(e.tensor_copy(out=hb[it % 2][:, ct, :],
                                                  in_=ps_mm[:, 512 * fi:512 * (fi + 1)]))
                blk.vector(f_v)
            return s_v.v

        def emit_post(j, h_done_v, hnext_done_v):
            seq, k = divmod(j, NIT)
            with nc.Block() as blk:
                def f_v0(e, j=j):
                    e.wait_ge(s_v.h, h_done_v)
                    if hnext_done_v is not None:
                        e.wait_ge(s_v.h, hnext_done_v)
                    s_v.inc(e.tensor_copy(out=hL[(j + 1) % 2][:],
                                          in_=hb[j % 2][:, :, IT - 1]))
                blk.vector(f_v0)

                ab_cs = s_a.v
                vb_cs = s_v.v
                def f_vconv(e, j=j, k=k, ab_cs=ab_cs):
                    for ct in range(FT):
                        w0 = cwt_sb[:, 3 * ct + 0:3 * ct + 1]
                        w1 = cwt_sb[:, 3 * ct + 1:3 * ct + 2]
                        w2 = cwt_sb[:, 3 * ct + 2:3 * ct + 3]
                        if ct >= 4:
                            e.wait_ge(s_a.h, ab_cs + ct - 3)
                        buf = ctmp[ct % 4]
                        e.tensor_scalar_mul(out=buf[:], in0=hb[j % 2][:, ct, :], scalar1=w1)
                        e.scalar_tensor_tensor(out=buf[:, 1:IT], in0=hb[j % 2][:, ct, 0:IT - 1],
                                               scalar=w0, in1=buf[:, 1:IT],
                                               op0=ALU.mult, op1=ALU.add)
                        e.scalar_tensor_tensor(out=buf[:, 0:IT - 1], in0=hb[j % 2][:, ct, 1:IT],
                                               scalar=w2, in1=buf[:, 0:IT - 1],
                                               op0=ALU.mult, op1=ALU.add)
                        if k < NIT - 1:
                            e.scalar_tensor_tensor(out=buf[:, IT - 1:IT],
                                                   in0=hb[(j + 1) % 2][:, ct, 0:1],
                                                   scalar=w2, in1=buf[:, IT - 1:IT],
                                                   op0=ALU.mult, op1=ALU.add)
                        if k > 0:
                            e.scalar_tensor_tensor(out=buf[:, 0:1], in0=hL[j % 2][:, ct:ct + 1],
                                                   scalar=w0, in1=buf[:, 0:1],
                                                   op0=ALU.mult, op1=ALU.add)
                        if ct % 4 == 3:
                            s_v.inc(e.drain())
                blk.vector(f_vconv)

                def f_asilu(e, j=j, vb_cs=vb_cs):
                    for ct in range(FT):
                        e.wait_ge(s_v.h, vb_cs + ct // 4 + 1)
                        s_a.inc(e.activation(out=hb[j % 2][:, ct, :], in_=ctmp[ct % 4][:],
                                             func=AF.Silu, bias=cbias_sb[:, ct:ct + 1]))
                blk.scalar(f_asilu)
                s_v.v += 0

                if KPOST < 2:
                    return
                ab = s_a.v
                def f_v(e, j=j, ab=ab):
                    e.wait_ge(s_a.h, ab)
                    hbr = hb[j % 2][:].rearrange("p ct t -> p t ct")
                    mr = mtree[:].rearrange("p (t o) -> p t o", o=1)
                    s_v.inc(e.reduce_max(out=mr, in_=hbr, axis=AX.X,
                                         apply_absolute_value=True))
                blk.vector(f_v)

                if KPOST < 3:
                    return
                tb = s_t.v
                def f_t(e, tb=tb):
                    s_v.wait(e)
                    for tt in range(4):
                        if tt >= 2:
                            e.wait_ge(s_v.h, s_v.v + tt - 1)
                        s_t.inc(e.matmul(out=tr2_slot(tt),
                                         lhsT=mtree[:, 128 * tt:128 * (tt + 1)],
                                         rhs=identf_sb[:], start=True, stop=True))
                blk.tensor(f_t)

                def f_v(e, tb=tb):
                    for tt in range(4):
                        e.wait_ge(s_t.h, tb + tt + 1)
                        s_v.inc(e.reduce_max(out=m2c[:, tt:tt + 1],
                                             in_=tr2_slot(tt),
                                             axis=AX.X))
                    e.drain()
                    e.tensor_scalar(out=m2c[:], in0=m2c[:], scalar1=1e-5, scalar2=None, op0=ALU.max)
                    e.drain()
                    e.reciprocal(out=s2v[:], in_=m2c[:])
                    e.tensor_scalar_mul(out=c2v[:], in0=m2c[:], scalar1=1.0 / 127.0)
                    e.drain()
                    e.tensor_scalar_mul(out=s2v[:], in0=s2v[:], scalar1=127.0)
                    e.tensor_mul(out=c2v[:], in0=c2v[:], in1=mw2[:].broadcast_to((128, 4)))
                    e.drain()
                    s_v.inc(e.nop())
                blk.vector(f_v)

                tb = s_t.v
                def f_t(e, tb=tb):
                    s_v.wait(e)
                    for tt in range(4):
                        ins = e.matmul(out=ps_bc2[0:1, 128 * tt:128 * (tt + 1)],
                                       lhsT=s2v[:, tt:tt + 1], rhs=identf_sb[:],
                                       start=True, stop=True)
                    s_t.inc(ins)
                blk.tensor(f_t)

                def f_v(e, tb=tb):
                    e.wait_ge(s_t.h, tb + 1)
                    e.tensor_copy(out=s4blk[0:1, :], in_=ps_bc2[0:1, :])
                    s_v.inc(e.drain())
                blk.vector(f_v)

                tb = s_t.v
                def f_t(e, tb=tb):
                    s_v.wait(e)
                    s_t.inc(e.matmul(out=ps_bc2[:], lhsT=onesfr_sb[0:1, :], rhs=s4blk[0:1, :],
                                     start=True, stop=True))
                blk.tensor(f_t)

                if KPOST < 4:
                    return
                def f_v(e, tb=tb, j=j):
                    e.wait_ge(s_t.h, tb + 1)
                    e.tensor_copy(out=s2b[:], in_=ps_bc2[:])
                    for ct in range(FT):
                        buf = ctmp[ct % 2]
                        e.tensor_mul(out=buf[:], in0=hb[j % 2][:, ct, :], in1=s2b[:])
                        e.tensor_scalar(out=hb[j % 2][:, ct, :], in0=buf[:],
                                        scalar1=mvec[:], scalar2=mvec[:],
                                        op0=ALU.add, op1=ALU.subtract)
                    s_q.inc(e.drain())
                blk.vector(f_v)

                if KPOST < 5:
                    return
                TPN = 1 if KPOST == 5 else 2
                qv = s_q.v
                def f_sync(e, j=j, seq=seq, k=k):
                    for tt in range(min(2, 2 * TPN)):
                        gtt = 4 * j + tt
                        if gtt >= 2:
                            e.wait_ge(s_sce.h, gtt - 1)
                        row = 128 * gtt
                        s_scb.inc(e.dma_start(out=scr_pb[gtt % 2][:], in_=scr_att[row:row + 128]), 16)
                    for tp in range(TPN):
                        for cg in range(8):
                            gi = 16 * j + 8 * tp + cg
                            if gi >= 2:
                                e.wait_ge(s_t2.h, gi - 1)
                            s_w2.inc(e.dma_start(
                                out=w2g[gi % 2][:],
                                in_=scr_w2[4 * cg:4 * cg + 4].rearrange("cc p m -> p cc m")), 16)
                    for tt in range(2, 2 * TPN):
                        gtt = 4 * j + tt
                        e.wait_ge(s_sce.h, gtt - 1)
                        row = 128 * gtt
                        s_scb.inc(e.dma_start(out=scr_pb[gtt % 2][:], in_=scr_att[row:row + 128]), 16)
                blk.sync(f_sync)

                def f_t(e, j=j, qv=qv):
                    e.wait_ge(s_q.h, qv)
                    for tp in range(TPN):
                        if tp == 1:
                            e.wait_ge(s_v.h, s_v.v + 4)
                        for cg in range(8):
                            gi = 16 * j + 8 * tp + cg
                            e.wait_ge(s_w2.h, 16 * (gi + 1))
                            for c4 in range(4):
                                ct = 4 * cg + c4
                                for ti in range(2):
                                    tt = 2 * tp + ti
                                    for mh in range(2):
                                        ins = e.matmul(
                                            out=ps_mm[:, 1024 * ti + 512 * mh:1024 * ti + 512 * (mh + 1)],
                                            lhsT=hb[j % 2][:, ct, 128 * tt:128 * (tt + 1)],
                                            rhs=w2g[gi % 2][:, c4, 512 * mh:512 * (mh + 1)],
                                            start=(cg == 0 and c4 == 0),
                                            stop=(cg == 7 and c4 == 3))
                            s_t2.inc(ins)
                blk.tensor(f_t)

                t2b = s_t2.v - 8 * TPN
                def f_v(e, j=j, t2b=t2b):
                    for tp in range(TPN):
                        e.wait_ge(s_t2.h, t2b + 8 * (tp + 1))
                        for ti in range(2):
                            tt = 2 * tp + ti
                            gtt = 4 * j + tt
                            e.wait_ge(s_scb.h, 16 * (gtt + 1))
                            if gtt >= 2:
                                e.wait_ge(s_o.h, 16 * (32 + gtt - 1))
                            for mh in range(2):
                                osl = outsb[gtt % 2][:, 512 * mh:512 * (mh + 1)]
                                psl = ps_mm[:, 1024 * ti + 512 * mh:1024 * ti + 512 * (mh + 1)]
                                ssl = scr_pb[gtt % 2][:, 512 * mh:512 * (mh + 1)]
                                if KEVAC == "scr":
                                    s_v.inc(e.tensor_copy(out=osl, in_=ssl))
                                elif KEVAC == "mm2":
                                    s_v.inc(e.tensor_scalar_mul(out=osl, in0=psl,
                                                                scalar1=c2v[:, tt:tt + 1]))
                                else:
                                    s_v.inc(e.scalar_tensor_tensor(
                                        out=osl, in0=psl, scalar=c2v[:, tt:tt + 1],
                                        in1=ssl, op0=ALU.mult, op1=ALU.add))
                            s_sce.inc(e.drain())
                blk.vector(f_v)

                vb = s_v.v - 4 * TPN
                def f_sync2(e, j=j, vb=vb):
                    for tt in range(2 * TPN):
                        gtt = 4 * j + tt
                        e.wait_ge(s_v.h, vb + 2 * (tt + 1))
                        row = 128 * gtt
                        s_o.inc(e.dma_start(out=out_h[row:row + 128], in_=outsb[gtt % 2][:]), 16)
                blk.sync(f_sync2)

        h_done = {}
        if KSTAGE == 4:
            h_done[0] = emit_h(0)
        elif KSTAGE == 5:
            h_done[0] = emit_h(0)
            h_done[1] = emit_h(1)
            emit_post(0, h_done[0], h_done[1])
        else:
            for seq in range(2):
                for k in range(NIT):
                    it = NIT * seq + k
                    h_done[it] = emit_h(it)
                    if k >= 1:
                        emit_post(it - 1, h_done[it - 1], h_done[it])
                emit_post(NIT * seq + NIT - 1, h_done[NIT * seq + NIT - 1], None)

        es_b.close()
    else:
        with nc.Block() as blk:
            blk.gpsimd(lambda e: e.wait_ge(s_o.h, s_o.v))

    kdump = os.environ.get("KDUMP", "")
    if kdump:
        dbg = {}
        for nm in kdump.split(","):
            t = dbg_tensors[nm]
            shp = [t.shape[0], int(np.prod(t.shape[1:]))] if len(t.shape) > 2 else list(t.shape)
            dt = t[:].dtype if hasattr(t[:], 'dtype') else F32
            dbg[nm] = nc.dram_tensor("dbg_" + nm, shp, dt, kind="ExternalOutput").ap()
        with nc.Block() as blk:
            def f_g(e):
                e.wait_ge(s_o.h, s_o.v)
                for nm in kdump.split(","):
                    t = dbg_tensors[nm]
                    s_gd.inc(e.dma_start(out=dbg[nm], in_=t[:].rearrange("p ... -> p (...)") if len(t.shape) > 2 else t[:]), 16)
            blk.gpsimd(f_g)

    with nc.Block() as blk:
        def f_g(e):
            e.wait_ge(s_o.h, s_o.v)
            e.wait_ge(s_gd.h, s_gd.v)
        blk.gpsimd(f_g)

    esg.close()
    return nc


_NC_CACHE = None


def kernel(**inputs):
    global _NC_CACHE
    x = np.ascontiguousarray(np.asarray(inputs["L0_attn_x"], np.float32))
    Wq = np.asarray(inputs["L0_attn_Wq"], np.float32)
    Wk = np.asarray(inputs["L0_attn_Wk"], np.float32)
    Wv = np.asarray(inputs["L0_attn_Wv"], np.float32)
    Wo = np.asarray(inputs["L0_attn_Wo"], np.float32)
    w1 = np.asarray(inputs["L0_ffn_w1"], np.float32)
    cw = np.asarray(inputs["L0_ffn_conv_w"], np.float32)
    cb = np.asarray(inputs["L0_ffn_conv_b"], np.float32)
    w2 = np.asarray(inputs["L0_ffn_w2"], np.float32)

    wcat = np.concatenate([Wq, Wk, Wv], axis=0)
    wcatT = np.ascontiguousarray(
        wcat.T.reshape(8, 128, 1536).transpose(1, 0, 2).reshape(128, 8 * 1536))
    woT = np.ascontiguousarray(
        Wo.T.reshape(8, 128, D).transpose(1, 0, 2).reshape(128, 8 * D))
    w1T = np.ascontiguousarray(w1.T.reshape(8, 128, DH))
    w2T = np.ascontiguousarray(w2.T.reshape(32, 128, D))
    cwt = np.ascontiguousarray(
        cw[:, 0, :].reshape(FT, 128, 3).transpose(1, 0, 2).reshape(128, FT * 3))
    cbias = np.ascontiguousarray(cb.reshape(FT, 128).T)

    consts = host_constants()

    if _NC_CACHE is None:
        _NC_CACHE = build_nc()
    nc = _NC_CACHE

    common = dict(wcatT=wcatT, woT=woT, w1T=w1T, w2T=w2T, cwt=cwt, cbias=cbias,
                  **consts)
    xr = x.reshape(B, S, D)
    in_maps = []
    for i in range(NCORE):
        m = dict(common)
        m["x"] = np.ascontiguousarray(xr[2 * i:2 * i + 2].reshape(TOK, D))
        in_maps.append(m)

    res = run_bass_kernel_spmd(nc, in_maps, core_ids=list(range(NCORE)))
    kernel._LAST = res.results
    out = np.concatenate([r["out"].reshape(2, S, D) for r in res.results], axis=0)
    return out


# version 29
# speedup vs baseline: 1.0399x; 1.0007x over previous
import numpy as np
import ml_dtypes
from contextlib import ExitStack

import concourse.bass as bass
import concourse.mybir as mybir
from concourse.bass_utils import run_bass_kernel_spmd

F32 = mybir.dt.float32
BF16 = mybir.dt.bfloat16
AF = mybir.ActivationFunctionType
ALU = mybir.AluOpType
AX = mybir.AxisListType

B, S, D = 16, 2048, 1024
NH, NKV, HD = 16, 4, 64
CHUNK = 128
NCHUNK = S // CHUNK
THETA = 10000.0
DH = 4096
FT = DH // 128
NCORE = 8
TOK = (B // NCORE) * S
EPS = 1e-5
IT = 512
NIT = S // IT

MAGIC23 = float(1.5 * 2.0 ** 23)
MAGIC_HI = float(np.float32(1.5 * 2.0 ** 52 - 1.5 * 2.0 ** 23))

PHASE_B = True
import os
KSTAGE = int(os.environ.get("KSTAGE", "99"))
KSUB = int(os.environ.get("KSUB", "99"))
KPOST = int(os.environ.get("KPOST", "99"))
KEVAC = os.environ.get("KEVAC", "")


def host_constants():
    import jax
    import jax.numpy as jnp

    with jax.default_device(jax.devices("cpu")[0]):
        slopes = jnp.asarray([2.0 ** (-8.0 / NH * (i + 1)) for i in range(NH)],
                             dtype=jnp.float32)
        tj = jnp.arange(CHUNK, dtype=jnp.float32)
        sl = slopes[:, None, None]
        mask = jnp.where(tj[:, None] >= tj[None, :],
                         jnp.exp(-sl * (tj[:, None] - tj[None, :])), 0.0)
        qdec = jnp.exp(-slopes[:, None] * (tj + 1.0))
        kdec_j = jnp.exp(-slopes[:, None] * (CHUNK - 1.0 - tj))
        bd_j = jnp.exp(-slopes * CHUNK)
        mask = np.asarray(mask)
        qdec = np.asarray(qdec)
        kdec_j = np.asarray(kdec_j)
        bd_j = np.asarray(bd_j)

    t = np.arange(S, dtype=np.float64)
    inv = 1.0 / THETA ** (np.arange(0, HD, 2, dtype=np.float64) / HD)
    freqs = np.outer(t, inv)
    emb = np.concatenate([freqs, freqs], axis=-1)
    cos32 = np.cos(emb).astype(np.float32)
    sin32 = np.sin(emb).astype(np.float32)
    cos_r = np.ascontiguousarray(cos32.reshape(NCHUNK, 128, HD).transpose(1, 0, 2))
    sin_pm = sin32.copy()
    sin_pm[:, :HD // 2] = -sin_pm[:, :HD // 2]
    sin_r = np.ascontiguousarray(sin_pm.reshape(NCHUNK, 128, HD).transpose(1, 0, 2))

    maskT = np.ascontiguousarray(mask.transpose(2, 0, 1))

    qdec_tile = np.empty((128, NH // 2, CHUNK), np.float32)
    for j in range(NH // 2):
        qdec_tile[:64, j, :] = qdec[2 * j]
        qdec_tile[64:, j, :] = qdec[2 * j + 1]

    kdec = np.ascontiguousarray(kdec_j.T)

    bdvec = np.empty((128, NH // 2), np.float32)
    for j in range(NH // 2):
        bdvec[:64, j] = bd_j[2 * j]
        bdvec[64:, j] = bd_j[2 * j + 1]

    return dict(
        cosr=cos_r.reshape(128, NCHUNK * HD),
        sinr=sin_r.reshape(128, NCHUNK * HD),
        maskT=maskT.reshape(128, NH * CHUNK),
        qdec=qdec_tile.reshape(128, (NH // 2) * CHUNK),
        kdec=kdec, bdvec=bdvec,
        identb=np.eye(128, dtype=ml_dtypes.bfloat16),
        identf=np.eye(128, dtype=np.float32),
        onesb=np.ones((4, 128), dtype=ml_dtypes.bfloat16),
        onesfr=np.ones((4, 128), dtype=np.float32),
        onesfc=np.ones((128, 1), dtype=np.float32),
    )


def build_nc():
    nc = bass.Bass("TRN2", target_bir_lowering=False, debug=False,
                   detect_race_conditions=False)
    di = lambda n, shp, dt=F32: nc.dram_tensor(n, shp, dt, kind="ExternalInput").ap()

    x_h = di("x", [TOK, D])
    wcat_h = di("wcatT", [128, 8 * 1536])
    woT_h = di("woT", [128, 8 * D])
    w1T_h = di("w1T", [8, 128, DH])
    w2T_h = di("w2T", [32, 128, D])
    cwt_h = di("cwt", [128, FT * 3])
    cbias_h = di("cbias", [128, FT])
    cos_h = di("cosr", [128, NCHUNK * HD])
    sin_h = di("sinr", [128, NCHUNK * HD])
    maskT_h = di("maskT", [128, NH * CHUNK])
    qdec_h = di("qdec", [128, (NH // 2) * CHUNK])
    kdec_h = di("kdec", [128, NH])
    bdvec_h = di("bdvec", [128, NH // 2])
    identb_h = di("identb", [128, 128], BF16)
    identf_h = di("identf", [128, 128])
    onesb_h = di("onesb", [4, 128], BF16)
    onesfr_h = di("onesfr", [4, 128])
    onesfc_h = di("onesfc", [128, 1])

    out_h = nc.dram_tensor("out", [TOK, D], F32, kind="ExternalOutput").ap()
    scr_att = nc.dram_tensor("scr_att", [TOK, D], F32, kind="Internal").ap()
    scr_w1 = nc.dram_tensor("scr_w1", [8, 8, 128, 512], BF16, kind="Internal").ap()
    scr_w2 = nc.dram_tensor("scr_w2", [32, 128, D], BF16, kind="Internal").ap()

    esg = ExitStack()
    SB = lambda es, n, shp, dt=F32: es.enter_context(nc.sbuf_tensor(n, shp, dt))
    PSUM = lambda es, n, shp: es.enter_context(nc.psum_tensor(n, shp, F32))

    class Sem:
        __slots__ = ("h", "v")
        def __init__(self, name):
            self.h = esg.enter_context(nc.semaphore(name))
            self.v = 0
        def inc(self, instr, n=1):
            instr.then_inc(self.h, n)
            self.v += n
        def wait(self, e, v=None):
            e.wait_ge(self.h, self.v if v is None else v)

    mvec = SB(esg, "mvec", [128, 1])
    negM = SB(esg, "negM", [128, 1])
    actwarm = SB(esg, "actwarm", [128, 1])
    cwt_sb = SB(esg, "cwt_sb", [128, FT * 3])
    cbias_sb = SB(esg, "cbias_sb", [128, FT])
    identb_sb = SB(esg, "identb_sb", [128, 128], BF16)
    identf_sb = SB(esg, "identf_sb", [128, 128])
    onesb_sb = SB(esg, "onesb_sb", [4, 128], BF16)
    onesfr_sb = SB(esg, "onesfr_sb", [4, 128])
    onesfc_sb = SB(esg, "onesfc_sb", [128, 1])
    rall = SB(esg, "rall", [128, 2 * NCHUNK])
    mall = SB(esg, "mall", [128, 2 * NCHUNK])
    shist = SB(esg, "shist", [128, 2 * NCHUNK])
    vbat = SB(esg, "vbat", [128, 2 * NCHUNK])
    ball = SB(esg, "ball", [128, 2 * NCHUNK])
    mw1 = SB(esg, "mw1", [128, 1])
    mw2 = SB(esg, "mw2", [128, 1])
    sw12 = SB(esg, "sw12", [128, 2])
    wcat_sb = SB(esg, "wcat_sb", [128, 8 * 1536], BF16)
    woT_sb = SB(esg, "woT_sb", [128, 8 * D], BF16)

    s_c = Sem("s_c"); s_x = Sem("s_x"); s_w = Sem("s_w"); s_gd = Sem("s_gd")
    s_v = Sem("s_v"); s_a = Sem("s_a"); s_t = Sem("s_t"); s_o = Sem("s_o")
    s_ts = Sem("s_ts"); s_tk = Sem("s_tk"); s_m = Sem("s_m"); s_k = Sem("s_k")
    s_e = Sem("s_e"); s_q = Sem("s_q"); s_we = Sem("s_we"); s_wo = Sem("s_wo")
    s_xb = Sem("s_xb"); s_scb = Sem("s_scb"); s_w1 = Sem("s_w1"); s_w2 = Sem("s_w2")
    s_t1 = Sem("s_t1"); s_t2 = Sem("s_t2"); s_xc = Sem("s_xc"); s_sce = Sem("s_sce")
    s_k2 = Sem("s_k2")

    with nc.Block() as blk:
        def f_sync(e):
            for dst, src in ((cwt_sb, cwt_h), (cbias_sb, cbias_h), (identb_sb, identb_h),
                             (identf_sb, identf_h), (onesb_sb, onesb_h),
                             (onesfr_sb, onesfr_h), (onesfc_sb, onesfc_h)):
                s_c.inc(e.dma_start(out=dst[:], in_=src), 16)
        blk.sync(f_sync)

        def f_a_warm(e):
            s_c.wait(e)
            e.activation(out=actwarm[:], in_=onesfc_sb[:], func=AF.Ln)
            e.drain()
            e.activation(out=actwarm[:], in_=actwarm[:], func=AF.Exp)
            e.drain()
        blk.scalar(f_a_warm)

        def f_v(e):
            s_c.wait(e)
            e.memset(mvec[:], 0.25)
            e.drain()
            e.tensor_scalar(out=mvec[:], in0=mvec[:], scalar1=MAGIC23, scalar2=MAGIC23,
                            op0=ALU.add, op1=ALU.subtract)
            e.drain()
            e.tensor_scalar(out=mvec[:], in0=mvec[:], scalar1=0.25, scalar2=None, op0=ALU.is_equal)
            e.drain()
            e.tensor_scalar(out=mvec[:], in0=mvec[:], scalar1=MAGIC_HI, scalar2=MAGIC23,
                            op0=ALU.mult, op1=ALU.add)
            e.drain()
            s_v.inc(e.tensor_scalar_mul(out=negM[:], in0=mvec[:], scalar1=-1.0))
        blk.vector(f_v)

    es_p = ExitStack()
    wtile = [SB(es_p, f"wtile{i}", [128, DH]) for i in range(2)]
    wqb = [SB(es_p, f"wqb{i}", [128, DH], BF16) for i in range(2)]
    wsum1 = SB(es_p, "wsum1", [128, 1])
    wsum2 = SB(es_p, "wsum2", [128, 1])
    rtmp = SB(es_p, "rtmp", [128, 1])
    sc2 = SB(es_p, "sc2", [1, 2])
    wttmp = SB(es_p, "wttmp", [128, DH])
    ps_pp = PSUM(es_p, "ps_pp", [128, 16])

    NWT = 40
    def w_src(i):
        return (w1T_h[i], DH) if i < 8 else (w2T_h[i - 8], D)

    with nc.Block() as blk:
        v0 = s_v.v
        def f_sync(e):
            for i in range(NWT):
                if i >= 2:
                    e.wait_ge(s_v.h, v0 + i - 1)
                src, flen = w_src(i)
                s_w.inc(e.dma_start(out=wtile[i % 2][:, 0:flen], in_=src), 16)
        blk.sync(f_sync)

        def f_v(e):
            e.memset(wsum1[:], 0.0)
            e.memset(wsum2[:], 0.0)
            for i in range(NWT):
                _, flen = w_src(i)
                e.wait_ge(s_w.h, 16 * (i + 1))
                e.reduce_sum(out=rtmp[:], in_=wtile[i % 2][:, 0:flen], axis=AX.X,
                             apply_absolute_value=True)
                e.drain()
                acc = wsum1 if i < 8 else wsum2
                s_v.inc(e.tensor_add(out=acc[:], in0=acc[:], in1=rtmp[:]))
            s_v.inc(e.drain())
        blk.vector(f_v)

        def f_t(e):
            s_v.wait(e)
            e.matmul(out=ps_pp[0:1, 0:1], lhsT=wsum1[:], rhs=onesfc_sb[:], start=True, stop=True)
            s_t.inc(e.matmul(out=ps_pp[0:1, 1:2], lhsT=wsum2[:], rhs=onesfc_sb[:],
                             start=True, stop=True))
        blk.tensor(f_t)

        def f_v2(e):
            s_t.wait(e)
            e.tensor_scalar(out=sc2[0:1, 0:1], in0=ps_pp[0:1, 0:1],
                            scalar1=1.0 / (DH * D), scalar2=EPS, op0=ALU.mult, op1=ALU.max)
            e.tensor_scalar(out=sc2[0:1, 1:2], in0=ps_pp[0:1, 1:2],
                            scalar1=1.0 / (DH * D), scalar2=EPS, op0=ALU.mult, op1=ALU.max)
            s_v.inc(e.drain())
        blk.vector(f_v2)

        def f_t2(e):
            s_v.wait(e)
            s_t.inc(e.matmul(out=ps_pp[:, 2:4], lhsT=onesfr_sb[0:1, :], rhs=sc2[0:1, :],
                             start=True, stop=True))
        blk.tensor(f_t2)

        def f_v3(e):
            s_t.wait(e)
            e.tensor_copy(out=mw1[:], in_=ps_pp[:, 2:3])
            e.tensor_copy(out=mw2[:], in_=ps_pp[:, 3:4])
            e.drain()
            e.reciprocal(out=sw12[:, 0:1], in_=mw1[:])
            e.reciprocal(out=sw12[:, 1:2], in_=mw2[:])
            s_v.inc(e.drain())
        blk.vector(f_v3)

    with nc.Block() as blk:
        v0 = s_v.v
        g0 = s_gd.v
        def f_sync(e):
            s_v.wait(e)
            for i in range(NWT):
                if i >= 2:
                    e.wait_ge(s_v.h, v0 + i - 1)
                src, flen = w_src(i)
                s_w.inc(e.dma_start(out=wtile[i % 2][:, 0:flen], in_=src), 16)
        blk.sync(f_sync)

        w0 = s_w.v - 16 * NWT
        def f_v(e):
            for i in range(NWT):
                _, flen = w_src(i)
                col = 0 if i < 8 else 1
                e.wait_ge(s_w.h, w0 + 16 * (i + 1))
                if i >= 2:
                    e.wait_ge(s_gd.h, g0 + 16 * (i - 1))
                sl = (slice(0, 128), slice(0, flen))
                e.tensor_scalar(out=wttmp[sl], in0=wtile[i % 2][sl], scalar1=sw12[:, col:col + 1],
                                scalar2=mvec[:], op0=ALU.mult, op1=ALU.add)
                e.tensor_scalar(out=wttmp[sl], in0=wttmp[sl], scalar1=mvec[:],
                                scalar2=1.0, op0=ALU.subtract, op1=ALU.min)
                s_v.inc(e.tensor_scalar(out=wqb[i % 2][sl], in0=wttmp[sl],
                                        scalar1=-1.0, scalar2=None, op0=ALU.max))
        blk.vector(f_v)

        def f_g(e):
            for i in range(NWT):
                e.wait_ge(s_v.h, v0 + i + 1)
                if i < 8:
                    dst_ap = scr_w1[:, i].rearrange("g p f -> p g f")
                    s_gd.inc(e.dma_start(out=dst_ap, in_=wqb[i % 2][:]), 16)
                else:
                    s_gd.inc(e.dma_start(out=scr_w2[i - 8], in_=wqb[i % 2][:, 0:D]), 16)
        blk.gpsimd(f_g)

    p0tmp = SB(es_p, "p0tmp", [128, D])
    with nc.Block() as blk:
        w0b = s_w.v
        v0b = s_v.v
        a0b = s_a.v
        def f_sync(e, w0b=w0b, v0b=v0b, a0b=a0b):
            for c in range(2 * NCHUNK):
                if c >= 2:
                    e.wait_ge(s_v.h, v0b + c - 1)
                    e.wait_ge(s_a.h, a0b + c - 1)
                s_w.inc(e.dma_start(out=wtile[c % 2][:, 0:D], in_=x_h[128 * c:128 * (c + 1)]), 16)
        blk.sync(f_sync)

        def f_v(e, w0b=w0b):
            for c in range(2 * NCHUNK):
                e.wait_ge(s_w.h, w0b + 16 * (c + 1))
                s_v.inc(e.reduce_sum(out=mall[:, c:c + 1], in_=wtile[c % 2][:, 0:D], axis=AX.X))
        blk.vector(f_v)

        def f_a(e, w0b=w0b, v0b=v0b):
            for c in range(2 * NCHUNK):
                e.wait_ge(s_v.h, v0b + c + 1)
                s_a.inc(e.activation(out=p0tmp[:], in_=wtile[c % 2][:, 0:D], func=AF.Square,
                                     accum_out=shist[:, c:c + 1]))
            s_a.inc(e.drain())
        blk.scalar(f_a)

        def f_v2(e):
            s_a.wait(e)
            e.drain()
            e.tensor_scalar_mul(out=mall[:], in0=mall[:], scalar1=1.0 / D)
            e.drain()
            e.tensor_mul(out=vbat[:], in0=mall[:], in1=mall[:])
            e.tensor_scalar(out=shist[:], in0=shist[:], scalar1=1.0 / D, scalar2=EPS,
                            op0=ALU.mult, op1=ALU.add)
            e.drain()
            e.tensor_sub(out=vbat[:], in0=shist[:], in1=vbat[:])
            s_v.inc(e.drain())
        blk.vector(f_v2)

        def f_a2(e):
            s_v.wait(e)
            e.activation(out=vbat[:], in_=vbat[:], func=AF.Ln)
            e.drain()
            e.activation(out=rall[:], in_=vbat[:], func=AF.Exp, scale=-0.5)
            s_a.inc(e.drain())
        blk.scalar(f_a2)

        def f_v3(e):
            s_a.wait(e)
            e.tensor_mul(out=ball[:], in0=mall[:], in1=rall[:])
            e.drain()
            e.tensor_scalar_mul(out=ball[:], in0=ball[:], scalar1=-1.0)
            s_v.inc(e.drain())
        blk.vector(f_v3)

    with nc.Block() as blk:
        def f_g(e):
            s_gd.inc(e.dma_start(out=wcat_sb[:], in_=wcat_h), 16)
            s_gd.inc(e.dma_start(out=woT_sb[:], in_=woT_h), 16)
        blk.gpsimd(f_g)

    es_p.close()
    es_a = ExitStack()
    xs = [SB(es_a, f"xs{i}", [128, D]) for i in range(2)]
    att = [SB(es_a, f"att{i}", [128, D]) for i in range(2)]
    xsum = SB(es_a, "xsum", [128, 1])
    sqsum = SB(es_a, "sqsum", [128, 1])
    mcol = SB(es_a, "mcol", [128, 1])
    msq = SB(es_a, "msq", [128, 1])
    vtmp = SB(es_a, "vtmp", [128, 1])
    ltmp = SB(es_a, "ltmp", [128, 1])
    btmp = SB(es_a, "btmp", [128, 1])
    xn_bf = SB(es_a, "xn_bf", [128, D], BF16)
    xnT = SB(es_a, "xnT", [128, 8, 128], BF16)
    tmpa = SB(es_a, "tmpa", [128, D])
    tmpb = SB(es_a, "tmpb", [128, D])
    q_bf = SB(es_a, "q_bf", [128, D], BF16)
    k_bf = SB(es_a, "k_bf", [128, 256], BF16)
    v_bf = SB(es_a, "v_bf", [128, 256], BF16)
    qT = SB(es_a, "qT", [128, 8, 128], BF16)
    qdq = SB(es_a, "qdq", [128, 8, 128], BF16)
    kTba = SB(es_a, "kTba", [128, 4, 128], BF16)
    kdk = SB(es_a, "kdk", [128, NH * 64], BF16)
    sc_bf = SB(es_a, "sc_bf", [128, 4 * 128], BF16)
    oT = SB(es_a, "oT", [128, 8, 128], BF16)
    kv_sb = [SB(es_a, f"kv_sb{i}", [128, 8, 64]) for i in range(2)]
    kv_bf = SB(es_a, "kv_bf", [128, 8, 64], BF16)
    cos_sb = SB(es_a, "cos_sb", [128, NCHUNK * HD])
    sin_sb = SB(es_a, "sin_sb", [128, NCHUNK * HD])
    maskT_sb = SB(es_a, "maskT_sb", [128, NH * CHUNK])
    qdec_sb = SB(es_a, "qdec_sb", [128, (NH // 2) * CHUNK])
    kdec_sb = SB(es_a, "kdec_sb", [128, NH])
    bdvec_sb = SB(es_a, "bdvec_sb", [128, NH // 2])

    dbg_tensors = dict(xn_bf=xn_bf, q_bf=q_bf, k_bf=k_bf, v_bf=v_bf, qT=qT,
                       kTba=kTba, qdq=qdq, kdk=kdk, sc_bf=sc_bf, oT=oT,
                       kv_sb0=kv_sb[0], kv_bf=kv_bf, rall=rall, ball=ball,
                       mvec=mvec, negM=negM, wcat_sb=wcat_sb, woT_sb=woT_sb,
                       mw1=mw1, mw2=mw2, sw12=sw12, xnT=xnT)
    ps_qkv = PSUM(es_a, "ps_qkv", [128, 1536])
    ps_sctr = PSUM(es_a, "ps_sctr", [128, 1024])
    ps_o = PSUM(es_a, "ps_o", [128, 1024])
    ps_kao = PSUM(es_a, "ps_kao", [128, 512])

    def tr_slot(j):
        return ps_sctr[:, 512 * (j % 2):512 * (j % 2) + 128]

    def sc_slot(h):
        base = 512 * (h % 2) + 128 * ((h // 2) % 2)
        return ps_sctr[:, base:base + 128]

    def o_slot(j):
        base = 512 * (j % 2) + 128 * ((j // 2) % 2)
        return ps_o[:, base:base + 128]

    with nc.Block() as blk:
        def f_sync(e):
            e.wait_ge(s_gd.h, s_gd.v)
            for dst, src in ((cos_sb, cos_h), (sin_sb, sin_h), (maskT_sb, maskT_h),
                             (qdec_sb, qdec_h), (kdec_sb, kdec_h), (bdvec_sb, bdvec_h)):
                s_c.inc(e.dma_start(out=dst[:], in_=src), 16)
            s_x.inc(e.dma_start(out=xs[0][:], in_=x_h[0:128]), 16)
        blk.sync(f_sync)

    ln_ready = s_v.v
    rope_done = {}
    attst_target = scr_att if (PHASE_B and KSTAGE >= 4) else out_h

    n_chunks = 0 if KSTAGE < 2 else (1 if KSTAGE == 2 else 2 * NCHUNK)
    for c in range(n_chunks):
        seq, cc = divmod(c, NCHUNK)
        col = slice(c, c + 1)
        with nc.Block() as blk:
            if c + 1 < 2 * NCHUNK:
                def f_sync(e, c=c):
                    if c + 1 >= 2:
                        e.wait_ge(s_we.h, 2 * c)
                    s_x.inc(e.dma_start(out=xs[(c + 1) % 2][:],
                                        in_=x_h[128 * (c + 1):128 * (c + 2)]), 16)
                blk.sync(f_sync)

            def f_a(e, c=c, col=col, lnv=ln_ready):
                e.wait_ge(s_x.h, 16 * (c + 1))
                if c == 0:
                    e.wait_ge(s_v.h, lnv)
                s_a.inc(e.activation(out=xn_bf[:], in_=xs[c % 2][:], func=AF.Identity,
                                     scale=rall[:, col], bias=ball[:, col]))
            blk.scalar(f_a)

            if KSUB < 2:
                continue
            tb = s_t.v; ab = s_a.v
            def f_t(e, tb=tb, ab=ab, c=c):
                s_a.wait(e)
                if c == 0:
                    s_c.wait(e)
                else:
                    e.wait_ge(s_m.h, 16 * c)
                for j in range(8):
                    if j >= 2:
                        e.wait_ge(s_a.h, ab + j - 1)
                    s_t.inc(e.matmul(out=tr_slot(j),
                                     lhsT=xn_bf[:, 128 * j:128 * (j + 1)], rhs=identb_sb[:],
                                     start=True, stop=True))
            blk.tensor(f_t)

            def f_a(e, tb=tb):
                for j in range(8):
                    e.wait_ge(s_t.h, tb + j + 1)
                    s_a.inc(e.activation(out=xnT[:, j, :], in_=tr_slot(j), func=AF.Identity))
            blk.scalar(f_a)

            if KSUB < 3:
                continue
            ab = s_a.v
            def f_t(e, c=c, ab=ab):
                e.wait_ge(s_a.h, ab)
                e.wait_ge(s_gd.h, 16 * 42)
                if c > 0:
                    e.wait_ge(s_v.h, rope_done[c - 1])
                for dc in range(8):
                    for b3 in range(3):
                        ins = e.matmul(out=ps_qkv[:, 512 * b3:512 * (b3 + 1)],
                                       lhsT=xnT[:, dc, :],
                                       rhs=wcat_sb[:, 1536 * dc + 512 * b3:1536 * dc + 512 * (b3 + 1)],
                                       start=(dc == 0), stop=(dc == 7))
                s_t.inc(ins)
            blk.tensor(f_t)

            def f_v(e, c=c, cc=cc):
                s_t.wait(e)
                if c == 0:
                    s_c.wait(e)
                csl = cos_sb[:, HD * cc:HD * (cc + 1)].rearrange("p (o d) -> p o d", o=1)
                ssl = sin_sb[:, HD * cc:HD * (cc + 1)].rearrange("p (o d) -> p o d", o=1)
                qv = ps_qkv[:, 0:D].rearrange("p (h d) -> p h d", h=NH)
                ta = tmpa[:].rearrange("p (h d) -> p h d", h=NH)
                tb_ = tmpb[:].rearrange("p (h d) -> p h d", h=NH)
                e.tensor_mul(out=ta, in0=qv, in1=csl.broadcast_to((128, NH, HD)))
                e.tensor_mul(out=tb_[:, :, 0:32], in0=qv[:, :, 32:64],
                             in1=ssl[:, :, 0:32].broadcast_to((128, NH, 32)))
                e.tensor_mul(out=tb_[:, :, 32:64], in0=qv[:, :, 0:32],
                             in1=ssl[:, :, 32:64].broadcast_to((128, NH, 32)))
                e.tensor_add(out=q_bf[:], in0=tmpa[:], in1=tmpb[:])
                kv_ = ps_qkv[:, D:D + 256].rearrange("p (h d) -> p h d", h=NKV)
                ta4 = tmpa[:, 0:256].rearrange("p (h d) -> p h d", h=NKV)
                tb4 = tmpb[:, 0:256].rearrange("p (h d) -> p h d", h=NKV)
                e.tensor_mul(out=ta4, in0=kv_, in1=csl.broadcast_to((128, NKV, HD)))
                e.tensor_mul(out=tb4[:, :, 0:32], in0=kv_[:, :, 32:64],
                             in1=ssl[:, :, 0:32].broadcast_to((128, NKV, 32)))
                e.tensor_mul(out=tb4[:, :, 32:64], in0=kv_[:, :, 0:32],
                             in1=ssl[:, :, 32:64].broadcast_to((128, NKV, 32)))
                e.tensor_add(out=k_bf[:], in0=tmpa[:, 0:256], in1=tmpb[:, 0:256])
                e.tensor_copy(out=v_bf[:], in_=ps_qkv[:, D + 256:D + 512])
                for hh in range(NH):
                    g = hh // 4
                    ins = e.tensor_scalar_mul(out=kdk[:, 64 * hh:64 * (hh + 1)],
                                              in0=k_bf[:, 64 * g:64 * (g + 1)],
                                              scalar1=kdec_sb[:, hh:hh + 1])
                s_v.inc(ins)
                rope_done[c] = s_v.v
            blk.vector(f_v)

            if KSUB < 4:
                continue
            tb = s_t.v; ab = s_a.v
            def f_t(e, tb=tb, ab=ab, c=c):
                e.wait_ge(s_v.h, rope_done[c])
                for j in range(8):
                    if j >= 2:
                        e.wait_ge(s_a.h, ab + j - 1)
                    s_t.inc(e.matmul(out=tr_slot(j),
                                     lhsT=q_bf[:, 128 * j:128 * (j + 1)], rhs=identb_sb[:],
                                     start=True, stop=True))
                for g in range(4):
                    jj = 8 + g
                    e.wait_ge(s_a.h, ab + jj - 1)
                    sl = tr_slot(jj)
                    e.matmul(out=sl[0:64, :], lhsT=k_bf[:, 64 * g:64 * (g + 1)],
                             rhs=identb_sb[:], start=True, stop=True)
                    s_t.inc(e.matmul(out=sl[64:128, :], lhsT=k_bf[:, 64 * g:64 * (g + 1)],
                                     rhs=identb_sb[:], start=True, stop=True))
            blk.tensor(f_t)

            def f_a(e, tb=tb):
                for j in range(8):
                    e.wait_ge(s_t.h, tb + j + 1)
                    s_a.inc(e.activation(out=qT[:, j, :], in_=tr_slot(j), func=AF.Identity))
                for g in range(4):
                    jj = 8 + g
                    e.wait_ge(s_t.h, tb + jj + 1)
                    s_a.inc(e.activation(out=kTba[:, g, :], in_=tr_slot(jj), func=AF.Identity))
            blk.scalar(f_a)

            ab = s_a.v
            def f_v(e, ab=ab, c=c):
                e.wait_ge(s_a.h, ab - 4)
                if c == 0:
                    s_c.wait(e)
                for j in range(8):
                    ins = e.tensor_mul(out=qdq[:, j, :], in0=qT[:, j, :],
                                       in1=qdec_sb[:, 128 * j:128 * (j + 1)])
                s_q.inc(ins)
            blk.vector(f_v)

            if KSUB < 5:
                continue
            TSb = s_ts.v; TKb = s_tk.v; Mb = s_m.v; Kb = s_k.v; Eb = s_e.v
            Ab = s_a.v
            def f_t(e, c=c, cc=cc, TSb=TSb, TKb=TKb, Mb=Mb, Kb=Kb, Eb=Eb, Ab=Ab):
                e.wait_ge(s_a.h, Ab)
                e.wait_ge(s_q.h, s_q.v)
                if c > 0:
                    e.wait_ge(s_we.h, 2 * c)
                for hh in range(NH):
                    g = hh // 4
                    p64 = 64 * (hh % 2)
                    j = hh // 2
                    slot = hh % 4
                    if hh >= 2:
                        e.wait_ge(s_m.h, Mb + hh - 1)
                    s_ts.inc(e.matmul(out=sc_slot(hh),
                                      lhsT=kTba[p64:p64 + 64, g, :],
                                      rhs=qT[p64:p64 + 64, hh // 2, :],
                                      start=True, stop=True))
                    e.wait_ge(s_m.h, Mb + hh + 1)
                    if j >= 2 and hh % 2 == 0:
                        e.wait_ge(s_e.h, Eb + j - 1)
                    first = (cc == 0)
                    e.matmul(out=o_slot(j)[p64:p64 + 64, :],
                             lhsT=v_bf[:, 64 * g:64 * (g + 1)],
                             rhs=sc_bf[:, 128 * slot:128 * (slot + 1)],
                             start=True, stop=first)
                    if not first:
                        e.wait_ge(s_k2.h, c)
                        e.matmul(out=o_slot(j)[p64:p64 + 64, :],
                                 lhsT=kv_bf[p64:p64 + 64, j, :],
                                 rhs=qdq[p64:p64 + 64, j, :],
                                 start=False, stop=True)
                    kslot = j % 4
                    if j >= 1 and hh % 2 == 0:
                        e.wait_ge(s_k.h, Kb + j)
                    ins = e.matmul(out=ps_kao[p64:p64 + 64, 64 * kslot:64 * (kslot + 1)],
                                   lhsT=kdk[:, 64 * hh:64 * (hh + 1)],
                                   rhs=v_bf[:, 64 * g:64 * (g + 1)],
                                   start=True, stop=True)
                    if hh % 2 == 1:
                        s_tk.inc(ins)
            blk.tensor(f_t)

            def f_v(e, cc=cc, TSb=TSb, TKb=TKb, c=c):
                if c == 0:
                    s_c.wait(e)
                for hh in range(NH):
                    slot = hh % 4
                    e.wait_ge(s_ts.h, TSb + hh + 1)
                    s_m.inc(e.tensor_mul(out=sc_bf[:, 128 * slot:128 * (slot + 1)],
                                         in0=sc_slot(hh),
                                         in1=maskT_sb[:, 128 * hh:128 * (hh + 1)]))
                    if hh % 2 == 1:
                        j = hh // 2
                        kslot = j % 4
                        e.wait_ge(s_tk.h, TKb + j + 1)
                        if cc == 0:
                            e.tensor_copy(out=kv_sb[c % 2][:, j, :],
                                          in_=ps_kao[:, 64 * kslot:64 * (kslot + 1)])
                            s_k.inc(e.tensor_copy(out=kv_bf[:, j, :],
                                          in_=ps_kao[:, 64 * kslot:64 * (kslot + 1)]))
                        else:
                            e.scalar_tensor_tensor(
                                out=kv_sb[c % 2][:, j, :], in0=kv_sb[(c - 1) % 2][:, j, :],
                                scalar=bdvec_sb[:, j:j + 1],
                                in1=ps_kao[:, 64 * kslot:64 * (kslot + 1)],
                                op0=ALU.mult, op1=ALU.add)
                            s_k.inc(e.scalar_tensor_tensor(
                                out=kv_bf[:, j, :], in0=kv_sb[(c - 1) % 2][:, j, :],
                                scalar=bdvec_sb[:, j:j + 1],
                                in1=ps_kao[:, 64 * kslot:64 * (kslot + 1)],
                                op0=ALU.mult, op1=ALU.add))
                s_k2.inc(e.drain())
            blk.vector(f_v)

            def f_a(e, TKb=TKb):
                for j in range(8):
                    e.wait_ge(s_tk.h, TKb + j + 1)
                    s_e.inc(e.activation(out=oT[:, j, :], in_=o_slot(j), func=AF.Identity))
            blk.scalar(f_a)

            if KSUB < 6:
                continue
            Eb2 = s_e.v
            def f_t(e, c=c, Eb2=Eb2):
                e.wait_ge(s_e.h, Eb2)
                s_k.wait(e)
                for half in range(2):
                    if 2 * c + half >= 1:
                        e.wait_ge(s_we.h, 2 * c + half)
                    for j in range(8):
                        ins = e.matmul(out=ps_kao[:],
                                       lhsT=oT[:, j, :],
                                       rhs=woT_sb[:, D * j + 512 * half:D * j + 512 * (half + 1)],
                                       start=(j == 0), stop=(j == 7))
                    s_wo.inc(ins)
            blk.tensor(f_t)

            def f_v(e, c=c):
                for half in range(2):
                    e.wait_ge(s_wo.h, 2 * c + half + 1)
                    if c >= 2:
                        e.wait_ge(s_o.h, 16 * (c - 1))
                    s_we.inc(e.tensor_add(out=att[c % 2][:, 512 * half:512 * (half + 1)],
                                          in0=ps_kao[:],
                                          in1=xs[c % 2][:, 512 * half:512 * (half + 1)]))
            blk.vector(f_v)

            def f_sync(e, c=c):
                e.wait_ge(s_we.h, 2 * c + 2)
                s_o.inc(e.dma_start(out=attst_target[128 * c:128 * (c + 1)],
                                    in_=att[c % 2][:]), 16)
            blk.sync(f_sync)

    es_a.close()

    if PHASE_B and KSTAGE >= 4:
        es_b = ExitStack()
        hb = [SB(es_b, f"hb{i}", [128, FT, IT], BF16) for i in range(2)]
        hL = [SB(es_b, f"hL{i}", [128, FT], BF16) for i in range(2)]
        x_pb = [SB(es_b, f"x_pb{i}", [128, D]) for i in range(2)]
        scr_pb = [SB(es_b, f"scr_pb{i}", [128, D]) for i in range(2)]
        outsb = [SB(es_b, f"outsb{i}", [128, D]) for i in range(2)]
        xn_t = SB(es_b, "xn_t", [128, D])
        t1_t = SB(es_b, "t1_t", [128, D])
        xq_sb = SB(es_b, "xq_sb", [128, 4, D], BF16)
        xqT_sb = SB(es_b, "xqT_sb", [128, 8, IT], BF16)
        w1g = [SB(es_b, f"w1g{i}", [128, 8, 512], BF16) for i in range(2)]
        w2g = [SB(es_b, f"w2g{i}", [128, 4, D], BF16) for i in range(2)]
        ctmp = [SB(es_b, f"ctmp{i}", [128, IT]) for i in range(4)]
        mtree = SB(es_b, "mtree", [128, IT])
        m2c = SB(es_b, "m2c", [128, 4])
        s2v = SB(es_b, "s2v", [128, 4])
        c2v = SB(es_b, "c2v", [128, 4])
        s4blk = SB(es_b, "s4blk", [1, IT])
        s2b = SB(es_b, "s2b", [128, IT])
        q1s = SB(es_b, "q1s", [128, 1])
        q1c = SB(es_b, "q1c", [128, 1])
        q1b = SB(es_b, "q1b", [128, 1])
        dbg_tensors.update(hb0=hb[0], hb1=hb[1], mtree=mtree, m2c=m2c, s2v=s2v,
                           c2v=c2v, s2b=s2b, xq_sb=xq_sb, xqT_sb=xqT_sb, hL0=hL[0], hL1=hL[1])
        ps_mm = PSUM(es_b, "ps_mm", [128, 2048])
        ps_tr2 = PSUM(es_b, "ps_tr2", [128, 1024])
        tr2_slot = lambda i: ps_tr2[:, 512 * (i % 2):512 * (i % 2) + 128]
        ps_bc2 = PSUM(es_b, "ps_bc2", [128, 512])
        ps_fill = PSUM(es_b, "ps_fill", [128, 512])

        def emit_h(it):
            seq, k = divmod(it, NIT)
            with nc.Block() as blk:
                def f_sync(e, it=it):
                    if it == 0:
                        e.wait_ge(s_o.h, s_o.v)
                    for tt in range(4):
                        gtt = 4 * it + tt
                        if gtt >= 2:
                            e.wait_ge(s_xc.h, gtt - 1)
                        row = 128 * gtt
                        s_xb.inc(e.dma_start(out=x_pb[gtt % 2][:],
                                             in_=x_h[row:row + 128]), 16)
                    for g in range(8):
                        gg = 8 * it + g
                        if gg >= 2:
                            e.wait_ge(s_t1.h, gg - 1)
                        s_w1.inc(e.dma_start(out=w1g[g % 2][:],
                                             in_=scr_w1[g].rearrange("dc p f -> p dc f")), 16)
                blk.sync(f_sync)

                for tt in range(4):
                    gtt = 4 * it + tt
                    col = slice(seq * NCHUNK + 4 * k + tt, seq * NCHUNK + 4 * k + tt + 1)
                    def f_a1(e, gtt=gtt, col=col):
                        e.wait_ge(s_xb.h, 16 * (gtt + 1))
                        s_a.inc(e.activation(out=xn_t[:], in_=x_pb[gtt % 2][:], func=AF.Identity,
                                             scale=rall[:, col], bias=ball[:, col]))
                        s_xc.inc(e.drain())
                    blk.scalar(f_a1)

                    def f_v1(e):
                        s_a.wait(e)
                        e.reduce_max(out=q1s[:], in_=xn_t[:], axis=AX.X, apply_absolute_value=True)
                        e.drain()
                        e.tensor_scalar(out=q1s[:], in0=q1s[:], scalar1=1e-5, scalar2=None, op0=ALU.max)
                        e.drain()
                        e.tensor_scalar_mul(out=q1c[:], in0=q1s[:], scalar1=1.0 / 127.0)
                        e.reciprocal(out=q1s[:], in_=q1s[:])
                        e.drain()
                        e.tensor_mul(out=q1c[:], in0=q1c[:], in1=mw1[:])
                        e.tensor_scalar_mul(out=q1s[:], in0=q1s[:], scalar1=127.0)
                        e.drain()
                        e.tensor_mul(out=q1b[:], in0=negM[:], in1=q1c[:])
                        s_v.inc(e.drain())
                    blk.vector(f_v1)

                    def f_a2(e, tt=tt):
                        s_v.wait(e)
                        e.activation(out=t1_t[:], in_=xn_t[:], func=AF.Identity,
                                     scale=q1s[:], bias=mvec[:])
                        e.activation(out=xn_t[:], in_=t1_t[:], func=AF.Identity,
                                     bias=negM[:])
                        s_a.inc(e.activation(out=xq_sb[:, tt, :], in_=xn_t[:], func=AF.Identity,
                                             scale=q1c[:]))
                    blk.scalar(f_a2)

                tb = s_t.v; ab = s_a.v
                def f_t(e, tb=tb, ab=ab):
                    for tt in range(4):
                        e.wait_ge(s_a.h, ab - 8 + 2 * (tt + 1))
                        for j in range(8):
                            idx = 8 * tt + j
                            if idx >= 2:
                                e.wait_ge(s_a.h, ab + idx - 1)
                            s_t.inc(e.matmul(out=tr2_slot(idx),
                                             lhsT=xq_sb[:, tt, 128 * j:128 * (j + 1)],
                                             rhs=identb_sb[:], start=True, stop=True))
                blk.tensor(f_t)

                def f_a(e, tb=tb):
                    for idx in range(32):
                        tt, j = divmod(idx, 8)
                        e.wait_ge(s_t.h, tb + idx + 1)
                        s_a.inc(e.activation(out=xqT_sb[:, j, 128 * tt:128 * (tt + 1)],
                                             in_=tr2_slot(idx), func=AF.Identity))
                blk.scalar(f_a)

                ab = s_a.v
                vb_mm1 = s_v.v
                def f_t(e, it=it, ab=ab, vb_mm1=vb_mm1):
                    e.wait_ge(s_a.h, ab)
                    for g in range(8):
                        e.wait_ge(s_w1.h, 16 * (8 * it + g + 1))
                        if g >= 1:
                            e.wait_ge(s_v.h, vb_mm1 + 4 * g)
                        for dc in range(8):
                            for fi in range(4):
                                ins = e.matmul(out=ps_mm[:, 512 * fi:512 * (fi + 1)],
                                               lhsT=w1g[g % 2][:, dc, 128 * fi:128 * (fi + 1)],
                                               rhs=xqT_sb[:, dc, :],
                                               start=(dc == 0), stop=(dc == 7))
                        s_t1.inc(ins)
                blk.tensor(f_t)

                t1b = s_t1.v - 8
                def f_v(e, it=it, t1b=t1b):
                    for g in range(8):
                        e.wait_ge(s_t1.h, t1b + g + 1)
                        for fi in range(4):
                            ct = 4 * g + fi
                            s_v.inc(e.tensor_copy(out=hb[it % 2][:, ct, :],
                                                  in_=ps_mm[:, 512 * fi:512 * (fi + 1)]))
                blk.vector(f_v)
            return s_v.v

        def emit_post(j, h_done_v, hnext_done_v):
            seq, k = divmod(j, NIT)
            with nc.Block() as blk:
                def f_v0(e, j=j):
                    e.wait_ge(s_v.h, h_done_v)
                    if hnext_done_v is not None:
                        e.wait_ge(s_v.h, hnext_done_v)
                    s_v.inc(e.tensor_copy(out=hL[(j + 1) % 2][:],
                                          in_=hb[j % 2][:, :, IT - 1]))
                blk.vector(f_v0)

                ab_cs = s_a.v
                vb_cs = s_v.v
                def f_vconv(e, j=j, k=k, ab_cs=ab_cs):
                    for ct in range(FT):
                        w0 = cwt_sb[:, 3 * ct + 0:3 * ct + 1]
                        w1 = cwt_sb[:, 3 * ct + 1:3 * ct + 2]
                        w2 = cwt_sb[:, 3 * ct + 2:3 * ct + 3]
                        if ct >= 4:
                            e.wait_ge(s_a.h, ab_cs + ct - 3)
                        buf = ctmp[ct % 4]
                        e.tensor_scalar_mul(out=buf[:], in0=hb[j % 2][:, ct, :], scalar1=w1)
                        e.scalar_tensor_tensor(out=buf[:, 1:IT], in0=hb[j % 2][:, ct, 0:IT - 1],
                                               scalar=w0, in1=buf[:, 1:IT],
                                               op0=ALU.mult, op1=ALU.add)
                        e.scalar_tensor_tensor(out=buf[:, 0:IT - 1], in0=hb[j % 2][:, ct, 1:IT],
                                               scalar=w2, in1=buf[:, 0:IT - 1],
                                               op0=ALU.mult, op1=ALU.add)
                        if k < NIT - 1:
                            e.scalar_tensor_tensor(out=buf[:, IT - 1:IT],
                                                   in0=hb[(j + 1) % 2][:, ct, 0:1],
                                                   scalar=w2, in1=buf[:, IT - 1:IT],
                                                   op0=ALU.mult, op1=ALU.add)
                        if k > 0:
                            e.scalar_tensor_tensor(out=buf[:, 0:1], in0=hL[j % 2][:, ct:ct + 1],
                                                   scalar=w0, in1=buf[:, 0:1],
                                                   op0=ALU.mult, op1=ALU.add)
                        if ct % 4 == 3:
                            s_v.inc(e.drain())
                blk.vector(f_vconv)

                def f_asilu(e, j=j, vb_cs=vb_cs):
                    for ct in range(FT):
                        e.wait_ge(s_v.h, vb_cs + ct // 4 + 1)
                        s_a.inc(e.activation(out=hb[j % 2][:, ct, :], in_=ctmp[ct % 4][:],
                                             func=AF.Silu, bias=cbias_sb[:, ct:ct + 1]))
                blk.scalar(f_asilu)
                s_v.v += 0

                if KPOST < 2:
                    return
                ab = s_a.v
                def f_v(e, j=j, ab=ab):
                    e.wait_ge(s_a.h, ab)
                    hbr = hb[j % 2][:].rearrange("p ct t -> p t ct")
                    mr = mtree[:].rearrange("p (t o) -> p t o", o=1)
                    s_v.inc(e.reduce_max(out=mr, in_=hbr, axis=AX.X,
                                         apply_absolute_value=True))
                blk.vector(f_v)

                if KPOST < 3:
                    return
                tb = s_t.v
                def f_t(e, tb=tb, vb_cs=vb_cs):
                    for g in range(8):
                        e.wait_ge(s_v.h, vb_cs + g + 1)
                        for _ in range(10):
                            e.matmul(out=ps_fill[:, 0:128], lhsT=identf_sb[:],
                                     rhs=identf_sb[:], start=True, stop=True)
                    s_v.wait(e)
                    for tt in range(4):
                        if tt >= 2:
                            e.wait_ge(s_v.h, s_v.v + tt - 1)
                        s_t.inc(e.matmul(out=tr2_slot(tt),
                                         lhsT=mtree[:, 128 * tt:128 * (tt + 1)],
                                         rhs=identf_sb[:], start=True, stop=True))
                blk.tensor(f_t)

                def f_v(e, tb=tb):
                    for tt in range(4):
                        e.wait_ge(s_t.h, tb + tt + 1)
                        s_v.inc(e.reduce_max(out=m2c[:, tt:tt + 1],
                                             in_=tr2_slot(tt),
                                             axis=AX.X))
                    e.drain()
                    e.tensor_scalar(out=m2c[:], in0=m2c[:], scalar1=1e-5, scalar2=None, op0=ALU.max)
                    e.drain()
                    e.reciprocal(out=s2v[:], in_=m2c[:])
                    e.tensor_scalar_mul(out=c2v[:], in0=m2c[:], scalar1=1.0 / 127.0)
                    e.drain()
                    e.tensor_scalar_mul(out=s2v[:], in0=s2v[:], scalar1=127.0)
                    e.tensor_mul(out=c2v[:], in0=c2v[:], in1=mw2[:].broadcast_to((128, 4)))
                    e.drain()
                    s_v.inc(e.nop())
                blk.vector(f_v)

                tb = s_t.v
                def f_t(e, tb=tb):
                    s_v.wait(e)
                    for tt in range(4):
                        ins = e.matmul(out=ps_bc2[0:1, 128 * tt:128 * (tt + 1)],
                                       lhsT=s2v[:, tt:tt + 1], rhs=identf_sb[:],
                                       start=True, stop=True)
                    s_t.inc(ins)
                blk.tensor(f_t)

                def f_v(e, tb=tb):
                    e.wait_ge(s_t.h, tb + 1)
                    e.tensor_copy(out=s4blk[0:1, :], in_=ps_bc2[0:1, :])
                    s_v.inc(e.drain())
                blk.vector(f_v)

                tb = s_t.v
                def f_t(e, tb=tb):
                    s_v.wait(e)
                    s_t.inc(e.matmul(out=ps_bc2[:], lhsT=onesfr_sb[0:1, :], rhs=s4blk[0:1, :],
                                     start=True, stop=True))
                blk.tensor(f_t)

                if KPOST < 4:
                    return
                def f_v(e, tb=tb, j=j):
                    e.wait_ge(s_t.h, tb + 1)
                    e.tensor_copy(out=s2b[:], in_=ps_bc2[:])
                    for ct in range(FT):
                        buf = ctmp[ct % 2]
                        e.tensor_mul(out=buf[:], in0=hb[j % 2][:, ct, :], in1=s2b[:])
                        e.tensor_scalar(out=hb[j % 2][:, ct, :], in0=buf[:],
                                        scalar1=mvec[:], scalar2=mvec[:],
                                        op0=ALU.add, op1=ALU.subtract)
                    s_q.inc(e.drain())
                blk.vector(f_v)

                if KPOST < 5:
                    return
                TPN = 1 if KPOST == 5 else 2
                qv = s_q.v
                def f_sync(e, j=j, seq=seq, k=k):
                    for tt in range(min(2, 2 * TPN)):
                        gtt = 4 * j + tt
                        if gtt >= 2:
                            e.wait_ge(s_sce.h, gtt - 1)
                        row = 128 * gtt
                        s_scb.inc(e.dma_start(out=scr_pb[gtt % 2][:], in_=scr_att[row:row + 128]), 16)
                    for tp in range(TPN):
                        for cg in range(8):
                            gi = 16 * j + 8 * tp + cg
                            if gi >= 2:
                                e.wait_ge(s_t2.h, gi - 1)
                            s_w2.inc(e.dma_start(
                                out=w2g[gi % 2][:],
                                in_=scr_w2[4 * cg:4 * cg + 4].rearrange("cc p m -> p cc m")), 16)
                    for tt in range(2, 2 * TPN):
                        gtt = 4 * j + tt
                        e.wait_ge(s_sce.h, gtt - 1)
                        row = 128 * gtt
                        s_scb.inc(e.dma_start(out=scr_pb[gtt % 2][:], in_=scr_att[row:row + 128]), 16)
                blk.sync(f_sync)

                def f_t(e, j=j, qv=qv):
                    e.wait_ge(s_q.h, qv)
                    for tp in range(TPN):
                        if tp == 1:
                            e.wait_ge(s_v.h, s_v.v + 4)
                        for cg in range(8):
                            gi = 16 * j + 8 * tp + cg
                            e.wait_ge(s_w2.h, 16 * (gi + 1))
                            for c4 in range(4):
                                ct = 4 * cg + c4
                                for ti in range(2):
                                    tt = 2 * tp + ti
                                    for mh in range(2):
                                        ins = e.matmul(
                                            out=ps_mm[:, 1024 * ti + 512 * mh:1024 * ti + 512 * (mh + 1)],
                                            lhsT=hb[j % 2][:, ct, 128 * tt:128 * (tt + 1)],
                                            rhs=w2g[gi % 2][:, c4, 512 * mh:512 * (mh + 1)],
                                            start=(cg == 0 and c4 == 0),
                                            stop=(cg == 7 and c4 == 3))
                            s_t2.inc(ins)
                blk.tensor(f_t)

                t2b = s_t2.v - 8 * TPN
                def f_v(e, j=j, t2b=t2b):
                    for tp in range(TPN):
                        e.wait_ge(s_t2.h, t2b + 8 * (tp + 1))
                        for ti in range(2):
                            tt = 2 * tp + ti
                            gtt = 4 * j + tt
                            e.wait_ge(s_scb.h, 16 * (gtt + 1))
                            if gtt >= 2:
                                e.wait_ge(s_o.h, 16 * (32 + gtt - 1))
                            for mh in range(2):
                                osl = outsb[gtt % 2][:, 512 * mh:512 * (mh + 1)]
                                psl = ps_mm[:, 1024 * ti + 512 * mh:1024 * ti + 512 * (mh + 1)]
                                ssl = scr_pb[gtt % 2][:, 512 * mh:512 * (mh + 1)]
                                if KEVAC == "scr":
                                    s_v.inc(e.tensor_copy(out=osl, in_=ssl))
                                elif KEVAC == "mm2":
                                    s_v.inc(e.tensor_scalar_mul(out=osl, in0=psl,
                                                                scalar1=c2v[:, tt:tt + 1]))
                                else:
                                    s_v.inc(e.scalar_tensor_tensor(
                                        out=osl, in0=psl, scalar=c2v[:, tt:tt + 1],
                                        in1=ssl, op0=ALU.mult, op1=ALU.add))
                            s_sce.inc(e.drain())
                blk.vector(f_v)

                vb = s_v.v - 4 * TPN
                def f_sync2(e, j=j, vb=vb):
                    for tt in range(2 * TPN):
                        gtt = 4 * j + tt
                        e.wait_ge(s_v.h, vb + 2 * (tt + 1))
                        row = 128 * gtt
                        s_o.inc(e.dma_start(out=out_h[row:row + 128], in_=outsb[gtt % 2][:]), 16)
                blk.sync(f_sync2)

        h_done = {}
        if KSTAGE == 4:
            h_done[0] = emit_h(0)
        elif KSTAGE == 5:
            h_done[0] = emit_h(0)
            h_done[1] = emit_h(1)
            emit_post(0, h_done[0], h_done[1])
        else:
            for seq in range(2):
                for k in range(NIT):
                    it = NIT * seq + k
                    h_done[it] = emit_h(it)
                    if k >= 1:
                        emit_post(it - 1, h_done[it - 1], h_done[it])
                emit_post(NIT * seq + NIT - 1, h_done[NIT * seq + NIT - 1], None)

        es_b.close()
    else:
        with nc.Block() as blk:
            blk.gpsimd(lambda e: e.wait_ge(s_o.h, s_o.v))

    kdump = os.environ.get("KDUMP", "")
    if kdump:
        dbg = {}
        for nm in kdump.split(","):
            t = dbg_tensors[nm]
            shp = [t.shape[0], int(np.prod(t.shape[1:]))] if len(t.shape) > 2 else list(t.shape)
            dt = t[:].dtype if hasattr(t[:], 'dtype') else F32
            dbg[nm] = nc.dram_tensor("dbg_" + nm, shp, dt, kind="ExternalOutput").ap()
        with nc.Block() as blk:
            def f_g(e):
                e.wait_ge(s_o.h, s_o.v)
                for nm in kdump.split(","):
                    t = dbg_tensors[nm]
                    s_gd.inc(e.dma_start(out=dbg[nm], in_=t[:].rearrange("p ... -> p (...)") if len(t.shape) > 2 else t[:]), 16)
            blk.gpsimd(f_g)

    with nc.Block() as blk:
        def f_g(e):
            e.wait_ge(s_o.h, s_o.v)
            e.wait_ge(s_gd.h, s_gd.v)
        blk.gpsimd(f_g)

    esg.close()
    return nc


_NC_CACHE = None


def kernel(**inputs):
    global _NC_CACHE
    x = np.ascontiguousarray(np.asarray(inputs["L0_attn_x"], np.float32))
    Wq = np.asarray(inputs["L0_attn_Wq"], np.float32)
    Wk = np.asarray(inputs["L0_attn_Wk"], np.float32)
    Wv = np.asarray(inputs["L0_attn_Wv"], np.float32)
    Wo = np.asarray(inputs["L0_attn_Wo"], np.float32)
    w1 = np.asarray(inputs["L0_ffn_w1"], np.float32)
    cw = np.asarray(inputs["L0_ffn_conv_w"], np.float32)
    cb = np.asarray(inputs["L0_ffn_conv_b"], np.float32)
    w2 = np.asarray(inputs["L0_ffn_w2"], np.float32)

    wcat = np.concatenate([Wq, Wk, Wv], axis=0)
    wcatT = np.ascontiguousarray(
        wcat.T.reshape(8, 128, 1536).transpose(1, 0, 2).reshape(128, 8 * 1536))
    woT = np.ascontiguousarray(
        Wo.T.reshape(8, 128, D).transpose(1, 0, 2).reshape(128, 8 * D))
    w1T = np.ascontiguousarray(w1.T.reshape(8, 128, DH))
    w2T = np.ascontiguousarray(w2.T.reshape(32, 128, D))
    cwt = np.ascontiguousarray(
        cw[:, 0, :].reshape(FT, 128, 3).transpose(1, 0, 2).reshape(128, FT * 3))
    cbias = np.ascontiguousarray(cb.reshape(FT, 128).T)

    consts = host_constants()

    if _NC_CACHE is None:
        _NC_CACHE = build_nc()
    nc = _NC_CACHE

    common = dict(wcatT=wcatT, woT=woT, w1T=w1T, w2T=w2T, cwt=cwt, cbias=cbias,
                  **consts)
    xr = x.reshape(B, S, D)
    in_maps = []
    for i in range(NCORE):
        m = dict(common)
        m["x"] = np.ascontiguousarray(xr[2 * i:2 * i + 2].reshape(TOK, D))
        in_maps.append(m)

    res = run_bass_kernel_spmd(nc, in_maps, core_ids=list(range(NCORE)))
    kernel._LAST = res.results
    out = np.concatenate([r["out"].reshape(2, S, D) for r in res.results], axis=0)
    return out


# version 30
# speedup vs baseline: 1.0403x; 1.0004x over previous
import numpy as np
import ml_dtypes
from contextlib import ExitStack

import concourse.bass as bass
import concourse.mybir as mybir
from concourse.bass_utils import run_bass_kernel_spmd

F32 = mybir.dt.float32
BF16 = mybir.dt.bfloat16
AF = mybir.ActivationFunctionType
ALU = mybir.AluOpType
AX = mybir.AxisListType

B, S, D = 16, 2048, 1024
NH, NKV, HD = 16, 4, 64
CHUNK = 128
NCHUNK = S // CHUNK
THETA = 10000.0
DH = 4096
FT = DH // 128
NCORE = 8
TOK = (B // NCORE) * S
EPS = 1e-5
IT = 512
NIT = S // IT

MAGIC23 = float(1.5 * 2.0 ** 23)
MAGIC_HI = float(np.float32(1.5 * 2.0 ** 52 - 1.5 * 2.0 ** 23))

PHASE_B = True
import os
KSTAGE = int(os.environ.get("KSTAGE", "99"))
KSUB = int(os.environ.get("KSUB", "99"))
KPOST = int(os.environ.get("KPOST", "99"))
KEVAC = os.environ.get("KEVAC", "")


def host_constants():
    import jax
    import jax.numpy as jnp

    with jax.default_device(jax.devices("cpu")[0]):
        slopes = jnp.asarray([2.0 ** (-8.0 / NH * (i + 1)) for i in range(NH)],
                             dtype=jnp.float32)
        tj = jnp.arange(CHUNK, dtype=jnp.float32)
        sl = slopes[:, None, None]
        mask = jnp.where(tj[:, None] >= tj[None, :],
                         jnp.exp(-sl * (tj[:, None] - tj[None, :])), 0.0)
        qdec = jnp.exp(-slopes[:, None] * (tj + 1.0))
        kdec_j = jnp.exp(-slopes[:, None] * (CHUNK - 1.0 - tj))
        bd_j = jnp.exp(-slopes * CHUNK)
        mask = np.asarray(mask)
        qdec = np.asarray(qdec)
        kdec_j = np.asarray(kdec_j)
        bd_j = np.asarray(bd_j)

    t = np.arange(S, dtype=np.float64)
    inv = 1.0 / THETA ** (np.arange(0, HD, 2, dtype=np.float64) / HD)
    freqs = np.outer(t, inv)
    emb = np.concatenate([freqs, freqs], axis=-1)
    cos32 = np.cos(emb).astype(np.float32)
    sin32 = np.sin(emb).astype(np.float32)
    cos_r = np.ascontiguousarray(cos32.reshape(NCHUNK, 128, HD).transpose(1, 0, 2))
    sin_pm = sin32.copy()
    sin_pm[:, :HD // 2] = -sin_pm[:, :HD // 2]
    sin_r = np.ascontiguousarray(sin_pm.reshape(NCHUNK, 128, HD).transpose(1, 0, 2))

    maskT = np.ascontiguousarray(mask.transpose(2, 0, 1))

    qdec_tile = np.empty((128, NH // 2, CHUNK), np.float32)
    for j in range(NH // 2):
        qdec_tile[:64, j, :] = qdec[2 * j]
        qdec_tile[64:, j, :] = qdec[2 * j + 1]

    kdec = np.ascontiguousarray(kdec_j.T)

    bdvec = np.empty((128, NH // 2), np.float32)
    for j in range(NH // 2):
        bdvec[:64, j] = bd_j[2 * j]
        bdvec[64:, j] = bd_j[2 * j + 1]

    return dict(
        cosr=cos_r.reshape(128, NCHUNK * HD),
        sinr=sin_r.reshape(128, NCHUNK * HD),
        maskT=maskT.reshape(128, NH * CHUNK),
        qdec=qdec_tile.reshape(128, (NH // 2) * CHUNK),
        kdec=kdec, bdvec=bdvec,
        identb=np.eye(128, dtype=ml_dtypes.bfloat16),
        identf=np.eye(128, dtype=np.float32),
        onesb=np.ones((4, 128), dtype=ml_dtypes.bfloat16),
        onesfr=np.ones((4, 128), dtype=np.float32),
        onesfc=np.ones((128, 1), dtype=np.float32),
    )


def build_nc():
    nc = bass.Bass("TRN2", target_bir_lowering=False, debug=False,
                   detect_race_conditions=False)
    di = lambda n, shp, dt=F32: nc.dram_tensor(n, shp, dt, kind="ExternalInput").ap()

    x_h = di("x", [TOK, D])
    wcat_h = di("wcatT", [128, 8 * 1536])
    woT_h = di("woT", [128, 8 * D])
    w1T_h = di("w1T", [8, 128, DH])
    w2T_h = di("w2T", [32, 128, D])
    cwt_h = di("cwt", [128, FT * 3])
    cbias_h = di("cbias", [128, FT])
    cos_h = di("cosr", [128, NCHUNK * HD])
    sin_h = di("sinr", [128, NCHUNK * HD])
    maskT_h = di("maskT", [128, NH * CHUNK])
    qdec_h = di("qdec", [128, (NH // 2) * CHUNK])
    kdec_h = di("kdec", [128, NH])
    bdvec_h = di("bdvec", [128, NH // 2])
    identb_h = di("identb", [128, 128], BF16)
    identf_h = di("identf", [128, 128])
    onesb_h = di("onesb", [4, 128], BF16)
    onesfr_h = di("onesfr", [4, 128])
    onesfc_h = di("onesfc", [128, 1])

    out_h = nc.dram_tensor("out", [TOK, D], F32, kind="ExternalOutput").ap()
    scr_att = nc.dram_tensor("scr_att", [TOK, D], F32, kind="Internal").ap()
    scr_w1 = nc.dram_tensor("scr_w1", [8, 8, 128, 512], BF16, kind="Internal").ap()
    scr_w2 = nc.dram_tensor("scr_w2", [32, 128, D], BF16, kind="Internal").ap()

    esg = ExitStack()
    SB = lambda es, n, shp, dt=F32: es.enter_context(nc.sbuf_tensor(n, shp, dt))
    PSUM = lambda es, n, shp: es.enter_context(nc.psum_tensor(n, shp, F32))

    class Sem:
        __slots__ = ("h", "v")
        def __init__(self, name):
            self.h = esg.enter_context(nc.semaphore(name))
            self.v = 0
        def inc(self, instr, n=1):
            instr.then_inc(self.h, n)
            self.v += n
        def wait(self, e, v=None):
            e.wait_ge(self.h, self.v if v is None else v)

    mvec = SB(esg, "mvec", [128, 1])
    negM = SB(esg, "negM", [128, 1])
    actwarm = SB(esg, "actwarm", [128, 1])
    cwt_sb = SB(esg, "cwt_sb", [128, FT * 3])
    cbias_sb = SB(esg, "cbias_sb", [128, FT])
    identb_sb = SB(esg, "identb_sb", [128, 128], BF16)
    identf_sb = SB(esg, "identf_sb", [128, 128])
    onesb_sb = SB(esg, "onesb_sb", [4, 128], BF16)
    onesfr_sb = SB(esg, "onesfr_sb", [4, 128])
    onesfc_sb = SB(esg, "onesfc_sb", [128, 1])
    rall = SB(esg, "rall", [128, 2 * NCHUNK])
    mall = SB(esg, "mall", [128, 2 * NCHUNK])
    shist = SB(esg, "shist", [128, 2 * NCHUNK])
    vbat = SB(esg, "vbat", [128, 2 * NCHUNK])
    ball = SB(esg, "ball", [128, 2 * NCHUNK])
    mw1 = SB(esg, "mw1", [128, 1])
    mw2 = SB(esg, "mw2", [128, 1])
    sw12 = SB(esg, "sw12", [128, 2])
    wcat_sb = SB(esg, "wcat_sb", [128, 8 * 1536], BF16)
    woT_sb = SB(esg, "woT_sb", [128, 8 * D], BF16)

    s_c = Sem("s_c"); s_x = Sem("s_x"); s_w = Sem("s_w"); s_gd = Sem("s_gd")
    s_v = Sem("s_v"); s_a = Sem("s_a"); s_t = Sem("s_t"); s_o = Sem("s_o")
    s_ts = Sem("s_ts"); s_tk = Sem("s_tk"); s_m = Sem("s_m"); s_k = Sem("s_k")
    s_e = Sem("s_e"); s_q = Sem("s_q"); s_we = Sem("s_we"); s_wo = Sem("s_wo")
    s_xb = Sem("s_xb"); s_scb = Sem("s_scb"); s_w1 = Sem("s_w1"); s_w2 = Sem("s_w2")
    s_t1 = Sem("s_t1"); s_t2 = Sem("s_t2"); s_xc = Sem("s_xc"); s_sce = Sem("s_sce")
    s_k2 = Sem("s_k2")

    with nc.Block() as blk:
        def f_sync(e):
            for dst, src in ((cwt_sb, cwt_h), (cbias_sb, cbias_h), (identb_sb, identb_h),
                             (identf_sb, identf_h), (onesb_sb, onesb_h),
                             (onesfr_sb, onesfr_h), (onesfc_sb, onesfc_h)):
                s_c.inc(e.dma_start(out=dst[:], in_=src), 16)
        blk.sync(f_sync)

        def f_a_warm(e):
            s_c.wait(e)
            e.activation(out=actwarm[:], in_=onesfc_sb[:], func=AF.Ln)
            e.drain()
            e.activation(out=actwarm[:], in_=actwarm[:], func=AF.Exp)
            e.drain()
        blk.scalar(f_a_warm)

        def f_v(e):
            s_c.wait(e)
            e.memset(mvec[:], 0.25)
            e.drain()
            e.tensor_scalar(out=mvec[:], in0=mvec[:], scalar1=MAGIC23, scalar2=MAGIC23,
                            op0=ALU.add, op1=ALU.subtract)
            e.drain()
            e.tensor_scalar(out=mvec[:], in0=mvec[:], scalar1=0.25, scalar2=None, op0=ALU.is_equal)
            e.drain()
            e.tensor_scalar(out=mvec[:], in0=mvec[:], scalar1=MAGIC_HI, scalar2=MAGIC23,
                            op0=ALU.mult, op1=ALU.add)
            e.drain()
            s_v.inc(e.tensor_scalar_mul(out=negM[:], in0=mvec[:], scalar1=-1.0))
        blk.vector(f_v)

    es_p = ExitStack()
    wtile = [SB(es_p, f"wtile{i}", [128, DH]) for i in range(2)]
    wqb = [SB(es_p, f"wqb{i}", [128, DH], BF16) for i in range(2)]
    wsum1 = SB(es_p, "wsum1", [128, 1])
    wsum2 = SB(es_p, "wsum2", [128, 1])
    rtmp = SB(es_p, "rtmp", [128, 1])
    sc2 = SB(es_p, "sc2", [1, 2])
    wttmp = SB(es_p, "wttmp", [128, DH])
    ps_pp = PSUM(es_p, "ps_pp", [128, 16])

    NWT = 40
    def w_src(i):
        return (w1T_h[i], DH) if i < 8 else (w2T_h[i - 8], D)

    with nc.Block() as blk:
        v0 = s_v.v
        def f_sync(e):
            for i in range(NWT):
                if i >= 2:
                    e.wait_ge(s_v.h, v0 + i - 1)
                src, flen = w_src(i)
                s_w.inc(e.dma_start(out=wtile[i % 2][:, 0:flen], in_=src), 16)
        blk.sync(f_sync)

        def f_v(e):
            e.memset(wsum1[:], 0.0)
            e.memset(wsum2[:], 0.0)
            for i in range(NWT):
                _, flen = w_src(i)
                e.wait_ge(s_w.h, 16 * (i + 1))
                e.reduce_sum(out=rtmp[:], in_=wtile[i % 2][:, 0:flen], axis=AX.X,
                             apply_absolute_value=True)
                e.drain()
                acc = wsum1 if i < 8 else wsum2
                s_v.inc(e.tensor_add(out=acc[:], in0=acc[:], in1=rtmp[:]))
            s_v.inc(e.drain())
        blk.vector(f_v)

        def f_t(e):
            s_v.wait(e)
            e.matmul(out=ps_pp[0:1, 0:1], lhsT=wsum1[:], rhs=onesfc_sb[:], start=True, stop=True)
            s_t.inc(e.matmul(out=ps_pp[0:1, 1:2], lhsT=wsum2[:], rhs=onesfc_sb[:],
                             start=True, stop=True))
        blk.tensor(f_t)

        def f_v2(e):
            s_t.wait(e)
            e.tensor_scalar(out=sc2[0:1, 0:1], in0=ps_pp[0:1, 0:1],
                            scalar1=1.0 / (DH * D), scalar2=EPS, op0=ALU.mult, op1=ALU.max)
            e.tensor_scalar(out=sc2[0:1, 1:2], in0=ps_pp[0:1, 1:2],
                            scalar1=1.0 / (DH * D), scalar2=EPS, op0=ALU.mult, op1=ALU.max)
            s_v.inc(e.drain())
        blk.vector(f_v2)

        def f_t2(e):
            s_v.wait(e)
            s_t.inc(e.matmul(out=ps_pp[:, 2:4], lhsT=onesfr_sb[0:1, :], rhs=sc2[0:1, :],
                             start=True, stop=True))
        blk.tensor(f_t2)

        def f_v3(e):
            s_t.wait(e)
            e.tensor_copy(out=mw1[:], in_=ps_pp[:, 2:3])
            e.tensor_copy(out=mw2[:], in_=ps_pp[:, 3:4])
            e.drain()
            e.reciprocal(out=sw12[:, 0:1], in_=mw1[:])
            e.reciprocal(out=sw12[:, 1:2], in_=mw2[:])
            s_v.inc(e.drain())
        blk.vector(f_v3)

    with nc.Block() as blk:
        v0 = s_v.v
        g0 = s_gd.v
        def f_sync(e):
            s_v.wait(e)
            for i in range(NWT):
                if i >= 2:
                    e.wait_ge(s_v.h, v0 + i - 1)
                src, flen = w_src(i)
                s_w.inc(e.dma_start(out=wtile[i % 2][:, 0:flen], in_=src), 16)
        blk.sync(f_sync)

        w0 = s_w.v - 16 * NWT
        def f_v(e):
            for i in range(NWT):
                _, flen = w_src(i)
                col = 0 if i < 8 else 1
                e.wait_ge(s_w.h, w0 + 16 * (i + 1))
                if i >= 2:
                    e.wait_ge(s_gd.h, g0 + 16 * (i - 1))
                sl = (slice(0, 128), slice(0, flen))
                e.tensor_scalar(out=wttmp[sl], in0=wtile[i % 2][sl], scalar1=sw12[:, col:col + 1],
                                scalar2=mvec[:], op0=ALU.mult, op1=ALU.add)
                e.tensor_scalar(out=wttmp[sl], in0=wttmp[sl], scalar1=mvec[:],
                                scalar2=1.0, op0=ALU.subtract, op1=ALU.min)
                s_v.inc(e.tensor_scalar(out=wqb[i % 2][sl], in0=wttmp[sl],
                                        scalar1=-1.0, scalar2=None, op0=ALU.max))
        blk.vector(f_v)

        def f_g(e):
            for i in range(NWT):
                e.wait_ge(s_v.h, v0 + i + 1)
                if i < 8:
                    dst_ap = scr_w1[:, i].rearrange("g p f -> p g f")
                    s_gd.inc(e.dma_start(out=dst_ap, in_=wqb[i % 2][:]), 16)
                else:
                    s_gd.inc(e.dma_start(out=scr_w2[i - 8], in_=wqb[i % 2][:, 0:D]), 16)
        blk.gpsimd(f_g)

    p0tmp = SB(es_p, "p0tmp", [128, D])
    with nc.Block() as blk:
        w0b = s_w.v
        v0b = s_v.v
        a0b = s_a.v
        def f_sync(e, w0b=w0b, v0b=v0b, a0b=a0b):
            for c in range(2 * NCHUNK):
                if c >= 2:
                    e.wait_ge(s_v.h, v0b + c - 1)
                    e.wait_ge(s_a.h, a0b + c - 1)
                s_w.inc(e.dma_start(out=wtile[c % 2][:, 0:D], in_=x_h[128 * c:128 * (c + 1)]), 16)
        blk.sync(f_sync)

        def f_v(e, w0b=w0b):
            for c in range(2 * NCHUNK):
                e.wait_ge(s_w.h, w0b + 16 * (c + 1))
                s_v.inc(e.reduce_sum(out=mall[:, c:c + 1], in_=wtile[c % 2][:, 0:D], axis=AX.X))
        blk.vector(f_v)

        def f_a(e, w0b=w0b, v0b=v0b):
            for c in range(2 * NCHUNK):
                e.wait_ge(s_v.h, v0b + c + 1)
                s_a.inc(e.activation(out=p0tmp[:], in_=wtile[c % 2][:, 0:D], func=AF.Square,
                                     accum_out=shist[:, c:c + 1]))
            s_a.inc(e.drain())
        blk.scalar(f_a)

        def f_v2(e):
            s_a.wait(e)
            e.drain()
            e.tensor_scalar_mul(out=mall[:], in0=mall[:], scalar1=1.0 / D)
            e.drain()
            e.tensor_mul(out=vbat[:], in0=mall[:], in1=mall[:])
            e.tensor_scalar(out=shist[:], in0=shist[:], scalar1=1.0 / D, scalar2=EPS,
                            op0=ALU.mult, op1=ALU.add)
            e.drain()
            e.tensor_sub(out=vbat[:], in0=shist[:], in1=vbat[:])
            s_v.inc(e.drain())
        blk.vector(f_v2)

        def f_a2(e):
            s_v.wait(e)
            e.activation(out=vbat[:], in_=vbat[:], func=AF.Ln)
            e.drain()
            e.activation(out=rall[:], in_=vbat[:], func=AF.Exp, scale=-0.5)
            s_a.inc(e.drain())
        blk.scalar(f_a2)

        def f_v3(e):
            s_a.wait(e)
            e.tensor_mul(out=ball[:], in0=mall[:], in1=rall[:])
            e.drain()
            e.tensor_scalar_mul(out=ball[:], in0=ball[:], scalar1=-1.0)
            s_v.inc(e.drain())
        blk.vector(f_v3)

    with nc.Block() as blk:
        def f_g(e):
            s_gd.inc(e.dma_start(out=wcat_sb[:], in_=wcat_h), 16)
            s_gd.inc(e.dma_start(out=woT_sb[:], in_=woT_h), 16)
        blk.gpsimd(f_g)

    es_p.close()
    es_a = ExitStack()
    xs = [SB(es_a, f"xs{i}", [128, D]) for i in range(2)]
    att = [SB(es_a, f"att{i}", [128, D]) for i in range(2)]
    xsum = SB(es_a, "xsum", [128, 1])
    sqsum = SB(es_a, "sqsum", [128, 1])
    mcol = SB(es_a, "mcol", [128, 1])
    msq = SB(es_a, "msq", [128, 1])
    vtmp = SB(es_a, "vtmp", [128, 1])
    ltmp = SB(es_a, "ltmp", [128, 1])
    btmp = SB(es_a, "btmp", [128, 1])
    xn_bf = SB(es_a, "xn_bf", [128, D], BF16)
    xnT = SB(es_a, "xnT", [128, 8, 128], BF16)
    tmpa = SB(es_a, "tmpa", [128, D])
    tmpb = SB(es_a, "tmpb", [128, D])
    q_bf = SB(es_a, "q_bf", [128, D], BF16)
    k_bf = SB(es_a, "k_bf", [128, 256], BF16)
    v_bf = SB(es_a, "v_bf", [128, 256], BF16)
    qT = SB(es_a, "qT", [128, 8, 128], BF16)
    qdq = SB(es_a, "qdq", [128, 8, 128], BF16)
    kTba = SB(es_a, "kTba", [128, 4, 128], BF16)
    kdk = SB(es_a, "kdk", [128, NH * 64], BF16)
    sc_bf = SB(es_a, "sc_bf", [128, 4 * 128], BF16)
    oT = SB(es_a, "oT", [128, 8, 128], BF16)
    kv_sb = [SB(es_a, f"kv_sb{i}", [128, 8, 64]) for i in range(2)]
    kv_bf = SB(es_a, "kv_bf", [128, 8, 64], BF16)
    cos_sb = SB(es_a, "cos_sb", [128, NCHUNK * HD])
    sin_sb = SB(es_a, "sin_sb", [128, NCHUNK * HD])
    maskT_sb = SB(es_a, "maskT_sb", [128, NH * CHUNK])
    qdec_sb = SB(es_a, "qdec_sb", [128, (NH // 2) * CHUNK])
    kdec_sb = SB(es_a, "kdec_sb", [128, NH])
    bdvec_sb = SB(es_a, "bdvec_sb", [128, NH // 2])

    dbg_tensors = dict(xn_bf=xn_bf, q_bf=q_bf, k_bf=k_bf, v_bf=v_bf, qT=qT,
                       kTba=kTba, qdq=qdq, kdk=kdk, sc_bf=sc_bf, oT=oT,
                       kv_sb0=kv_sb[0], kv_bf=kv_bf, rall=rall, ball=ball,
                       mvec=mvec, negM=negM, wcat_sb=wcat_sb, woT_sb=woT_sb,
                       mw1=mw1, mw2=mw2, sw12=sw12, xnT=xnT)
    ps_qkv = PSUM(es_a, "ps_qkv", [128, 1536])
    ps_sctr = PSUM(es_a, "ps_sctr", [128, 1024])
    ps_o = PSUM(es_a, "ps_o", [128, 1024])
    ps_kao = PSUM(es_a, "ps_kao", [128, 512])

    def tr_slot(j):
        return ps_sctr[:, 512 * (j % 2):512 * (j % 2) + 128]

    def sc_slot(h):
        base = 512 * (h % 2) + 128 * ((h // 2) % 2)
        return ps_sctr[:, base:base + 128]

    def o_slot(j):
        base = 512 * (j % 2) + 128 * ((j // 2) % 2)
        return ps_o[:, base:base + 128]

    with nc.Block() as blk:
        def f_sync(e):
            e.wait_ge(s_gd.h, s_gd.v)
            for dst, src in ((cos_sb, cos_h), (sin_sb, sin_h), (maskT_sb, maskT_h),
                             (qdec_sb, qdec_h), (kdec_sb, kdec_h), (bdvec_sb, bdvec_h)):
                s_c.inc(e.dma_start(out=dst[:], in_=src), 16)
            s_x.inc(e.dma_start(out=xs[0][:], in_=x_h[0:128]), 16)
        blk.sync(f_sync)

    ln_ready = s_v.v
    rope_done = {}
    attst_target = scr_att if (PHASE_B and KSTAGE >= 4) else out_h

    n_chunks = 0 if KSTAGE < 2 else (1 if KSTAGE == 2 else 2 * NCHUNK)
    for c in range(n_chunks):
        seq, cc = divmod(c, NCHUNK)
        col = slice(c, c + 1)
        with nc.Block() as blk:
            if c + 1 < 2 * NCHUNK:
                def f_sync(e, c=c):
                    if c + 1 >= 2:
                        e.wait_ge(s_we.h, 2 * c)
                    s_x.inc(e.dma_start(out=xs[(c + 1) % 2][:],
                                        in_=x_h[128 * (c + 1):128 * (c + 2)]), 16)
                blk.sync(f_sync)

            def f_a(e, c=c, col=col, lnv=ln_ready):
                e.wait_ge(s_x.h, 16 * (c + 1))
                if c == 0:
                    e.wait_ge(s_v.h, lnv)
                s_a.inc(e.activation(out=xn_bf[:], in_=xs[c % 2][:], func=AF.Identity,
                                     scale=rall[:, col], bias=ball[:, col]))
            blk.scalar(f_a)

            if KSUB < 2:
                continue
            tb = s_t.v; ab = s_a.v
            def f_t(e, tb=tb, ab=ab, c=c):
                s_a.wait(e)
                if c == 0:
                    s_c.wait(e)
                else:
                    e.wait_ge(s_m.h, 16 * c)
                for j in range(8):
                    if j >= 2:
                        e.wait_ge(s_a.h, ab + j - 1)
                    s_t.inc(e.matmul(out=tr_slot(j),
                                     lhsT=xn_bf[:, 128 * j:128 * (j + 1)], rhs=identb_sb[:],
                                     start=True, stop=True))
            blk.tensor(f_t)

            def f_a(e, tb=tb):
                for j in range(8):
                    e.wait_ge(s_t.h, tb + j + 1)
                    s_a.inc(e.activation(out=xnT[:, j, :], in_=tr_slot(j), func=AF.Identity))
            blk.scalar(f_a)

            if KSUB < 3:
                continue
            ab = s_a.v
            def f_t(e, c=c, ab=ab):
                e.wait_ge(s_a.h, ab)
                e.wait_ge(s_gd.h, 16 * 42)
                if c > 0:
                    e.wait_ge(s_v.h, rope_done[c - 1])
                for dc in range(8):
                    for b3 in range(3):
                        ins = e.matmul(out=ps_qkv[:, 512 * b3:512 * (b3 + 1)],
                                       lhsT=xnT[:, dc, :],
                                       rhs=wcat_sb[:, 1536 * dc + 512 * b3:1536 * dc + 512 * (b3 + 1)],
                                       start=(dc == 0), stop=(dc == 7))
                s_t.inc(ins)
            blk.tensor(f_t)

            def f_v(e, c=c, cc=cc):
                s_t.wait(e)
                if c == 0:
                    s_c.wait(e)
                csl = cos_sb[:, HD * cc:HD * (cc + 1)].rearrange("p (o d) -> p o d", o=1)
                ssl = sin_sb[:, HD * cc:HD * (cc + 1)].rearrange("p (o d) -> p o d", o=1)
                qv = ps_qkv[:, 0:D].rearrange("p (h d) -> p h d", h=NH)
                ta = tmpa[:].rearrange("p (h d) -> p h d", h=NH)
                tb_ = tmpb[:].rearrange("p (h d) -> p h d", h=NH)
                e.tensor_mul(out=ta, in0=qv, in1=csl.broadcast_to((128, NH, HD)))
                e.tensor_mul(out=tb_[:, :, 0:32], in0=qv[:, :, 32:64],
                             in1=ssl[:, :, 0:32].broadcast_to((128, NH, 32)))
                e.tensor_mul(out=tb_[:, :, 32:64], in0=qv[:, :, 0:32],
                             in1=ssl[:, :, 32:64].broadcast_to((128, NH, 32)))
                e.tensor_add(out=q_bf[:], in0=tmpa[:], in1=tmpb[:])
                kv_ = ps_qkv[:, D:D + 256].rearrange("p (h d) -> p h d", h=NKV)
                ta4 = tmpa[:, 0:256].rearrange("p (h d) -> p h d", h=NKV)
                tb4 = tmpb[:, 0:256].rearrange("p (h d) -> p h d", h=NKV)
                e.tensor_mul(out=ta4, in0=kv_, in1=csl.broadcast_to((128, NKV, HD)))
                e.tensor_mul(out=tb4[:, :, 0:32], in0=kv_[:, :, 32:64],
                             in1=ssl[:, :, 0:32].broadcast_to((128, NKV, 32)))
                e.tensor_mul(out=tb4[:, :, 32:64], in0=kv_[:, :, 0:32],
                             in1=ssl[:, :, 32:64].broadcast_to((128, NKV, 32)))
                e.tensor_add(out=k_bf[:], in0=tmpa[:, 0:256], in1=tmpb[:, 0:256])
                e.tensor_copy(out=v_bf[:], in_=ps_qkv[:, D + 256:D + 512])
                for hh in range(NH):
                    g = hh // 4
                    ins = e.tensor_scalar_mul(out=kdk[:, 64 * hh:64 * (hh + 1)],
                                              in0=k_bf[:, 64 * g:64 * (g + 1)],
                                              scalar1=kdec_sb[:, hh:hh + 1])
                s_v.inc(ins)
                rope_done[c] = s_v.v
            blk.vector(f_v)

            if KSUB < 4:
                continue
            tb = s_t.v; ab = s_a.v
            def f_t(e, tb=tb, ab=ab, c=c):
                e.wait_ge(s_v.h, rope_done[c])
                for j in range(8):
                    if j >= 2:
                        e.wait_ge(s_a.h, ab + j - 1)
                    s_t.inc(e.matmul(out=tr_slot(j),
                                     lhsT=q_bf[:, 128 * j:128 * (j + 1)], rhs=identb_sb[:],
                                     start=True, stop=True))
                for g in range(4):
                    jj = 8 + g
                    e.wait_ge(s_a.h, ab + jj - 1)
                    sl = tr_slot(jj)
                    e.matmul(out=sl[0:64, :], lhsT=k_bf[:, 64 * g:64 * (g + 1)],
                             rhs=identb_sb[:], start=True, stop=True)
                    s_t.inc(e.matmul(out=sl[64:128, :], lhsT=k_bf[:, 64 * g:64 * (g + 1)],
                                     rhs=identb_sb[:], start=True, stop=True))
            blk.tensor(f_t)

            def f_a(e, tb=tb):
                for j in range(8):
                    e.wait_ge(s_t.h, tb + j + 1)
                    s_a.inc(e.activation(out=qT[:, j, :], in_=tr_slot(j), func=AF.Identity))
                for g in range(4):
                    jj = 8 + g
                    e.wait_ge(s_t.h, tb + jj + 1)
                    s_a.inc(e.activation(out=kTba[:, g, :], in_=tr_slot(jj), func=AF.Identity))
            blk.scalar(f_a)

            ab = s_a.v
            def f_v(e, ab=ab, c=c):
                e.wait_ge(s_a.h, ab - 4)
                if c == 0:
                    s_c.wait(e)
                for j in range(8):
                    ins = e.tensor_mul(out=qdq[:, j, :], in0=qT[:, j, :],
                                       in1=qdec_sb[:, 128 * j:128 * (j + 1)])
                s_q.inc(ins)
            blk.vector(f_v)

            if KSUB < 5:
                continue
            TSb = s_ts.v; TKb = s_tk.v; Mb = s_m.v; Kb = s_k.v; Eb = s_e.v
            Ab = s_a.v
            def f_t(e, c=c, cc=cc, TSb=TSb, TKb=TKb, Mb=Mb, Kb=Kb, Eb=Eb, Ab=Ab):
                e.wait_ge(s_a.h, Ab)
                e.wait_ge(s_q.h, s_q.v)
                if c > 0:
                    e.wait_ge(s_we.h, 2 * c)
                for hh in range(NH):
                    g = hh // 4
                    p64 = 64 * (hh % 2)
                    j = hh // 2
                    slot = hh % 4
                    if hh >= 2:
                        e.wait_ge(s_m.h, Mb + hh - 1)
                    s_ts.inc(e.matmul(out=sc_slot(hh),
                                      lhsT=kTba[p64:p64 + 64, g, :],
                                      rhs=qT[p64:p64 + 64, hh // 2, :],
                                      start=True, stop=True))
                    e.wait_ge(s_m.h, Mb + hh + 1)
                    if j >= 2 and hh % 2 == 0:
                        e.wait_ge(s_e.h, Eb + j - 1)
                    first = (cc == 0)
                    e.matmul(out=o_slot(j)[p64:p64 + 64, :],
                             lhsT=v_bf[:, 64 * g:64 * (g + 1)],
                             rhs=sc_bf[:, 128 * slot:128 * (slot + 1)],
                             start=True, stop=first)
                    if not first:
                        e.wait_ge(s_k2.h, c)
                        e.matmul(out=o_slot(j)[p64:p64 + 64, :],
                                 lhsT=kv_bf[p64:p64 + 64, j, :],
                                 rhs=qdq[p64:p64 + 64, j, :],
                                 start=False, stop=True)
                    kslot = j % 4
                    if j >= 1 and hh % 2 == 0:
                        e.wait_ge(s_k.h, Kb + j)
                    ins = e.matmul(out=ps_kao[p64:p64 + 64, 64 * kslot:64 * (kslot + 1)],
                                   lhsT=kdk[:, 64 * hh:64 * (hh + 1)],
                                   rhs=v_bf[:, 64 * g:64 * (g + 1)],
                                   start=True, stop=True)
                    if hh % 2 == 1:
                        s_tk.inc(ins)
            blk.tensor(f_t)

            def f_v(e, cc=cc, TSb=TSb, TKb=TKb, c=c):
                if c == 0:
                    s_c.wait(e)
                for hh in range(NH):
                    slot = hh % 4
                    e.wait_ge(s_ts.h, TSb + hh + 1)
                    s_m.inc(e.tensor_mul(out=sc_bf[:, 128 * slot:128 * (slot + 1)],
                                         in0=sc_slot(hh),
                                         in1=maskT_sb[:, 128 * hh:128 * (hh + 1)]))
                    if hh % 2 == 1:
                        j = hh // 2
                        kslot = j % 4
                        e.wait_ge(s_tk.h, TKb + j + 1)
                        if cc == 0:
                            e.tensor_copy(out=kv_sb[c % 2][:, j, :],
                                          in_=ps_kao[:, 64 * kslot:64 * (kslot + 1)])
                            s_k.inc(e.tensor_copy(out=kv_bf[:, j, :],
                                          in_=ps_kao[:, 64 * kslot:64 * (kslot + 1)]))
                        else:
                            e.scalar_tensor_tensor(
                                out=kv_sb[c % 2][:, j, :], in0=kv_sb[(c - 1) % 2][:, j, :],
                                scalar=bdvec_sb[:, j:j + 1],
                                in1=ps_kao[:, 64 * kslot:64 * (kslot + 1)],
                                op0=ALU.mult, op1=ALU.add)
                            s_k.inc(e.scalar_tensor_tensor(
                                out=kv_bf[:, j, :], in0=kv_sb[(c - 1) % 2][:, j, :],
                                scalar=bdvec_sb[:, j:j + 1],
                                in1=ps_kao[:, 64 * kslot:64 * (kslot + 1)],
                                op0=ALU.mult, op1=ALU.add))
                s_k2.inc(e.drain())
            blk.vector(f_v)

            def f_a(e, TKb=TKb):
                for j in range(8):
                    e.wait_ge(s_tk.h, TKb + j + 1)
                    s_e.inc(e.activation(out=oT[:, j, :], in_=o_slot(j), func=AF.Identity))
            blk.scalar(f_a)

            if KSUB < 6:
                continue
            Eb2 = s_e.v
            def f_t(e, c=c, Eb2=Eb2):
                e.wait_ge(s_e.h, Eb2)
                s_k.wait(e)
                for half in range(2):
                    if 2 * c + half >= 1:
                        e.wait_ge(s_we.h, 2 * c + half)
                    for j in range(8):
                        ins = e.matmul(out=ps_kao[:],
                                       lhsT=oT[:, j, :],
                                       rhs=woT_sb[:, D * j + 512 * half:D * j + 512 * (half + 1)],
                                       start=(j == 0), stop=(j == 7))
                    s_wo.inc(ins)
            blk.tensor(f_t)

            def f_v(e, c=c):
                for half in range(2):
                    e.wait_ge(s_wo.h, 2 * c + half + 1)
                    if c >= 2:
                        e.wait_ge(s_o.h, 16 * (c - 1))
                    s_we.inc(e.tensor_add(out=att[c % 2][:, 512 * half:512 * (half + 1)],
                                          in0=ps_kao[:],
                                          in1=xs[c % 2][:, 512 * half:512 * (half + 1)]))
            blk.vector(f_v)

            def f_sync(e, c=c):
                e.wait_ge(s_we.h, 2 * c + 2)
                s_o.inc(e.dma_start(out=attst_target[128 * c:128 * (c + 1)],
                                    in_=att[c % 2][:]), 16)
            blk.sync(f_sync)

    es_a.close()

    if PHASE_B and KSTAGE >= 4:
        es_b = ExitStack()
        hb = [SB(es_b, f"hb{i}", [128, FT, IT], BF16) for i in range(2)]
        hL = [SB(es_b, f"hL{i}", [128, FT], BF16) for i in range(2)]
        x_pb = [SB(es_b, f"x_pb{i}", [128, D]) for i in range(3)]
        scr_pb = [SB(es_b, f"scr_pb{i}", [128, D]) for i in range(2)]
        outsb = [SB(es_b, f"outsb{i}", [128, D]) for i in range(2)]
        xn_t = SB(es_b, "xn_t", [128, D])
        t1_t = SB(es_b, "t1_t", [128, D])
        xq_sb = SB(es_b, "xq_sb", [128, 4, D], BF16)
        xqT_sb = SB(es_b, "xqT_sb", [128, 8, IT], BF16)
        w1g = [SB(es_b, f"w1g{i}", [128, 8, 512], BF16) for i in range(2)]
        w2g = [SB(es_b, f"w2g{i}", [128, 4, D], BF16) for i in range(2)]
        ctmp = [SB(es_b, f"ctmp{i}", [128, IT]) for i in range(4)]
        mtree = SB(es_b, "mtree", [128, IT])
        m2c = SB(es_b, "m2c", [128, 4])
        s2v = SB(es_b, "s2v", [128, 4])
        c2v = SB(es_b, "c2v", [128, 4])
        s4blk = SB(es_b, "s4blk", [1, IT])
        s2b = SB(es_b, "s2b", [128, IT])
        q1s = SB(es_b, "q1s", [128, 1])
        q1c = SB(es_b, "q1c", [128, 1])
        q1b = SB(es_b, "q1b", [128, 1])
        dbg_tensors.update(hb0=hb[0], hb1=hb[1], mtree=mtree, m2c=m2c, s2v=s2v,
                           c2v=c2v, s2b=s2b, xq_sb=xq_sb, xqT_sb=xqT_sb, hL0=hL[0], hL1=hL[1])
        ps_mm = PSUM(es_b, "ps_mm", [128, 2048])
        ps_tr2 = PSUM(es_b, "ps_tr2", [128, 1024])
        tr2_slot = lambda i: ps_tr2[:, 512 * (i % 2):512 * (i % 2) + 128]
        ps_bc2 = PSUM(es_b, "ps_bc2", [128, 512])

        def emit_h(it):
            seq, k = divmod(it, NIT)
            with nc.Block() as blk:
                def f_sync(e, it=it):
                    if it == 0:
                        e.wait_ge(s_o.h, s_o.v)
                    for tt in range(4):
                        gtt = 4 * it + tt
                        if gtt >= 3:
                            e.wait_ge(s_xc.h, gtt - 2)
                        row = 128 * gtt
                        s_xb.inc(e.dma_start(out=x_pb[gtt % 3][:],
                                             in_=x_h[row:row + 128]), 16)
                    for g in range(8):
                        gg = 8 * it + g
                        if gg >= 2:
                            e.wait_ge(s_t1.h, gg - 1)
                        s_w1.inc(e.dma_start(out=w1g[g % 2][:],
                                             in_=scr_w1[g].rearrange("dc p f -> p dc f")), 16)
                blk.sync(f_sync)

                for tt in range(4):
                    gtt = 4 * it + tt
                    col = slice(seq * NCHUNK + 4 * k + tt, seq * NCHUNK + 4 * k + tt + 1)
                    def f_a1(e, gtt=gtt, col=col):
                        e.wait_ge(s_xb.h, 16 * (gtt + 1))
                        s_a.inc(e.activation(out=xn_t[:], in_=x_pb[gtt % 3][:], func=AF.Identity,
                                             scale=rall[:, col], bias=ball[:, col]))
                        s_xc.inc(e.drain())
                    blk.scalar(f_a1)

                    def f_v1(e):
                        s_a.wait(e)
                        e.reduce_max(out=q1s[:], in_=xn_t[:], axis=AX.X, apply_absolute_value=True)
                        e.drain()
                        e.tensor_scalar(out=q1s[:], in0=q1s[:], scalar1=1e-5, scalar2=None, op0=ALU.max)
                        e.drain()
                        e.tensor_scalar_mul(out=q1c[:], in0=q1s[:], scalar1=1.0 / 127.0)
                        e.reciprocal(out=q1s[:], in_=q1s[:])
                        e.drain()
                        e.tensor_mul(out=q1c[:], in0=q1c[:], in1=mw1[:])
                        e.tensor_scalar_mul(out=q1s[:], in0=q1s[:], scalar1=127.0)
                        e.drain()
                        e.tensor_mul(out=q1b[:], in0=negM[:], in1=q1c[:])
                        s_v.inc(e.drain())
                    blk.vector(f_v1)

                    def f_a2(e, tt=tt):
                        s_v.wait(e)
                        e.activation(out=t1_t[:], in_=xn_t[:], func=AF.Identity,
                                     scale=q1s[:], bias=mvec[:])
                        e.activation(out=xn_t[:], in_=t1_t[:], func=AF.Identity,
                                     bias=negM[:])
                        s_a.inc(e.activation(out=xq_sb[:, tt, :], in_=xn_t[:], func=AF.Identity,
                                             scale=q1c[:]))
                    blk.scalar(f_a2)

                tb = s_t.v; ab = s_a.v
                def f_t(e, tb=tb, ab=ab):
                    for tt in range(4):
                        e.wait_ge(s_a.h, ab - 8 + 2 * (tt + 1))
                        for j in range(8):
                            idx = 8 * tt + j
                            if idx >= 2:
                                e.wait_ge(s_a.h, ab + idx - 1)
                            s_t.inc(e.matmul(out=tr2_slot(idx),
                                             lhsT=xq_sb[:, tt, 128 * j:128 * (j + 1)],
                                             rhs=identb_sb[:], start=True, stop=True))
                blk.tensor(f_t)

                def f_a(e, tb=tb):
                    for idx in range(32):
                        tt, j = divmod(idx, 8)
                        e.wait_ge(s_t.h, tb + idx + 1)
                        s_a.inc(e.activation(out=xqT_sb[:, j, 128 * tt:128 * (tt + 1)],
                                             in_=tr2_slot(idx), func=AF.Identity))
                blk.scalar(f_a)

                ab = s_a.v
                vb_mm1 = s_v.v
                def f_t(e, it=it, ab=ab, vb_mm1=vb_mm1):
                    e.wait_ge(s_a.h, ab)
                    for g in range(8):
                        e.wait_ge(s_w1.h, 16 * (8 * it + g + 1))
                        if g >= 1:
                            e.wait_ge(s_v.h, vb_mm1 + 4 * g)
                        for dc in range(8):
                            for fi in range(4):
                                ins = e.matmul(out=ps_mm[:, 512 * fi:512 * (fi + 1)],
                                               lhsT=w1g[g % 2][:, dc, 128 * fi:128 * (fi + 1)],
                                               rhs=xqT_sb[:, dc, :],
                                               start=(dc == 0), stop=(dc == 7))
                        s_t1.inc(ins)
                blk.tensor(f_t)

                t1b = s_t1.v - 8
                def f_v(e, it=it, t1b=t1b):
                    for g in range(8):
                        e.wait_ge(s_t1.h, t1b + g + 1)
                        for fi in range(4):
                            ct = 4 * g + fi
                            s_v.inc(e.tensor_copy(out=hb[it % 2][:, ct, :],
                                                  in_=ps_mm[:, 512 * fi:512 * (fi + 1)]))
                blk.vector(f_v)
            return s_v.v

        def emit_post(j, h_done_v, hnext_done_v):
            seq, k = divmod(j, NIT)
            with nc.Block() as blk:
                def f_v0(e, j=j):
                    e.wait_ge(s_v.h, h_done_v)
                    if hnext_done_v is not None:
                        e.wait_ge(s_v.h, hnext_done_v)
                    s_v.inc(e.tensor_copy(out=hL[(j + 1) % 2][:],
                                          in_=hb[j % 2][:, :, IT - 1]))
                blk.vector(f_v0)

                ab_cs = s_a.v
                vb_cs = s_v.v
                def f_vconv(e, j=j, k=k, ab_cs=ab_cs):
                    for ct in range(FT):
                        w0 = cwt_sb[:, 3 * ct + 0:3 * ct + 1]
                        w1 = cwt_sb[:, 3 * ct + 1:3 * ct + 2]
                        w2 = cwt_sb[:, 3 * ct + 2:3 * ct + 3]
                        if ct >= 4:
                            e.wait_ge(s_a.h, ab_cs + ct - 3)
                        buf = ctmp[ct % 4]
                        e.tensor_scalar_mul(out=buf[:], in0=hb[j % 2][:, ct, :], scalar1=w1)
                        e.scalar_tensor_tensor(out=buf[:, 1:IT], in0=hb[j % 2][:, ct, 0:IT - 1],
                                               scalar=w0, in1=buf[:, 1:IT],
                                               op0=ALU.mult, op1=ALU.add)
                        e.scalar_tensor_tensor(out=buf[:, 0:IT - 1], in0=hb[j % 2][:, ct, 1:IT],
                                               scalar=w2, in1=buf[:, 0:IT - 1],
                                               op0=ALU.mult, op1=ALU.add)
                        if k < NIT - 1:
                            e.scalar_tensor_tensor(out=buf[:, IT - 1:IT],
                                                   in0=hb[(j + 1) % 2][:, ct, 0:1],
                                                   scalar=w2, in1=buf[:, IT - 1:IT],
                                                   op0=ALU.mult, op1=ALU.add)
                        if k > 0:
                            e.scalar_tensor_tensor(out=buf[:, 0:1], in0=hL[j % 2][:, ct:ct + 1],
                                                   scalar=w0, in1=buf[:, 0:1],
                                                   op0=ALU.mult, op1=ALU.add)
                        if ct % 4 == 3:
                            s_v.inc(e.drain())
                blk.vector(f_vconv)

                def f_asilu(e, j=j, vb_cs=vb_cs):
                    for ct in range(FT):
                        e.wait_ge(s_v.h, vb_cs + ct // 4 + 1)
                        s_a.inc(e.activation(out=hb[j % 2][:, ct, :], in_=ctmp[ct % 4][:],
                                             func=AF.Silu, bias=cbias_sb[:, ct:ct + 1]))
                blk.scalar(f_asilu)
                s_v.v += 0

                if KPOST < 2:
                    return
                ab = s_a.v
                def f_v(e, j=j, ab=ab):
                    e.wait_ge(s_a.h, ab)
                    hbr = hb[j % 2][:].rearrange("p ct t -> p t ct")
                    mr = mtree[:].rearrange("p (t o) -> p t o", o=1)
                    s_v.inc(e.reduce_max(out=mr, in_=hbr, axis=AX.X,
                                         apply_absolute_value=True))
                blk.vector(f_v)

                if KPOST < 3:
                    return
                tb = s_t.v
                def f_t(e, tb=tb):
                    s_v.wait(e)
                    for tt in range(4):
                        if tt >= 2:
                            e.wait_ge(s_v.h, s_v.v + tt - 1)
                        s_t.inc(e.matmul(out=tr2_slot(tt),
                                         lhsT=mtree[:, 128 * tt:128 * (tt + 1)],
                                         rhs=identf_sb[:], start=True, stop=True))
                blk.tensor(f_t)

                def f_v(e, tb=tb):
                    for tt in range(4):
                        e.wait_ge(s_t.h, tb + tt + 1)
                        s_v.inc(e.reduce_max(out=m2c[:, tt:tt + 1],
                                             in_=tr2_slot(tt),
                                             axis=AX.X))
                    e.drain()
                    e.tensor_scalar(out=m2c[:], in0=m2c[:], scalar1=1e-5, scalar2=None, op0=ALU.max)
                    e.drain()
                    e.reciprocal(out=s2v[:], in_=m2c[:])
                    e.tensor_scalar_mul(out=c2v[:], in0=m2c[:], scalar1=1.0 / 127.0)
                    e.drain()
                    e.tensor_scalar_mul(out=s2v[:], in0=s2v[:], scalar1=127.0)
                    e.tensor_mul(out=c2v[:], in0=c2v[:], in1=mw2[:].broadcast_to((128, 4)))
                    e.drain()
                    s_v.inc(e.nop())
                blk.vector(f_v)

                tb = s_t.v
                def f_t(e, tb=tb):
                    s_v.wait(e)
                    for tt in range(4):
                        ins = e.matmul(out=ps_bc2[0:1, 128 * tt:128 * (tt + 1)],
                                       lhsT=s2v[:, tt:tt + 1], rhs=identf_sb[:],
                                       start=True, stop=True)
                    s_t.inc(ins)
                blk.tensor(f_t)

                def f_v(e, tb=tb):
                    e.wait_ge(s_t.h, tb + 1)
                    e.tensor_copy(out=s4blk[0:1, :], in_=ps_bc2[0:1, :])
                    s_v.inc(e.drain())
                blk.vector(f_v)

                tb = s_t.v
                def f_t(e, tb=tb):
                    s_v.wait(e)
                    s_t.inc(e.matmul(out=ps_bc2[:], lhsT=onesfr_sb[0:1, :], rhs=s4blk[0:1, :],
                                     start=True, stop=True))
                blk.tensor(f_t)

                if KPOST < 4:
                    return
                def f_v(e, tb=tb, j=j):
                    e.wait_ge(s_t.h, tb + 1)
                    e.tensor_copy(out=s2b[:], in_=ps_bc2[:])
                    for ct in range(FT):
                        buf = ctmp[ct % 2]
                        e.tensor_mul(out=buf[:], in0=hb[j % 2][:, ct, :], in1=s2b[:])
                        e.tensor_scalar(out=hb[j % 2][:, ct, :], in0=buf[:],
                                        scalar1=mvec[:], scalar2=mvec[:],
                                        op0=ALU.add, op1=ALU.subtract)
                    s_q.inc(e.drain())
                blk.vector(f_v)

                if KPOST < 5:
                    return
                TPN = 1 if KPOST == 5 else 2
                qv = s_q.v
                def f_sync(e, j=j, seq=seq, k=k):
                    for tt in range(min(2, 2 * TPN)):
                        gtt = 4 * j + tt
                        if gtt >= 2:
                            e.wait_ge(s_sce.h, gtt - 1)
                        row = 128 * gtt
                        s_scb.inc(e.dma_start(out=scr_pb[gtt % 2][:], in_=scr_att[row:row + 128]), 16)
                    for tp in range(TPN):
                        for cg in range(8):
                            gi = 16 * j + 8 * tp + cg
                            if gi >= 2:
                                e.wait_ge(s_t2.h, gi - 1)
                            s_w2.inc(e.dma_start(
                                out=w2g[gi % 2][:],
                                in_=scr_w2[4 * cg:4 * cg + 4].rearrange("cc p m -> p cc m")), 16)
                    for tt in range(2, 2 * TPN):
                        gtt = 4 * j + tt
                        e.wait_ge(s_sce.h, gtt - 1)
                        row = 128 * gtt
                        s_scb.inc(e.dma_start(out=scr_pb[gtt % 2][:], in_=scr_att[row:row + 128]), 16)
                blk.sync(f_sync)

                def f_t(e, j=j, qv=qv):
                    e.wait_ge(s_q.h, qv)
                    for tp in range(TPN):
                        if tp == 1:
                            e.wait_ge(s_v.h, s_v.v + 4)
                        for cg in range(8):
                            gi = 16 * j + 8 * tp + cg
                            e.wait_ge(s_w2.h, 16 * (gi + 1))
                            for c4 in range(4):
                                ct = 4 * cg + c4
                                for ti in range(2):
                                    tt = 2 * tp + ti
                                    for mh in range(2):
                                        ins = e.matmul(
                                            out=ps_mm[:, 1024 * ti + 512 * mh:1024 * ti + 512 * (mh + 1)],
                                            lhsT=hb[j % 2][:, ct, 128 * tt:128 * (tt + 1)],
                                            rhs=w2g[gi % 2][:, c4, 512 * mh:512 * (mh + 1)],
                                            start=(cg == 0 and c4 == 0),
                                            stop=(cg == 7 and c4 == 3))
                            s_t2.inc(ins)
                blk.tensor(f_t)

                t2b = s_t2.v - 8 * TPN
                def f_v(e, j=j, t2b=t2b):
                    for tp in range(TPN):
                        e.wait_ge(s_t2.h, t2b + 8 * (tp + 1))
                        for ti in range(2):
                            tt = 2 * tp + ti
                            gtt = 4 * j + tt
                            e.wait_ge(s_scb.h, 16 * (gtt + 1))
                            if gtt >= 2:
                                e.wait_ge(s_o.h, 16 * (32 + gtt - 1))
                            for mh in range(2):
                                osl = outsb[gtt % 2][:, 512 * mh:512 * (mh + 1)]
                                psl = ps_mm[:, 1024 * ti + 512 * mh:1024 * ti + 512 * (mh + 1)]
                                ssl = scr_pb[gtt % 2][:, 512 * mh:512 * (mh + 1)]
                                if KEVAC == "scr":
                                    s_v.inc(e.tensor_copy(out=osl, in_=ssl))
                                elif KEVAC == "mm2":
                                    s_v.inc(e.tensor_scalar_mul(out=osl, in0=psl,
                                                                scalar1=c2v[:, tt:tt + 1]))
                                else:
                                    s_v.inc(e.scalar_tensor_tensor(
                                        out=osl, in0=psl, scalar=c2v[:, tt:tt + 1],
                                        in1=ssl, op0=ALU.mult, op1=ALU.add))
                            s_sce.inc(e.drain())
                blk.vector(f_v)

                vb = s_v.v - 4 * TPN
                def f_sync2(e, j=j, vb=vb):
                    for tt in range(2 * TPN):
                        gtt = 4 * j + tt
                        e.wait_ge(s_v.h, vb + 2 * (tt + 1))
                        row = 128 * gtt
                        s_o.inc(e.dma_start(out=out_h[row:row + 128], in_=outsb[gtt % 2][:]), 16)
                blk.sync(f_sync2)

        h_done = {}
        if KSTAGE == 4:
            h_done[0] = emit_h(0)
        elif KSTAGE == 5:
            h_done[0] = emit_h(0)
            h_done[1] = emit_h(1)
            emit_post(0, h_done[0], h_done[1])
        else:
            for seq in range(2):
                for k in range(NIT):
                    it = NIT * seq + k
                    h_done[it] = emit_h(it)
                    if k >= 1:
                        emit_post(it - 1, h_done[it - 1], h_done[it])
                emit_post(NIT * seq + NIT - 1, h_done[NIT * seq + NIT - 1], None)

        es_b.close()
    else:
        with nc.Block() as blk:
            blk.gpsimd(lambda e: e.wait_ge(s_o.h, s_o.v))

    kdump = os.environ.get("KDUMP", "")
    if kdump:
        dbg = {}
        for nm in kdump.split(","):
            t = dbg_tensors[nm]
            shp = [t.shape[0], int(np.prod(t.shape[1:]))] if len(t.shape) > 2 else list(t.shape)
            dt = t[:].dtype if hasattr(t[:], 'dtype') else F32
            dbg[nm] = nc.dram_tensor("dbg_" + nm, shp, dt, kind="ExternalOutput").ap()
        with nc.Block() as blk:
            def f_g(e):
                e.wait_ge(s_o.h, s_o.v)
                for nm in kdump.split(","):
                    t = dbg_tensors[nm]
                    s_gd.inc(e.dma_start(out=dbg[nm], in_=t[:].rearrange("p ... -> p (...)") if len(t.shape) > 2 else t[:]), 16)
            blk.gpsimd(f_g)

    with nc.Block() as blk:
        def f_g(e):
            e.wait_ge(s_o.h, s_o.v)
            e.wait_ge(s_gd.h, s_gd.v)
        blk.gpsimd(f_g)

    esg.close()
    return nc


_NC_CACHE = None


def kernel(**inputs):
    global _NC_CACHE
    x = np.ascontiguousarray(np.asarray(inputs["L0_attn_x"], np.float32))
    Wq = np.asarray(inputs["L0_attn_Wq"], np.float32)
    Wk = np.asarray(inputs["L0_attn_Wk"], np.float32)
    Wv = np.asarray(inputs["L0_attn_Wv"], np.float32)
    Wo = np.asarray(inputs["L0_attn_Wo"], np.float32)
    w1 = np.asarray(inputs["L0_ffn_w1"], np.float32)
    cw = np.asarray(inputs["L0_ffn_conv_w"], np.float32)
    cb = np.asarray(inputs["L0_ffn_conv_b"], np.float32)
    w2 = np.asarray(inputs["L0_ffn_w2"], np.float32)

    wcat = np.concatenate([Wq, Wk, Wv], axis=0)
    wcatT = np.ascontiguousarray(
        wcat.T.reshape(8, 128, 1536).transpose(1, 0, 2).reshape(128, 8 * 1536))
    woT = np.ascontiguousarray(
        Wo.T.reshape(8, 128, D).transpose(1, 0, 2).reshape(128, 8 * D))
    w1T = np.ascontiguousarray(w1.T.reshape(8, 128, DH))
    w2T = np.ascontiguousarray(w2.T.reshape(32, 128, D))
    cwt = np.ascontiguousarray(
        cw[:, 0, :].reshape(FT, 128, 3).transpose(1, 0, 2).reshape(128, FT * 3))
    cbias = np.ascontiguousarray(cb.reshape(FT, 128).T)

    consts = host_constants()

    if _NC_CACHE is None:
        _NC_CACHE = build_nc()
    nc = _NC_CACHE

    common = dict(wcatT=wcatT, woT=woT, w1T=w1T, w2T=w2T, cwt=cwt, cbias=cbias,
                  **consts)
    xr = x.reshape(B, S, D)
    in_maps = []
    for i in range(NCORE):
        m = dict(common)
        m["x"] = np.ascontiguousarray(xr[2 * i:2 * i + 2].reshape(TOK, D))
        in_maps.append(m)

    res = run_bass_kernel_spmd(nc, in_maps, core_ids=list(range(NCORE)))
    kernel._LAST = res.results
    out = np.concatenate([r["out"].reshape(2, S, D) for r in res.results], axis=0)
    return out
